# version 09
# baseline (speedup 1.0000x reference)
import sys, os
if '/opt/trn_rl_repo' not in sys.path:
    sys.path.insert(0, '/opt/trn_rl_repo')
import numpy as np
from contextlib import ExitStack
import concourse.bass as bass
import concourse.mybir as mybir

F32 = mybir.dt.float32
BF16 = mybir.dt.bfloat16
ALU = mybir.AluOpType
ACT = mybir.ActivationFunctionType

D, T = 1024, 2048
TG = T + 1
CH, NT = 8, 16
NEG = -3.0e38
PI = float(np.pi)


class Sem:
    def __init__(self, es, nc, name):
        self.h = es.enter_context(nc.semaphore(name))
        self.n = 0

    def inc(self, amt=1):
        self.n += amt
        return self.n


class Arena:
    def __init__(self, t, n):
        self.t, self.n, self.top = t, n, 0

    def buf(self, w):
        c = self.top
        self.top += w
        assert self.top <= self.n, f"arena overflow {self.top} > {self.n}"
        return self.t[:, c:c + w]

    def mark(self):
        return self.top

    def free(self, m):
        self.top = m


class Builder:
    def __init__(self, stage=99, probes=()):
        self.stage = stage
        self.probes = set(probes)
        self.dbg_tensors = {}
        nc = bass.Bass('TRN2', target_bir_lowering=False, detect_race_conditions=False)
        self.nc = nc
        self.es = ExitStack()
        es = self.es
        dt = lambda n, sh, dty=F32, kind="ExternalInput": nc.dram_tensor(n, sh, dty, kind=kind).ap()
        self.d_x = dt("xT", [D, T])
        self.d_wqkvT = dt("wqkvT", [D, 3 * D])
        self.d_bqkv = dt("bqkv", [3 * D])
        self.d_woT = dt("woT", [D, D])
        self.d_bo = dt("bo", [D])
        self.d_fwqkv = dt("fwqkv", [2, D, 1536])
        self.d_fqs = dt("fqs", [2, 64])
        self.d_fks = dt("fks", [2, 64])
        self.d_fwo = dt("fwo", [2, 512, D])
        self.d_flng = dt("flng", [2, D])
        self.d_flnb = dt("flnb", [2, D])
        self.d_fw1 = dt("fw1r", [2, D, 5460])
        self.d_fb1 = dt("fb1r", [2, 5460])
        self.d_fw2 = dt("fw2", [2, 2730, D])
        self.d_fb2 = dt("fb2", [2, D])
        self.d_out = dt("outT", [D, T], F32, "ExternalOutput")

        sb = lambda n, sh, dty: es.enter_context(nc.sbuf_tensor(n, sh, dty))
        self.xf = sb("xf", [128, CH * TG], F32)
        self.xb = sb("xb", [128, CH * TG], BF16)
        self.xfb = sb("xfb", [128, CH * TG], BF16)
        self.sv = sb("sv", [128, 64], F32)
        self.svb = sb("svb", [128, 16], BF16)
        self.ident = sb("ident", [128, 128], BF16)
        self.r01 = sb("r01", [128, 128], BF16)
        self.rotq = sb("rotq", [128, 128], BF16)
        self.rotk = sb("rotk", [128, 128], BF16)
        self.onesrow = sb("onesrow", [1, 128], F32)
        self.bd2 = sb("bd2", [128, 2], BF16)
        self.exp2m = sb("exp2m", [2, 128], F32)
        self.bqc = sb("bqc", [128, 24], F32)
        self.boc = sb("boc", [128, 8], F32)
        self.gcol = sb("gcol", [128, 16], F32)
        self.bcol = sb("bcol", [128, 16], F32)
        self.b2col = sb("b2col", [128, 16], F32)
        self.b1c = sb("b1c", [128, 88], F32)
        self.qsc = sb("qsc", [128, 8], F32)
        self.AF = 6400
        self.arf = sb("arf", [128, self.AF], F32)
        self.si32 = sb("si32", [128, 512], mybir.dt.int32)
        rem0 = nc.sbuf_bytes_remaining
        ABB = (rem0 - 256) // 2 // 128 * 128
        self.AB = ABB
        self.arb = sb("arb", [128, self.AB], BF16)
        ps = lambda n: es.enter_context(nc.psum_tensor(n, [128, 1024], F32))
        self.pA, self.pB, self.pC, self.pD = ps("pA"), ps("pB"), ps("pC"), ps("pD")

        self.sW = Sem(es, nc, "sW")
        self.sX = Sem(es, nc, "sX")
        self.sT = Sem(es, nc, "sT")
        self.sV = Sem(es, nc, "sV")
        self.sS = Sem(es, nc, "sS")
        self.sG = Sem(es, nc, "sG")

        self.prog = {k: [] for k in "tvsgy"}
        self._svcol = 0
        self.ab = Arena(self.arb, self.AB)
        self.build()

    def T(self, f): self.prog['t'].append(f)
    def V(self, f): self.prog['v'].append(f)
    def S(self, f): self.prog['s'].append(f)
    def G(self, f): self.prog['g'].append(f)
    def Y(self, f): self.prog['y'].append(f)

    def wait(self, eng, sem, n):
        if n > 0:
            self.prog[eng].append(lambda e: e.wait_ge(sem.h, n))

    def probe(self, name, ap):
        if name not in self.probes:
            return
        sh = list(ap.shape)
        d = self.nc.dram_tensor("dbg_" + name, sh, ap.dtype, kind="ExternalOutput").ap()
        self.dbg_tensors[name] = sh
        self.wait('y', self.sT, self.sT.n)
        self.wait('y', self.sV, self.sV.n)
        self.wait('y', self.sS, self.sS.n)
        self.Y(lambda e, dd=d, s=ap: e.dma_start(out=dd, in_=s).then_inc(self.sX.h, 16))
        self.sX.inc(16)

    def _mark(self, key, sem):
        lst = self.prog[key]
        f = lst[-1]
        lst[-1] = (lambda e, f=f, s=sem.h: f(e).then_inc(s, 1))
        return sem.inc()

    def tmark(self):
        return self._mark('t', self.sT)

    def vsync(self):
        self.wait('v', self.sV, self.vmark())

    def ssync(self):
        self.wait('s', self.sS, self.smark())

    def vmark(self):
        return self._mark('v', self.sV)

    def smark(self):
        return self._mark('s', self.sS)

    def svcol(self, w=1):
        c = self._svcol
        self._svcol += w
        assert self._svcol <= 64
        return self.sv[:, c:c + w]

    def xfc(self, c, t0, n, shift=False):
        o = c * TG + 1 + t0 - (1 if (shift and c >= 4) else 0)
        return self.xf[:, o:o + n]

    def xbc(self, c, t0, n, shift=False):
        o = c * TG + 1 + t0 - (1 if (shift and c >= 4) else 0)
        return self.xb[:, o:o + n]

    def xfbc(self, c, t0, n, shift=False):
        o = c * TG + 1 + t0 - (1 if (shift and c >= 4) else 0)
        return self.xfb[:, o:o + n]

    def rs_(self, c0, n): return self.arf[0:1, c0:c0 + n]
    def c0_(self, c0, n): return self.arf[0:1, 2048 + c0:2048 + c0 + n]
    def mu_(self, c0, n): return self.arf[0:1, 4096 + c0:4096 + c0 + n]

    def build(self):
        self.consts()
        self.ln1()
        if self.stage >= 1:
            for layer in range(min(2, self.stage)):
                self.ffn_layer(layer)
        if self.stage >= 3:
            self.attn()
        self.finish()
        nc = self.nc
        with nc.Block() as blk:
            @blk.tensor
            def _(e):
                for f in self.prog['t']: f(e)

            @blk.vector
            def _(e):
                for f in self.prog['v']: f(e)

            @blk.scalar
            def _(e):
                for f in self.prog['s']: f(e)

            @blk.gpsimd
            def _(e):
                for f in self.prog['g']: f(e)

            @blk.sync
            def _(e):
                for f in self.prog['y']: f(e)
        self.es.close()

    def consts(self):
        sX, sG = self.sX, self.sG
        for c in range(CH):
            dst, src = self.xfc(c, 0, T), self.d_x[c * 128:(c + 1) * 128, :]
            self.Y(lambda e, d=dst, s=src: e.dma_start(out=d, in_=s).then_inc(self.sX.h, 16))
            sX.inc(16)
        self.x_loaded = sX.n

        def colload(dst, src):
            self.Y(lambda e, d=dst, s=src: e.dma_start(
                out=d, in_=s, allow_slow_non_contiguous=True).then_inc(self.sX.h, 16))
            return sX.inc(16)

        colload(self.bqc[:], self.d_bqkv.rearrange("(j p) -> p j", p=128))
        colload(self.boc[:], self.d_bo.rearrange("(j p) -> p j", p=128))
        for l in range(2):
            colload(self.gcol[:, l * 8:l * 8 + 8], self.d_flng[l].rearrange("(j p) -> p j", p=128))
            colload(self.bcol[:, l * 8:l * 8 + 8], self.d_flnb[l].rearrange("(j p) -> p j", p=128))
            colload(self.b2col[:, l * 8:l * 8 + 8], self.d_fb2[l].rearrange("(j p) -> p j", p=128))
            colload(self.b1c[:, l * 44:l * 44 + 42], self.d_fb1[l, 0:5376].rearrange("(j p) -> p j", p=128))
            colload(self.b1c[0:42, l * 44 + 42:l * 44 + 43], self.d_fb1[l, 5376:5418].rearrange("(p o) -> p o", o=1))
            colload(self.b1c[0:42, l * 44 + 43:l * 44 + 44], self.d_fb1[l, 5418:5460].rearrange("(p o) -> p o", o=1))
            for j, dten in ((0, self.d_fqs), (1, self.d_fks)):
                colload(self.qsc[0:64, l * 4 + j:l * 4 + j + 1], dten[l].rearrange("(p o) -> p o", o=1))
                colload(self.qsc[64:128, l * 4 + j:l * 4 + j + 1], dten[l].rearrange("(p o) -> p o", o=1))
                for hp in range(4):
                    p0 = hp * 32
                    s0 = (p0 + 32) % 64
                    src32 = dten[l, s0:s0 + 32].rearrange("(p o) -> p o", o=1)
                    colload(self.qsc[p0:p0 + 32, l * 4 + 2 + j:l * 4 + 3 + j], src32)
        self.x_bias = sX.n

        iot = self.svcol()
        self.iot = iot
        self.G(lambda e: e.iota(iot, [[0, 1]], base=0, channel_multiplier=1,
                                allow_small_or_imprecise_dtypes=True).then_inc(self.sG.h, 1))
        self.g_iota = sG.inc()
        self.G(lambda e: e.memset(self.ident[:], 1.0).then_inc(self.sG.h, 1))
        m1 = sG.inc()
        self.wait('g', sG, m1)
        self.G(lambda e: e.affine_select(self.ident[:], self.ident[:], [[-1, 128]],
                                         ALU.is_equal, 0.0, base=0,
                                         channel_multiplier=1).then_inc(self.sG.h, 1))
        self.g_ident = sG.inc()
        for b in range(2):
            p0 = b * 64
            hi = self.r01[p0:p0 + 32, :]
            lo = self.r01[p0 + 32:p0 + 64, :]
            self.G(lambda e, a=hi: e.memset(a, 1.0).then_inc(self.sG.h, 1))
            self.G(lambda e, a=lo: e.memset(a, -1.0).then_inc(self.sG.h, 1))
        m2 = sG.inc(4)
        self.wait('g', sG, m2)
        for b in range(2):
            p0 = b * 64
            hi = self.r01[p0:p0 + 32, :]
            lo = self.r01[p0 + 32:p0 + 64, :]
            self.G(lambda e, a=hi, base=p0 + 32: e.affine_select(
                a, a, [[-1, 128]], ALU.is_equal, 0.0, base=base,
                channel_multiplier=1).then_inc(self.sG.h, 1))
            self.G(lambda e, a=lo, base=p0: e.affine_select(
                a, a, [[-1, 128]], ALU.is_equal, 0.0, base=base,
                channel_multiplier=1).then_inc(self.sG.h, 1))
        self.g_r01 = sG.inc(4)
        ob = self.svb[:, 0:1]
        self.ones128b = ob
        self.G(lambda e: e.memset(ob, 1.0).then_inc(self.sG.h, 1))
        self.G(lambda e: e.memset(self.onesrow[:], 1.0).then_inc(self.sG.h, 1))
        self.G(lambda e: e.memset(self.bd2[:], 0.0).then_inc(self.sG.h, 1))
        self.G(lambda e: e.memset(self.exp2m[:], 0.0).then_inc(self.sG.h, 1))
        m3 = sG.inc(4)
        self.wait('g', sG, m3)
        self.G(lambda e: e.memset(self.bd2[0:64, 0:1], 1.0).then_inc(self.sG.h, 1))
        self.G(lambda e: e.memset(self.bd2[64:128, 1:2], 1.0).then_inc(self.sG.h, 1))
        self.G(lambda e: e.memset(self.exp2m[:], 1.0).then_inc(self.sG.h, 1))
        mepre = sG.inc(3)
        self.wait('g', sG, mepre)
        self.G(lambda e: e.affine_select(self.exp2m[:], self.exp2m[:], [[1, 128]],
                                         ALU.is_ge, 0.0, base=0,
                                         channel_multiplier=-64).then_inc(self.sG.h, 1))
        me2 = sG.inc()
        self.wait('g', sG, me2)
        self.G(lambda e: e.affine_select(self.exp2m[:], self.exp2m[:], [[-1, 128]],
                                         ALU.is_ge, 0.0, base=63,
                                         channel_multiplier=64).then_inc(self.sG.h, 1))
        self.g_const = sG.inc()

        def mkmask(W, lo, hi):
            t = self.ab.buf(W)
            self.G(lambda e, a=t: e.memset(a, 0.0).then_inc(self.sG.h, 1))
            mset = sG.inc()
            self.wait('g', sG, mset)
            if lo is not None:
                self.G(lambda e, a=t, b=-lo: e.affine_select(
                    a, a, [[1, W]], ALU.is_ge, -1.0e9, base=b,
                    channel_multiplier=-1).then_inc(self.sG.h, 1))
                self.wait('g', sG, sG.inc())
            if hi is not None:
                self.G(lambda e, a=t, b=hi: e.affine_select(
                    a, a, [[-1, W]], ALU.is_ge, -1.0e9, base=b,
                    channel_multiplier=1).then_inc(self.sG.h, 1))
                self.wait('g', sG, sG.inc())
            return t
        self.am_int = mkmask(384, 1, 256)
        self.am_r0 = mkmask(128, None, 0)
        self.am_r1 = mkmask(256, None, 128)
        self.fm_int = mkmask(640, 0, 512)
        self.fm_r = [mkmask(128 * (r + 1), None, 128 * r) for r in range(4)]
        self.g_masks = sG.n

        self.wait('v', sG, self.g_iota)
        self.npib = self.svcol()
        self.V(lambda e, a=self.npib: e.memset(a, -PI))
        self.c64 = self.svcol()
        self.V(lambda e, a=self.c64: e.memset(a, 64.0))
        self.v_vecs = self.vmark()

    def ln_rows(self, psS, psQ, t_stats):
        krow = int(os.environ.get('KBROW', '99'))
        sV, sS = self.sV, self.sS
        self.wait('v', self.sT, t_stats)
        for tc in range(4):
            mu = self.mu_(tc * 512, 512)
            rs = self.rs_(tc * 512, 512)
            c0 = self.c0_(tc * 512, 512)
            self.V(lambda e, a=mu, b=psS[tc]: e.tensor_scalar(a, b, 1.0 / D, None, ALU.mult))
            self.V(lambda e, a=rs, b=psQ[tc]: e.tensor_scalar(a, b, 1.0 / D, 1e-5, ALU.mult, ALU.add))
            self.vsync()
            self.V(lambda e, a=c0, b=mu: e.tensor_mul(a, b, b))
            self.vsync()
            self.V(lambda e, a=rs, b=c0: e.tensor_sub(a, a, b))
            self.vsync()
            self.V(lambda e, a=rs: e.reciprocal(a, a))
        v1 = self.vmark()
        if krow < 2:
            return v1
        self.wait('s', sV, v1)
        for tc in range(4):
            rs = self.rs_(tc * 512, 512)
            self.S(lambda e, a=rs: e.sqrt(a, a))
        s1 = self.smark()
        if krow < 3:
            return v1
        self.wait('v', sS, s1)
        for tc in range(4):
            mu = self.mu_(tc * 512, 512)
            rs = self.rs_(tc * 512, 512)
            c0 = self.c0_(tc * 512, 512)
            self.V(lambda e, a=c0, b=mu, r=rs: e.scalar_tensor_tensor(a, b, -1.0, r, ALU.mult, ALU.mult))
        return self.vmark()

    def ln1(self):
        sub = int(os.environ.get('KBSUB', '99'))
        sT, sV, sS, sX = self.sT, self.sV, self.sS, self.sX
        for c in range(CH):
            for buf in (self.xf, self.xb, self.xfb):
                g = buf[:, c * TG:c * TG + 1]
                self.V(lambda e, a=g: e.memset(a, 0.0))
        self.v_guard = self.vmark()

        if sub < 1:
            return
        am = self.ab.mark()
        sq = [self.ab.buf(T), self.ab.buf(T)]
        s_cast = [self.sS.n + 2 * c + 2 for c in range(CH)]
        t_sqmm = [self.sT.n + c + 1 for c in range(CH)]
        self.wait('s', sX, self.x_loaded)
        for c in range(CH):
            xbch, xfch = self.xbc(c, 0, T), self.xfc(c, 0, T)
            if c >= 2:
                self.wait('s', sT, t_sqmm[c - 2])
            self.S(lambda e, a=xbch, b=xfch: e.copy(a, b).then_inc(self.sS.h, 1))
            sS.inc()
            self.S(lambda e, a=sq[c % 2], b=xbch: e.square(a, b).then_inc(self.sS.h, 1))
            sS.inc()
        assert sS.n == s_cast[-1]

        if sub < 2:
            return
        psS = [self.pA[0:1, 0:512], self.pA[0:1, 512:1024],
               self.pB[0:1, 0:512], self.pB[0:1, 512:1024]]
        psQ = [self.pC[0:1, 0:512], self.pC[0:1, 512:1024],
               self.pD[0:1, 0:512], self.pD[0:1, 512:1024]]
        ob = self.ones128b
        self.wait('t', self.sG, self.g_const)
        for c in range(CH):
            self.wait('t', sS, s_cast[c])
            for tc in range(4):
                xbch = self.xbc(c, tc * 512, 512)
                sqch = sq[c % 2][:, tc * 512:tc * 512 + 512]
                st, sp = (c == 0), (c == CH - 1)
                self.T(lambda e, o=psS[tc], r=xbch, st=st, sp=sp:
                       e.matmul(o, ob, r, start=st, stop=sp))
                self.T(lambda e, o=psQ[tc], r=sqch, st=st, sp=sp:
                       e.matmul(o, ob, r, start=st, stop=sp))
            self.tmark()
        assert sT.n == t_sqmm[-1]
        self.ab.free(am)

        if sub < 3:
            return
        v_rows = self.ln_rows(psS, psQ, sT.n)

        if sub < 4:
            return
        orow = self.onesrow[:]
        for tc in range(4):
            rs = self.rs_(tc * 512, 512)
            c0 = self.c0_(tc * 512, 512)
            rb = self.pA[:, (tc % 2) * 512:(tc % 2) * 512 + 512]
            cb = self.pB[:, (tc % 2) * 512:(tc % 2) * 512 + 512]
            self.wait('t', sV, v_rows if tc < 2 else sV.n)
            self.T(lambda e, o=rb, r=rs: e.matmul(o, orow, r, start=True, stop=True))
            self.T(lambda e, o=cb, r=c0: e.matmul(o, orow, r, start=True, stop=True))
            t_b = self.tmark()
            self.wait('v', sT, t_b)
            for c in range(CH):
                xfch = self.xfc(c, tc * 512, 512)
                xbch = self.xbc(c, tc * 512, 512)
                self.V(lambda e, a=xfch, r=rb: e.tensor_mul(a, a, r))
                self.V(lambda e, a=xfch, r=cb: e.tensor_add(a, a, r))
                self.V(lambda e, a=xbch, b=xfch: e.tensor_copy(a, b))
            self.vmark()
        self.v_ln1 = sV.n

    def ffn_tables(self):
        sT, sV, sS = self.sT, self.sV, self.sS
        self.tab_cx = self.ab.buf(T)
        self.tab_sx = self.ab.buf(T)
        self.tab_cdx = self.ab.buf(T)
        self.tab_sdx = self.ab.buf(T)
        trow = self.arf[0:1, 0:T]
        prow = self.arf[0:1, 2048:4096]
        invf = self.arf[0:1, 4096:4224]
        lnb = self.arf[0:1, 4224:4352]
        hrow = self.arf[0:1, 4480:4608]
        xs = self.arf[:, 4864:5376]
        mm = self.arf[:, 5376:5888]
        sn = self.arf[:, 4352:4864]
        rxs = self.arf[:, 5888:6400]
        self.wait('g', self.sV, self.sV.n)
        self.wait('g', self.sT, self.sT.n)
        self.G(lambda e: e.iota(hrow, [[0, 4], [2, 32]], base=0, channel_multiplier=0,
                                allow_small_or_imprecise_dtypes=True).then_inc(self.sG.h, 1))
        self.G(lambda e: e.iota(trow, [[1, T]], base=0, channel_multiplier=0,
                                allow_small_or_imprecise_dtypes=True).then_inc(self.sG.h, 1))
        g_rows = self.sG.inc(2)
        self.wait('v', self.sG, g_rows)
        self.wait('v', sV, self.v_ln1)
        LN1E4 = float(np.log(10000.0))
        self.V(lambda e, s=-LN1E4 / 64.0: e.tensor_scalar(invf, hrow, s, None, ALU.mult))
        self.V(lambda e: e.tensor_scalar(lnb, hrow, 25.6, 1.0 / 89.6, ALU.add, ALU.mult))
        self.V(lambda e: e.tensor_scalar(prow, trow, -1024.0, 1.0 / 256.0, ALU.add, ALU.mult))
        v_pre = self.vmark()
        self.wait('s', sV, v_pre)
        self.S(lambda e: e.activation(invf, invf, ACT.Exp))
        self.S(lambda e: e.activation(lnb, lnb, ACT.Ln))
        s_pre = self.smark()
        if "arfrow0" in self.probes and "arfrow0" not in self.dbg_tensors:
            self.probe("arfrow0", self.arf[0:1, 0:4608])
        self.wait('t', sS, s_pre)
        self.wait('t', sV, sV.n)
        self.wait('t', sS, sS.n)
        for tc in range(4):
            fr = self.pA[:, 0:512]
            lx = self.pB[:, 0:512]
            trch = trow[:, tc * 512:tc * 512 + 512]
            prch = prow[:, tc * 512:tc * 512 + 512]
            if tc > 0:
                self.wait('t', sV, sV.n)
            self.T(lambda e, o=fr, r=trch: e.matmul(o, invf, r, start=True, stop=True))
            self.T(lambda e, o=lx, r=prch: e.matmul(o, lnb, r, start=True, stop=True))
            t_o = self.tmark()
            cxc = self.tab_cx[:, tc * 512:tc * 512 + 512]
            sxc = self.tab_sx[:, tc * 512:tc * 512 + 512]
            cdc = self.tab_cdx[:, tc * 512:tc * 512 + 512]
            sdc = self.tab_sdx[:, tc * 512:tc * 512 + 512]
            self.wait('v', sT, t_o)
            I2PI = 1.0 / (2 * PI)
            ki = self.si32[:]
            def rr(buf, extra_turn, extra_add):
                self.V(lambda e, a=buf, b=fr: e.tensor_scalar(a, b, I2PI, extra_turn, ALU.mult, ALU.add)
                       if extra_turn is not None else e.tensor_scalar(a, b, I2PI, None, ALU.mult))
                self.vsync()
                self.V(lambda e, a=ki, b=buf: e.tensor_copy(a, b))
                self.vsync()
                self.V(lambda e, a=buf, b=ki: e.tensor_copy(a, b))
                self.vsync()
                self.V(lambda e, a=buf, b=fr: e.scalar_tensor_tensor(a, a, -2 * PI, b, ALU.mult, ALU.add))
                if extra_add:
                    self.vsync()
                    self.V(lambda e, a=buf: e.tensor_scalar(a, a, extra_add, None, ALU.add))
                self.vsync()
                self.V(lambda e, a=rxs, b=buf: e.tensor_scalar(a, b, PI, None, ALU.is_gt))
                self.vsync()
                self.V(lambda e, a=buf, b=rxs: e.scalar_tensor_tensor(a, b, -2 * PI, a, ALU.mult, ALU.add))
                self.vsync()
                self.V(lambda e, a=buf: e.tensor_scalar(a, a, PI, -PI, ALU.min, ALU.max))
            rr(sn, None, 0.0)
            v1 = self.vmark()
            self.wait('s', sV, v1)
            self.S(lambda e, a=sn: e.activation(a, a, ACT.Sin))
            s1 = self.smark()
            self.wait('v', sS, s1)
            rr(mm, 0.25, PI / 2)
            v2 = self.vmark()
            self.wait('s', sV, v2)
            self.S(lambda e, a=mm: e.activation(a, a, ACT.Sin))
            self.S(lambda e, a=xs, b=lx: e.activation(a, b, ACT.Exp))
            s2 = self.smark()
            self.wait('v', sS, s2)
            self.V(lambda e, a=rxs, b=xs: e.reciprocal(a, b))
            self.V(lambda e, a=cxc: e.tensor_mul(a, mm, xs))
            self.V(lambda e, a=sxc: e.tensor_mul(a, sn, xs))
            self.V(lambda e, a=cdc: e.tensor_mul(a, mm, rxs))
            self.V(lambda e, a=sdc: e.tensor_mul(a, sn, rxs))
            self.vmark()
            if tc == 0 and "tb0_sn" in self.probes and "tb0_sn" not in self.dbg_tensors:
                self.probe("tb0_sn", sn)
                self.probe("tb0_mm", mm)
                self.probe("tb0_xs", xs)
                self.probe("tb0_rxs", rxs)
        self.v_tables = sV.n
        if "tcx" in self.probes and "tcx" not in self.dbg_tensors:
            self.probe("tcx", self.tab_cx)
            self.probe("tsx", self.tab_sx)
            self.probe("tcdx", self.tab_cdx)
            self.probe("tsdx", self.tab_sdx)

    def banded_attention(self, cfg):
        sT, sV, sS = self.sT, self.sV, self.sS
        tiles = cfg['tiles']
        LB = cfg['lookback']
        inst = [(r, b) for r in tiles for b in range(2)]
        n = len(inst)
        t_sa = [0] * n
        t_tp = [0] * n
        t_pv = [0] * n
        v_nmx = [0] * n
        v_b2 = [0] * n
        v_pte = [0] * n
        v_ote = {}
        s_ln = [0] * n
        s_e2 = [0] * n
        NSA, NPT, NPOT = cfg['nsa'], cfg['npt'], cfg['npot']
        self.wait('t', sV, sV.n)
        self.wait('t', sS, sS.n)

        def kwW(r):
            kw = max(0, 128 * r - LB)
            return kw, 128 * r + 128 - kw

        for w in range(n + 4):
            j = w - 1
            if 0 <= j < n:
                r, b = inst[j]
                kw, W = kwW(r)
                self.wait('v', sT, t_sa[j])
                psa = cfg['psa'](j % NSA)
                nmx = cfg['nmx'](j % 2)
                mtile = cfg['mask'](r)
                self.V(lambda e, o=psa[:, 0:W], m=mtile: e.tensor_tensor(o, o, m, ALU.add))
                self.V(lambda e, a=nmx, b=psa[:, 0:W]:
                       e.reduce_max(a, b, axis=mybir.AxisListType.X, negate=True))
                v_nmx[j] = self.vmark()
            j = w - 2
            if 0 <= j < n:
                self.wait('v', sS, s_ln[j])
                nmx, lnsg, b2 = cfg['nmx'](j % 2), cfg['lnsg'](j % 2), cfg['b2'](j % 2)
                self.V(lambda e, a=b2, b=lnsg, c=nmx: e.scalar_tensor_tensor(a, b, -1.0, c, ALU.mult, ALU.add))
                v_b2[j] = self.vmark()
            j = w - 3
            if 0 <= j < n:
                r, b = inst[j]
                kw, W = kwW(r)
                self.wait('v', sT, t_tp[j])
                if j >= 2:
                    self.wait('v', sT, t_pv[j - 2])
                ppt = cfg['ppt'](j % NPT)
                pts = cfg['pts'](j % 2)
                self.V(lambda e, a=pts[:, 0:W], b=ppt[:, 0:W]: e.tensor_copy(a, b))
                v_pte[j] = self.vmark()
            j = w - 4
            if 0 <= j < n:
                r, b = inst[j]
                if b == 1:
                    self.wait('v', sT, t_pv[j])
                    slot = (j // 2) % NPOT
                    cfg['otevac'](r, cfg['pot'](slot))
                    v_ote[r] = self.vmark()
            j = w - 1
            if 0 <= j < n:
                r, b = inst[j]
                kw, W = kwW(r)
                self.wait('s', sV, v_nmx[j])
                psa = cfg['psa'](j % NSA)
                nmx, sg, lnsg = cfg['nmx'](j % 2), cfg['sg'](j % 2), cfg['lnsg'](j % 2)
                self.S(lambda e, o=cfg['garb'][:, 0:W], i=psa[:, 0:W], b=nmx, a=sg:
                       e.activation(o, i, ACT.Exp, bias=b, accum_out=a))
                self.ssync()
                self.S(lambda e, a=lnsg, b=sg: e.activation(a, b, ACT.Ln))
                s_ln[j] = self.smark()
            j = w - 2
            if 0 <= j < n:
                r, b = inst[j]
                kw, W = kwW(r)
                self.wait('s', sV, v_b2[j])
                if j >= 2:
                    self.wait('s', sT, t_tp[j - 2])
                psa = cfg['psa'](j % NSA)
                b2 = cfg['b2'](j % 2)
                pr = cfg['probs'](j % 2)
                self.S(lambda e, o=pr[:, 0:W], i=psa[:, 0:W], b=b2:
                       e.activation(o, i, ACT.Exp, bias=b).then_inc(self.sS.h, 1))
                s_e2[j] = sS.inc()
            if w < n:
                r, b = inst[w]
                kw, W = kwW(r)
                if w >= NSA:
                    self.wait('t', sS, s_e2[w - NSA])
                psa = cfg['psa'](w % NSA)
                q = cfg['qbuf'](b, r)
                k = cfg['kbuf'](b, kw, W)
                for m0 in range(0, W, 512):
                    m1 = min(W, m0 + 512)
                    self.T(lambda e, o=psa[:, m0:m1], q=q, kk=k[:, m0:m1]:
                           e.matmul(o, q, kk, start=True, stop=True))
                t_sa[w] = self.tmark()
            j = w - 2
            if 0 <= j < n:
                r, b = inst[j]
                kw, W = kwW(r)
                self.wait('t', sS, s_e2[j])
                if j >= NPT:
                    self.wait('t', sV, v_pte[j - NPT])
                ppt = cfg['ppt'](j % NPT)
                pr = cfg['probs'](j % 2)
                for kb in range(W // 128):
                    self.T(lambda e, o=ppt[:, kb * 128:kb * 128 + 128],
                           l=pr[:, kb * 128:kb * 128 + 128]:
                           e.matmul(o, l, self.ident[:], start=True, stop=True))
                t_tp[j] = self.tmark()
            j = w - 3
            if 0 <= j < n:
                r, b = inst[j]
                kw, W = kwW(r)
                self.wait('t', sV, v_pte[j])
                slot = (j // 2) % NPOT
                if b == 0 and j >= 2 * NPOT:
                    self.wait('t', sV, v_ote[inst[j - 2 * NPOT][0]])
                pot = cfg['pot'](slot)
                pts = cfg['pts'](j % 2)
                nkb = W // 128
                for kb in range(nkb):
                    tt = kw // 128 + kb
                    self.T(lambda e, o=pot[64 * b:64 * b + 64, 0:128],
                           l=cfg['vblk'](tt, b), rr=pts[:, kb * 128:kb * 128 + 128],
                           st=(kb == 0), sp=(kb == nkb - 1), b=b:
                           e.matmul(o, l, rr, start=st, stop=sp, tile_position=(0, 64 * b)))
                t_pv[j] = self.tmark()

    def ffn_layer(self, layer):
        sT, sV, sS, sW = self.sT, self.sV, self.sS, self.sW
        srcc = (lambda c, t0, nn, sh=True: self.xbc(c, t0, nn, sh)) if layer == 0 else \
               (lambda c, t0, nn, sh=True: self.xfbc(c, t0, nn, sh))
        qs_col = self.qsc[:, layer * 4 + 0:layer * 4 + 1]
        ks_col = self.qsc[:, layer * 4 + 1:layer * 4 + 2]
        qsp_col = self.qsc[:, layer * 4 + 2:layer * 4 + 3]
        ksp_col = self.qsc[:, layer * 4 + 3:layer * 4 + 4]

        self.wait('v', self.sG, self.g_r01)
        self.wait('v', self.sX, self.x_bias)
        self.V(lambda e: e.tensor_scalar(self.rotq[:], self.r01[:], qs_col, None, ALU.mult))
        self.V(lambda e: e.tensor_scalar(self.rotk[:], self.r01[:], ks_col, None, ALU.mult))
        v_rot = self.vmark()

        am0 = self.ab.mark()
        self.ffn_tables()
        qb = self.ab.buf(1024)
        kbuf = self.ab.buf(1536)
        vbuf = self.ab.buf(1536)
        wsl0 = self.ab.buf(3 * 1024)
        wsl = [wsl0, wsl0]
        oTf = self.ab.buf(4 * 1024)
        prb = [self.ab.buf(640) for _ in range(2)]
        ptsb = [self.ab.buf(640) for _ in range(2)]
        garb = self.ab.buf(640)
        qsq = ptsb[0][:, 0:512]
        mxc = [self.svcol() for _ in range(2)]
        nmxc = [self.svcol() for _ in range(2)]
        sgc = [self.svcol() for _ in range(2)]
        lnsgc = [self.svcol() for _ in range(2)]
        b2c = [self.svcol() for _ in range(2)]

        w_slot_free = [0, 0]
        for half in range(2):
            ht0 = half * 1024
            kv0 = max(0, ht0 - 512)
            klen = ht0 + 1024 - kv0
            for pf in range(4):
                slot = 0
                self.wait('g', sT, w_slot_free[slot])
                for part in range(3):
                    dst = wsl[slot][:, part * 1024:part * 1024 + 1024].rearrange(
                        "p (k f) -> p k f", f=128)
                    src = self.d_fwqkv[layer].rearrange("(k p) f -> p k f", p=128)[
                        :, :, part * 512 + 128 * pf: part * 512 + 128 * pf + 128]
                    self.G(lambda e, d=dst, s=src: e.dma_start(out=d, in_=s).then_inc(self.sW.h, 16))
                    sW.inc(16)
                w_ready = sW.n

                self.wait('t', sW, w_ready)
                self.wait('t', sV, sV.n)
                self.wait('t', sS, sS.n)
                if layer == 1:
                    self.wait('t', sS, self.s_xfb_next)
                pqk = self.pD[:, 512:1024]
                for part, dstbuf, t0r in ((0, qb, ht0), (1, kbuf, kv0), (2, vbuf, kv0)):
                    wsrc = wsl[slot][:, part * 1024:part * 1024 + 1024]
                    if part < 2:
                        for a0 in range(t0r, ht0 + 1024, 512):
                            nn = min(512, ht0 + 1024 - a0)
                            for kc in range(CH):
                                self.T(lambda e, o=pqk[:, 0:nn], l=wsrc[:, kc * 128:kc * 128 + 128],
                                       r=srcc(kc, a0, nn), st=(kc == 0), sp=(kc == CH - 1):
                                       e.matmul(o, l, r, start=st, stop=sp))
                            t_q = self.tmark()
                            self.wait('v', sT, t_q)
                            self.V(lambda e, a=dstbuf[:, a0 - t0r:a0 - t0r + nn], b=pqk[:, 0:nn]:
                                   e.tensor_copy(a, b))
                            v_e = self.vmark()
                            self.wait('t', sV, v_e)
                    else:
                        for tt in range(kv0 // 128, (ht0 + 1024) // 128):
                            for kc in range(CH):
                                self.T(lambda e, o=pqk[:, 0:128], l=srcc(kc, tt * 128, 128),
                                       r=wsrc[:, kc * 128:kc * 128 + 128], st=(kc == 0), sp=(kc == CH - 1):
                                       e.matmul(o, l, r, start=st, stop=sp))
                            t_q = self.tmark()
                            self.wait('v', sT, t_q)
                            self.V(lambda e, a=dstbuf[:, (tt - kv0 // 128) * 128:(tt - kv0 // 128) * 128 + 128],
                                   b=pqk[:, 0:128]: e.tensor_copy(a, b))
                            v_e = self.vmark()
                            self.wait('t', sV, v_e)
                w_slot_free[slot] = sT.n

                for which, buf, t0r, rot, scol, tcx, tsx in (
                        (0, qb, ht0, self.rotq, qs_col, self.tab_cx, self.tab_sx),
                        (1, kbuf, kv0, self.rotk, ks_col, self.tab_cdx, self.tab_sdx)):
                    nch = (ht0 + 1024 - t0r + 511) // 512
                    rr_full = self.arf[0:2, 0:512 * nch]
                    for tcx_i in range(nch):
                        a0 = t0r + tcx_i * 512
                        ch = buf[:, tcx_i * 512:tcx_i * 512 + 512]
                        self.wait('s', sV, sV.n)
                        self.S(lambda e, a=qsq, b=ch: e.square(a, b).then_inc(self.sS.h, 1))
                        s_sq = sS.inc()
                        self.wait('t', sS, s_sq)
                        pr = self.pC[0:2, 0:512]
                        self.T(lambda e, o=pr, r=qsq: e.matmul(o, self.bd2[:], r, start=True, stop=True))
                        t_st = self.tmark()
                        self.wait('v', sT, t_st)
                        rr = self.arf[0:2, tcx_i * 512:tcx_i * 512 + 512]
                        self.V(lambda e, a=rr, b=pr: e.tensor_scalar(a, b, 1e-24, None, ALU.max))
                        self.vsync()
                        self.V(lambda e, a=rr: e.reciprocal(a, a))
                        v_r = self.vmark()
                        self.wait('t', sV, v_r)
                        self.wait('s', sV, v_r)
                        if which == 0:
                            self.S(lambda e, a=rr: e.activation(
                                a, a, ACT.Sqrt, scale=self.c64[0:2, :]).then_inc(self.sS.h, 1))
                        else:
                            self.S(lambda e, a=rr: e.sqrt(a, a).then_inc(self.sS.h, 1))
                        s_rr = sS.inc()
                        self.wait('t', sS, s_rr)
                        self.wait('t', sV, v_rot)
                        prot = self.pA[:, 0:512]
                        prb_ = self.pB[:, 0:512]
                        self.T(lambda e, o=prot, l=rot, r=ch: e.matmul(o, l[:], r, start=True, stop=True))
                        self.T(lambda e, o=prb_, r=rr: e.matmul(o, self.exp2m[:], r, start=True, stop=True))
                        t_ro = self.tmark()
                        self.wait('v', sT, t_ro)
                        t1f = self.arf[:, 4864:5376]
                        t2f = self.arf[:, 5376:5888]
                        self.V(lambda e, a=t1f, b=ch, s=scol, cc=tcx[:, a0:a0 + 512]:
                               e.scalar_tensor_tensor(a, b, s, cc, ALU.mult, ALU.mult))
                        self.V(lambda e, a=t2f, b=prot, cc=tsx[:, a0:a0 + 512]:
                               e.tensor_mul(a, b, cc))
                        self.V(lambda e, a=t1f, b=t2f: e.tensor_add(a, a, b))
                        self.V(lambda e, a=ch, b=t1f, cc=prb_: e.tensor_mul(a, b, cc))
                        v_ap = self.vmark()
                        self.wait('t', sV, v_ap)

                if half == 0 and pf == 0:
                    self.probe(f"qb{layer}", qb)
                    self.probe(f"kb{layer}", kbuf[:, 0:1024])
                    self.probe(f"vb{layer}", vbuf[:, 0:1024])
                def mk_ote(pf, half):
                    def f(r, pot):
                        dst = oTf[:, pf * 1024 + (r - half * 8) * 128: pf * 1024 + (r - half * 8) * 128 + 128]
                        self.V(lambda e, a=dst, b=pot[:, 0:128]: e.tensor_copy(a, b))
                    return f
                cfg = dict(
                    tiles=list(range(half * 8, half * 8 + 8)),
                    lookback=512,
                    qbuf=lambda b, r: qb[64 * b:64 * b + 64, (r * 128 - ht0):(r * 128 - ht0) + 128],
                    kbuf=lambda b, kw, W: kbuf[64 * b:64 * b + 64, kw - kv0:kw - kv0 + W],
                    vblk=lambda tt, b: vbuf[:, (tt * 128 - kv0) + 64 * b:(tt * 128 - kv0) + 64 * b + 64],
                    mask=lambda r: (self.fm_r[r] if r < 4 else self.fm_int),
                    psa=lambda s: (self.pA, self.pB)[s][:, 0:640],
                    nsa=2,
                    ppt=lambda s: self.pC[:, 0:640],
                    npt=1,
                    pot=lambda s: self.pD[:, 0:128],
                    npot=1,
                    probs=lambda s: prb[s], pts=lambda s: ptsb[s], garb=garb,
                    mx=lambda s: mxc[s], nmx=lambda s: nmxc[s], sg=lambda s: sgc[s],
                    lnsg=lambda s: lnsgc[s], b2=lambda s: b2c[s],
                    otevac=mk_ote(pf, half),
                )
                self.banded_attention(cfg)
                if half == 0 and pf == 0:
                    self.wait('y', self.sV, self.sV.n)
                    self.probe(f"oTf{layer}", oTf[:, 0:1024])

            if half == 0:
                self.wait('y', self.sV, self.sV.n)
                self.probe(f"oTfall{layer}", oTf[:])
            am1 = self.ab.mark()
            wo_sl = [wsl[0][:, 0:512], wsl[0][:, 512:1024]]
            wo_free = [max(w_slot_free), max(w_slot_free)]
            self.wait('t', sV, sV.n)
            self.wait('t', sS, sS.n)
            for fc in range(CH):
                ws = fc % 2
                self.wait('g', sT, wo_free[ws])
                dstw = wo_sl[ws].rearrange("p (k f) -> p k f", f=128)
                srcw = self.d_fwo[layer].rearrange("(k p) f -> p k f", p=128)[:, :, fc * 128:fc * 128 + 128]
                self.G(lambda e, d=dstw, s=srcw: e.dma_start(out=d, in_=s).then_inc(self.sW.h, 16))
                w_wo = sW.inc(16)
                self.wait('t', sW, w_wo)
                for tc in range(2):
                    po = (self.pA, self.pB)[tc % 2][:, 0:512]
                    for kc in range(4):
                        self.T(lambda e, o=po, l=wo_sl[ws][:, kc * 128:kc * 128 + 128],
                               r=oTf[:, kc * 1024 + tc * 512:kc * 1024 + tc * 512 + 512],
                               st=(kc == 0), sp=(kc == 3): e.matmul(o, l, r, start=st, stop=sp))
                    t_wo = self.tmark()
                    self.wait('v', sT, t_wo)
                    xfch = self.xfc(fc, ht0 + tc * 512, 512)
                    self.V(lambda e, a=xfch, b=po: e.tensor_add(a, a, b))
                    v_wo = self.vmark()
                    self.wait('t', sV, v_wo)
                wo_free[ws] = sT.n
            self.ab.free(am1)
            w_slot_free[0] = max(w_slot_free[0], sT.n)
        v_wo_all = sV.n
        self.ab.free(am0)

        self.probe(f"xfwo{layer}", self.xf[:])
        self.wait('s', sV, v_wo_all)
        for c in range(CH):
            self.S(lambda e, a=self.xfbc(c, 0, T), b=self.xfc(c, 0, T): e.copy(a, b))
        s_xfb = self.smark()

        am2 = self.ab.mark()
        sq = [self.ab.buf(T), self.ab.buf(T)]
        s_sq2 = [sS.n + c + 1 for c in range(CH)]
        t_mm2 = [sT.n + c + 1 for c in range(CH)]
        for c in range(CH):
            if c >= 2:
                self.wait('s', sT, t_mm2[c - 2])
            xfsh = self.xfc(c, 0, T, True)
            self.S(lambda e, a=sq[c % 2], b=xfsh: e.square(a, b).then_inc(self.sS.h, 1))
            sS.inc()
        assert sS.n == s_sq2[-1]
        psS = [self.pA[0:1, 0:512], self.pA[0:1, 512:1024],
               self.pB[0:1, 0:512], self.pB[0:1, 512:1024]]
        psQ = [self.pC[0:1, 0:512], self.pC[0:1, 512:1024],
               self.pD[0:1, 0:512], self.pD[0:1, 512:1024]]
        self.wait('t', sS, s_xfb)
        for c in range(CH):
            self.wait('t', sS, s_sq2[c])
            for tc in range(4):
                xbs = self.xfbc(c, tc * 512, 512, True)
                sqch = sq[c % 2][:, tc * 512:tc * 512 + 512]
                st, sp = (c == 0), (c == CH - 1)
                self.T(lambda e, o=psS[tc], r=xbs, st=st, sp=sp:
                       e.matmul(o, self.ones128b, r, start=st, stop=sp))
                self.T(lambda e, o=psQ[tc], r=sqch, st=st, sp=sp:
                       e.matmul(o, self.ones128b, r, start=st, stop=sp))
            self.tmark()
        assert sT.n == t_mm2[-1]
        self.ab.free(am2)
        v_rows = self.ln_rows(psS, psQ, sT.n)

        snap = self.svcol(12)
        self.wait('v', self.sV, sV.n)
        for c in range(4, CH):
            for k in range(1, 4):
                dst = snap[:, (c - 4) * 3 + (k - 1):(c - 4) * 3 + k]
                srcb = self.xfc(c, 512 * k - 1, 1)
                self.V(lambda e, a=dst, b=srcb: e.tensor_copy(a, b))
        v_snap = self.vmark()
        self.probe(f"snap{layer}", snap)

        am3 = self.ab.mark()
        region = self.ab.buf(8192)
        h2n = region[:, 0:4096]
        w1sl = [region[:, 4096:6144], region[:, 6144:8192]]
        w2sl = [region[:, 0:2816], region[:, 2816:5632]]
        actb = self.ab.buf(22 * 512)
        gelb = [self.ab.buf(512), self.ab.buf(512)]
        gc = self.gcol[:, layer * 8:layer * 8 + 8]
        bc = self.bcol[:, layer * 8:layer * 8 + 8]
        b2cc = self.b2col[:, layer * 8:layer * 8 + 8]
        w1_free = [0, 0]
        w2_free = [0, 0]
        v_act = {}
        t_u_last, t_w2_last = 0, 0
        for tc in range(4):
            rs = self.rs_(tc * 512, 512)
            c0 = self.c0_(tc * 512, 512)
            rb = self.pA[:, 0:512]
            cb = self.pA[:, 512:1024]
            self.wait('t', sV, v_rows if tc == 0 else sV.n)
            self.T(lambda e, o=rb, r=rs: e.matmul(o, self.onesrow[:], r, start=True, stop=True))
            self.T(lambda e, o=cb, r=c0: e.matmul(o, self.onesrow[:], r, start=True, stop=True))
            t_bb = self.tmark()
            self.wait('v', sT, t_bb)
            tmpf2 = [self.arf[:, 4352:4864], self.arf[:, 4864:5376]]
            if tc > 0:
                self.wait('s', sT, t_u_last)
                self.wait('v', sT, t_w2_last)
            s_h2 = 0
            s_h2s = [0] * CH
            for c in range(CH):
                tmpf = tmpf2[c % 2]
                xfsh = self.xfc(c, tc * 512, 512, True)
                if c >= 2:
                    self.wait('v', sS, s_h2s[c - 2])
                self.V(lambda e, a=tmpf, b=xfsh, r=rb: e.tensor_mul(a, b, r))
                self.V(lambda e, a=tmpf, r=cb: e.tensor_add(a, a, r))
                if tc >= 1 and c >= 4:
                    sc = snap[:, (c - 4) * 3 + (tc - 1):(c - 4) * 3 + tc]
                    self.vsync()
                    self.V(lambda e, a=tmpf[:, 0:1], b=sc, r=rb[:, 0:1]: e.tensor_mul(a, b, r))
                    self.vsync()
                    self.V(lambda e, a=tmpf[:, 0:1], r=cb[:, 0:1]: e.tensor_add(a, a, r))
                v_t = self.vmark()
                self.wait('s', sV, v_t)
                self.S(lambda e, a=h2n[:, c * 512:c * 512 + 512], b=tmpf,
                       g=gc[:, c:c + 1], bb=bc[:, c:c + 1]:
                       e.activation(a, b, ACT.Identity, bias=bb, scale=g).then_inc(self.sS.h, 1))
                s_h2 = sS.inc()
                s_h2s[c] = s_h2
            s_h2n = s_h2
            if tc == 0:
                self.probe(f"h2n{layer}", h2n)
            if tc == 1:
                self.probe(f"h2nb{layer}", h2n)
            for ci in range(22):
                wslot = ci % 2
                wdt = min(128, 2730 - 128 * ci)
                self.wait('g', sT, w1_free[wslot])
                srcall = self.d_fw1[layer].rearrange("(k p) f -> p k f", p=128)
                for ag in range(2):
                    dstw = w1sl[wslot][:, ag * 1024:ag * 1024 + 1024].rearrange(
                        "p (k f) -> p k f", f=128)[:, :, 0:wdt]
                    s_ = srcall[:, :, 256 * ci + ag * wdt:256 * ci + (ag + 1) * wdt]
                    self.G(lambda e, d=dstw, s=s_:
                           e.dma_start(out=d, in_=s).then_inc(self.sW.h, 16))
                    sW.inc(16)
                w1_ready = sW.n
                ppair = (self.pB, self.pC)[ci % 2]
                pa = ppair[0:wdt, 0:512]
                pg = ppair[0:wdt, 512:1024]
                self.wait('t', sW, w1_ready)
                if ci == 0:
                    self.wait('t', sS, s_h2n)
                if ci >= 2:
                    self.wait('t', sV, v_act[ci - 2])
                for kc in range(CH):
                    self.T(lambda e, o=pa, l=w1sl[wslot][:, kc * 128:kc * 128 + wdt],
                           r=h2n[:, kc * 512:kc * 512 + 512], st=(kc == 0), sp=(kc == CH - 1):
                           e.matmul(o, l, r, start=st, stop=sp))
                for kc in range(CH):
                    self.T(lambda e, o=pg, l=w1sl[wslot][:, 1024 + kc * 128:1024 + kc * 128 + wdt],
                           r=h2n[:, kc * 512:kc * 512 + 512], st=(kc == 0), sp=(kc == CH - 1):
                           e.matmul(o, l, r, start=st, stop=sp))
                t_u = self.tmark()
                w1_free[wslot] = t_u
                t_u_last = t_u
                b1a = self.b1c[:, layer * 44 + 2 * ci:layer * 44 + 2 * ci + 1]
                b1g = self.b1c[:, layer * 44 + 2 * ci + 1:layer * 44 + 2 * ci + 2]
                self.wait('s', sT, t_u)
                if ci >= 2:
                    self.wait('s', sV, v_act[ci - 2])
                gbuf = gelb[ci % 2]
                self.S(lambda e, o=gbuf[0:wdt, 0:512], i=pg, b=b1g[0:wdt, :]:
                       e.activation(o, i, ACT.Gelu, bias=b).then_inc(self.sS.h, 1))
                s_gel = sS.inc()
                self.wait('v', sS, s_gel)
                dst = actb[0:wdt, ci * 512:ci * 512 + 512]
                self.V(lambda e, a=dst, b=pa, s=b1a[0:wdt, :], g=gbuf[0:wdt, 0:512]:
                       e.scalar_tensor_tensor(a, b, s, g, ALU.add, ALU.mult))
                v_act[ci] = self.vmark()
            v_act_all = sV.n
            if tc == 0:
                self.probe(f"act{layer}", actb)
            if tc == 1:
                self.probe(f"actb{layer}", actb)
            for fc in range(CH):
                wslot = fc % 2
                self.wait('g', sT, w2_free[wslot])
                self.wait('g', sT, t_u_last)
                dstw = w2sl[wslot].rearrange("p (k f) -> p k f", f=128)
                src1 = self.d_fw2[layer][0:2688, :].rearrange("(k p) f -> p k f", p=128)[
                    :, :, fc * 128:fc * 128 + 128]
                self.G(lambda e, d=dstw[:, 0:21, :], s=src1:
                       e.dma_start(out=d, in_=s).then_inc(self.sW.h, 16))
                sW.inc(16)
                src2 = self.d_fw2[layer][2688:2730, fc * 128:fc * 128 + 128].rearrange(
                    "(o p) f -> p o f", o=1)
                self.G(lambda e, d=dstw[0:42, 21:22, :], s=src2:
                       e.dma_start(out=d, in_=s).then_inc(self.sW.h, 16))
                w2_ready = sW.inc(16)
                po = (self.pC, self.pD)[fc % 2][:, 0:512]
                if fc < 2:
                    self.wait('t', sV, v_act_all)
                self.wait('t', sW, w2_ready)
                for kc in range(22):
                    wdt = min(128, 2730 - 128 * kc)
                    self.T(lambda e, o=po, l=w2sl[wslot][0:wdt, kc * 128:kc * 128 + 128],
                           r=actb[0:wdt, kc * 512:kc * 512 + 512], st=(kc == 0), sp=(kc == 21):
                           e.matmul(o, l, r, start=st, stop=sp))
                t_w2 = self.tmark()
                w2_free[wslot] = t_w2
                t_w2_last = t_w2
                self.wait('v', sT, t_w2)
                xfch = self.xfc(fc, tc * 512, 512)
                self.V(lambda e, a=xfch, b=po, s=b2cc[:, fc:fc + 1]:
                       e.scalar_tensor_tensor(a, b, s, a, ALU.add, ALU.add))
                self.vmark()
            w1_free[0] = w1_free[1] = t_w2_last
        self.ab.free(am3)
        if layer == 0:
            self.wait('s', sV, sV.n)
            self.wait('s', sT, sT.n)
            for c in range(CH):
                self.S(lambda e, a=self.xfbc(c, 0, T), b=self.xfc(c, 0, T): e.copy(a, b))
            self.s_xfb_next = self.smark()
        self.v_ffn = sV.n
        self.s_ffn = sS.n

    def attn(self):
        sT, sV, sS, sW = self.sT, self.sV, self.sS, self.sW
        am0 = self.ab.mark()
        qb = self.ab.buf(1024)
        kbuf = self.ab.buf(1280)
        vbuf = self.ab.buf(1280)
        wsl = [self.ab.buf(3 * 1024) for _ in range(2)]
        oT = self.ab.buf(CH * 1024)
        wot_sl = [self.ab.buf(1024), self.ab.buf(1024)]
        prb = [self.ab.buf(384) for _ in range(2)]
        ptsb = [self.ab.buf(384) for _ in range(2)]
        garb = self.ab.buf(384)
        mxc = [self.svcol() for _ in range(2)]
        nmxc = [self.svcol() for _ in range(2)]
        sgc = [self.svcol() for _ in range(2)]
        lnsgc = [self.svcol() for _ in range(2)]
        b2c = [self.svcol() for _ in range(2)]

        self.wait('g', sT, sT.n)
        self.wait('g', sV, sV.n)
        self.wait('g', sS, sS.n)
        w_slot_free = [0, 0]
        wot_free = [0, 0]
        for half in range(2):
            ht0 = half * 1024
            kv0 = max(0, ht0 - 256)
            for p in range(CH):
                slot = p % 2
                self.wait('g', sT, w_slot_free[slot])
                for part in range(3):
                    dst = wsl[slot][:, part * 1024:part * 1024 + 1024].rearrange(
                        "p (k f) -> p k f", f=128)
                    src = self.d_wqkvT.rearrange("(k p) f -> p k f", p=128)[
                        :, :, part * 1024 + 128 * p: part * 1024 + 128 * p + 128]
                    self.G(lambda e, d=dst, s=src: e.dma_start(out=d, in_=s).then_inc(self.sW.h, 16))
                    sW.inc(16)
                w_ready = sW.n
                self.wait('t', sW, w_ready)
                self.wait('t', sV, sV.n)
                self.wait('t', sS, sS.n)
                pqk = self.pD[:, 512:1024]
                for part, dstbuf, t0r in ((0, qb, ht0), (1, kbuf, kv0), (2, vbuf, kv0)):
                    wsrc = wsl[slot][:, part * 1024:part * 1024 + 1024]
                    if part < 2:
                        for a0 in range(t0r, ht0 + 1024, 512):
                            nn = min(512, ht0 + 1024 - a0)
                            for kc in range(CH):
                                self.T(lambda e, o=pqk[:, 0:nn], l=wsrc[:, kc * 128:kc * 128 + 128],
                                       r=self.xbc(kc, a0, nn), st=(kc == 0), sp=(kc == CH - 1):
                                       e.matmul(o, l, r, start=st, stop=sp))
                            t_q = self.tmark()
                            self.wait('v', sT, t_q)
                            bcl = self.bqc[:, part * 8 + p:part * 8 + p + 1]
                            sc2 = 0.125 if part == 0 else None
                            op1 = ALU.mult if part == 0 else ALU.bypass
                            self.V(lambda e, a=dstbuf[:, a0 - t0r:a0 - t0r + nn], b=pqk[:, 0:nn],
                                   s=bcl, s2=sc2, o1=op1: e.tensor_scalar(a, b, s, s2, ALU.add, o1))
                            v_e = self.vmark()
                            self.wait('t', sV, v_e)
                    else:
                        for tt in range(kv0 // 128, (ht0 + 1024) // 128):
                            for kc in range(CH):
                                self.T(lambda e, o=pqk[:, 0:128], l=self.xbc(kc, tt * 128, 128),
                                       r=wsrc[:, kc * 128:kc * 128 + 128], st=(kc == 0), sp=(kc == CH - 1):
                                       e.matmul(o, l, r, start=st, stop=sp))
                            t_q = self.tmark()
                            self.wait('v', sT, t_q)
                            self.V(lambda e, a=vbuf[:, (tt - kv0 // 128) * 128:(tt - kv0 // 128) * 128 + 128],
                                   b=pqk[:, 0:128]: e.tensor_copy(a, b))
                            v_e = self.vmark()
                            self.wait('t', sV, v_e)
                w_slot_free[slot] = sT.n

                if half == 0 and p == 0:
                    self.probe("aqb", qb)
                    self.probe("akb", kbuf[:, 0:1024])
                    self.probe("avb", vbuf[:, 0:1024])
                def mk_ote(p, half):
                    def f(r, pot):
                        dst = oT[:, p * 1024 + (r - half * 8) * 128: p * 1024 + (r - half * 8) * 128 + 128]
                        bv = self.bqc[:, 16 + p:16 + p + 1]
                        self.V(lambda e, a=dst, b=pot[:, 0:128], s=bv:
                               e.tensor_scalar(a, b, s, None, ALU.add))
                    return f
                cfg = dict(
                    tiles=list(range(half * 8, half * 8 + 8)),
                    lookback=256,
                    qbuf=lambda b, r: qb[64 * b:64 * b + 64, (r * 128 - ht0):(r * 128 - ht0) + 128],
                    kbuf=lambda b, kw, W: kbuf[64 * b:64 * b + 64, kw - kv0:kw - kv0 + W],
                    vblk=lambda tt, b: vbuf[:, (tt * 128 - kv0) + 64 * b:(tt * 128 - kv0) + 64 * b + 64],
                    mask=lambda r: (self.am_r0 if r == 0 else (self.am_r1 if r == 1 else self.am_int)),
                    psa=lambda s: (self.pA[:, 0:384], self.pA[:, 512:896], self.pB[:, 0:384])[s],
                    nsa=3,
                    ppt=lambda s: (self.pB[:, 512:896], self.pC[:, 0:384])[s],
                    npt=2,
                    pot=lambda s: (self.pC[:, 512:640], self.pD[:, 0:128])[s],
                    npot=2,
                    probs=lambda s: prb[s], pts=lambda s: ptsb[s], garb=garb,
                    mx=lambda s: mxc[s], nmx=lambda s: nmxc[s], sg=lambda s: sgc[s],
                    lnsg=lambda s: lnsgc[s], b2=lambda s: b2c[s],
                    otevac=mk_ote(p, half),
                )
                self.banded_attention(cfg)
                if half == 0 and p == 0:
                    self.wait('y', sV, sV.n)
                    self.probe("aoT", oT[:, 0:1024])

            self.wait('t', sV, sV.n)
            self.wait('t', sS, sS.n)
            if half == 1:
                self.probe("aoTall", oT[:])
            for fc in range(CH):
                ws = fc % 2
                self.wait('g', sT, wot_free[ws])
                dstw = wot_sl[ws].rearrange("p (k f) -> p k f", f=128)
                srcw = self.d_woT.rearrange("(k p) f -> p k f", p=128)[:, :, fc * 128:fc * 128 + 128]
                self.G(lambda e, d=dstw, s=srcw: e.dma_start(out=d, in_=s).then_inc(self.sW.h, 16))
                w_wo = sW.inc(16)
                self.wait('t', sW, w_wo)
                if half == 1 and fc == 7:
                    self.wait('y', sW, w_wo)
                    self.probe("wotfc7", wot_sl[ws])
                for tc in range(2):
                    po = (self.pA, self.pB)[tc % 2][:, 0:512]
                    for kc in range(CH):
                        self.T(lambda e, o=po, l=wot_sl[ws][:, kc * 128:kc * 128 + 128],
                               r=oT[:, kc * 1024 + tc * 512:kc * 1024 + tc * 512 + 512],
                               st=(kc == 0), sp=(kc == CH - 1): e.matmul(o, l, r, start=st, stop=sp))
                    t_wo = self.tmark()
                    self.wait('v', sT, t_wo)
                    xfch = self.xfc(fc, ht0 + tc * 512, 512)
                    self.V(lambda e, a=xfch, b=po, s=self.boc[:, fc:fc + 1]:
                           e.scalar_tensor_tensor(a, b, s, a, ALU.add, ALU.add))
                    v_wo = self.vmark()
                    self.wait('t', sV, v_wo)
                wot_free[ws] = sT.n
            if half == 1:
                self.wait('y', sV, sV.n)
                self.probe("aoT1", oT[:, 7 * 1024:8 * 1024])
                self.probe("awo", self.xf[:])
        self.ab.free(am0)

    def finish(self):
        sV, sX, sW = self.sV, self.sX, self.sW
        self.probe("xff", self.xf[:])
        self.wait('g', sV, sV.n)
        self.wait('g', self.sS, self.sS.n)
        self.wait('g', self.sT, self.sT.n)
        for c in range(CH):
            dst = self.xfc(c, 0, T)
            src = self.d_x[c * 128:(c + 1) * 128, :]
            self.G(lambda e, d=dst, s=src: e.dma_start(out=d, in_=s, accum_op=ALU.add).then_inc(self.sW.h, 16))
            sW.inc(16)
        g_acc = sW.n
        self.wait('y', sW, g_acc)
        for c in range(CH):
            dst = self.d_out[c * 128:(c + 1) * 128, :]
            src = self.xfc(c, 0, T)
            self.Y(lambda e, d=dst, s=src: e.dma_start(out=d, in_=s).then_inc(self.sX.h, 16))
            sX.inc(16)
        self.wait('y', sX, sX.n)


def build(stage=99, probes=()):
    return Builder(stage, probes).nc


def _reorder_w1(w1, b1):
    w1o = np.empty_like(w1)
    b1o = np.empty_like(b1)
    a_w, g_w = w1[:, :, :2730], w1[:, :, 2730:]
    a_b, g_b = b1[:, :2730], b1[:, 2730:]
    o = 0
    for c in range(22):
        wdt = min(128, 2730 - 128 * c)
        w1o[:, :, o:o + wdt] = a_w[:, :, 128 * c:128 * c + wdt]
        b1o[:, o:o + wdt] = a_b[:, 128 * c:128 * c + wdt]
        o += wdt
        w1o[:, :, o:o + wdt] = g_w[:, :, 128 * c:128 * c + wdt]
        b1o[:, o:o + wdt] = g_b[:, 128 * c:128 * c + wdt]
        o += wdt
    assert o == 5460
    return w1o, b1o


def kernel(L0_attn_x, L0_attn_Wqkv, L0_attn_bqkv, L0_attn_Wo, L0_attn_bo,
           L0_ffn_wqkv, L0_ffn_q_scale, L0_ffn_k_scale, L0_ffn_wo,
           L0_ffn_ln_g, L0_ffn_ln_b, L0_ffn_w1, L0_ffn_b1, L0_ffn_w2,
           L0_ffn_b2):
    from concourse.bass_utils import run_bass_kernel_spmd
    f32 = lambda a: np.ascontiguousarray(np.asarray(a, dtype=np.float32))
    x = f32(L0_attn_x)
    w1r, b1r = _reorder_w1(f32(L0_ffn_w1), f32(L0_ffn_b1))
    shared = {
        "wqkvT": np.ascontiguousarray(f32(L0_attn_Wqkv).T),
        "bqkv": f32(L0_attn_bqkv),
        "woT": np.ascontiguousarray(f32(L0_attn_Wo).T),
        "bo": f32(L0_attn_bo),
        "fwqkv": f32(L0_ffn_wqkv),
        "fqs": f32(L0_ffn_q_scale),
        "fks": f32(L0_ffn_k_scale),
        "fwo": f32(L0_ffn_wo),
        "flng": f32(L0_ffn_ln_g),
        "flnb": f32(L0_ffn_ln_b),
        "fw1r": np.ascontiguousarray(w1r),
        "fb1r": np.ascontiguousarray(b1r),
        "fw2": f32(L0_ffn_w2),
        "fb2": f32(L0_ffn_b2),
    }
    in_maps = [dict(shared, xT=np.ascontiguousarray(x[i].T)) for i in range(8)]
    nc = Builder(99).nc
    res = run_bass_kernel_spmd(nc, in_maps, core_ids=list(range(8)))
    out = np.stack([np.asarray(res.results[i]["outT"], dtype=np.float32).T
                    for i in range(8)])
    return out


# version 11
# speedup vs baseline: 1.1768x; 1.1768x over previous
import sys, os
if '/opt/trn_rl_repo' not in sys.path:
    sys.path.insert(0, '/opt/trn_rl_repo')
import numpy as np
from contextlib import ExitStack
import concourse.bass as bass
import concourse.mybir as mybir

F32 = mybir.dt.float32
BF16 = mybir.dt.bfloat16
ALU = mybir.AluOpType
ACT = mybir.ActivationFunctionType

D, T = 1024, 2048
TG = T + 1
CH, NT = 8, 16
NEG = -3.0e38
PI = float(np.pi)


class Sem:
    def __init__(self, es, nc, name):
        self.h = es.enter_context(nc.semaphore(name))
        self.n = 0

    def inc(self, amt=1):
        self.n += amt
        return self.n


class Arena:
    def __init__(self, t, n):
        self.t, self.n, self.top = t, n, 0

    def buf(self, w):
        c = self.top
        self.top += w
        assert self.top <= self.n, f"arena overflow {self.top} > {self.n}"
        return self.t[:, c:c + w]

    def mark(self):
        return self.top

    def free(self, m):
        self.top = m


class Builder:
    def __init__(self, stage=99, probes=()):
        self.stage = stage
        self.probes = set(probes)
        self.dbg_tensors = {}
        nc = bass.Bass('TRN2', target_bir_lowering=False, detect_race_conditions=False)
        self.nc = nc
        self.es = ExitStack()
        es = self.es
        dt = lambda n, sh, dty=F32, kind="ExternalInput": nc.dram_tensor(n, sh, dty, kind=kind).ap()
        self.d_x = dt("xT", [D, T])
        self.d_wqkvT = dt("wqkvT", [D, 3 * D])
        self.d_bqkv = dt("bqkv", [3 * D])
        self.d_woT = dt("woT", [D, D])
        self.d_bo = dt("bo", [D])
        self.d_fwqkv = dt("fwqkv", [2, D, 1536])
        self.d_fqs = dt("fqs", [2, 64])
        self.d_fks = dt("fks", [2, 64])
        self.d_fwo = dt("fwo", [2, 512, D])
        self.d_flng = dt("flng", [2, D])
        self.d_flnb = dt("flnb", [2, D])
        self.d_fw1 = dt("fw1r", [2, D, 5460])
        self.d_fb1 = dt("fb1r", [2, 5460])
        self.d_fw2 = dt("fw2", [2, 2730, D])
        self.d_fb2 = dt("fb2", [2, D])
        self.d_out = dt("outT", [D, T], F32, "ExternalOutput")

        sb = lambda n, sh, dty: es.enter_context(nc.sbuf_tensor(n, sh, dty))
        self.xf = sb("xf", [128, CH * TG], F32)
        self.xb = sb("xb", [128, CH * TG], BF16)
        self.xfb = sb("xfb", [128, CH * TG], BF16)
        self.sv = sb("sv", [128, 80], F32)
        self.svb = sb("svb", [128, 16], BF16)
        self.ident = sb("ident", [128, 128], BF16)
        self.r01 = sb("r01", [128, 128], BF16)
        self.rotq = sb("rotq", [128, 128], BF16)
        self.rotk = sb("rotk", [128, 128], BF16)
        self.onesrow = sb("onesrow", [1, 128], F32)
        self.bd2 = sb("bd2", [128, 2], BF16)
        self.exp2m = sb("exp2m", [2, 128], F32)
        self.bqc = sb("bqc", [128, 24], F32)
        self.boc = sb("boc", [128, 8], F32)
        self.gcol = sb("gcol", [128, 16], F32)
        self.bcol = sb("bcol", [128, 16], F32)
        self.b2col = sb("b2col", [128, 16], F32)
        self.b1c = sb("b1c", [128, 88], F32)
        self.qsc = sb("qsc", [128, 8], F32)
        self.AF = 6400
        self.arf = sb("arf", [128, self.AF], F32)
        self.si32 = sb("si32", [128, 512], mybir.dt.int32)
        rem0 = nc.sbuf_bytes_remaining
        ABB = (rem0 - 256) // 2 // 128 * 128
        self.AB = ABB
        self.arb = sb("arb", [128, self.AB], BF16)
        ps = lambda n: es.enter_context(nc.psum_tensor(n, [128, 1024], F32))
        self.pA, self.pB, self.pC, self.pD = ps("pA"), ps("pB"), ps("pC"), ps("pD")

        self.sW = Sem(es, nc, "sW")
        self.sX = Sem(es, nc, "sX")
        self.sT = Sem(es, nc, "sT")
        self.sV = Sem(es, nc, "sV")
        self.sS = Sem(es, nc, "sS")
        self.sG = Sem(es, nc, "sG")

        self.prog = {k: [] for k in "tvsgy"}
        self._svcol = 0
        self.ab = Arena(self.arb, self.AB)
        self.build()

    def T(self, f): self.prog['t'].append(f)
    def V(self, f): self.prog['v'].append(f)
    def S(self, f): self.prog['s'].append(f)
    def G(self, f): self.prog['g'].append(f)
    def Y(self, f): self.prog['y'].append(f)

    def wait(self, eng, sem, n):
        if n > 0:
            self.prog[eng].append(lambda e: e.wait_ge(sem.h, n))

    def probe(self, name, ap):
        if name not in self.probes:
            return
        sh = list(ap.shape)
        d = self.nc.dram_tensor("dbg_" + name, sh, ap.dtype, kind="ExternalOutput").ap()
        self.dbg_tensors[name] = sh
        self.wait('y', self.sT, self.sT.n)
        self.wait('y', self.sV, self.sV.n)
        self.wait('y', self.sS, self.sS.n)
        self.Y(lambda e, dd=d, s=ap: e.dma_start(out=dd, in_=s).then_inc(self.sX.h, 16))
        pd = self.sX.inc(16)
        for eng in 'tvsg':
            self.wait(eng, self.sX, pd)

    def _mark(self, key, sem):
        lst = self.prog[key]
        f = lst[-1]
        lst[-1] = (lambda e, f=f, s=sem.h: f(e).then_inc(s, 1))
        return sem.inc()

    def tmark(self):
        return self._mark('t', self.sT)

    def vsync(self):
        self.wait('v', self.sV, self.vmark())

    def ssync(self):
        self.wait('s', self.sS, self.smark())

    def vmark(self):
        return self._mark('v', self.sV)

    def smark(self):
        return self._mark('s', self.sS)

    def svcol(self, w=1):
        c = self._svcol
        self._svcol += w
        assert self._svcol <= 80
        return self.sv[:, c:c + w]

    def xfc(self, c, t0, n, shift=False):
        o = c * TG + 1 + t0 - (1 if (shift and c >= 4) else 0)
        return self.xf[:, o:o + n]

    def xbc(self, c, t0, n, shift=False):
        o = c * TG + 1 + t0 - (1 if (shift and c >= 4) else 0)
        return self.xb[:, o:o + n]

    def xfbc(self, c, t0, n, shift=False):
        o = c * TG + 1 + t0 - (1 if (shift and c >= 4) else 0)
        return self.xfb[:, o:o + n]

    def rs_(self, c0, n): return self.arf[0:1, c0:c0 + n]
    def c0_(self, c0, n): return self.arf[0:1, 2048 + c0:2048 + c0 + n]
    def mu_(self, c0, n): return self.arf[0:1, 4096 + c0:4096 + c0 + n]

    def build(self):
        self.consts()
        self.ln1()
        if self.stage >= 1:
            for layer in range(min(2, self.stage)):
                self.ffn_layer(layer)
        if self.stage >= 3:
            self.attn()
        self.finish()
        nc = self.nc
        with nc.Block() as blk:
            @blk.tensor
            def _(e):
                for f in self.prog['t']: f(e)

            @blk.vector
            def _(e):
                for f in self.prog['v']: f(e)

            @blk.scalar
            def _(e):
                for f in self.prog['s']: f(e)

            @blk.gpsimd
            def _(e):
                for f in self.prog['g']: f(e)

            @blk.sync
            def _(e):
                for f in self.prog['y']: f(e)
        self.es.close()

    def consts(self):
        sX, sG = self.sX, self.sG
        for c in range(CH):
            dst, src = self.xfc(c, 0, T), self.d_x[c * 128:(c + 1) * 128, :]
            self.Y(lambda e, d=dst, s=src: e.dma_start(out=d, in_=s).then_inc(self.sX.h, 16))
            sX.inc(16)
        self.x_loaded = sX.n

        def colload(dst, src):
            self.Y(lambda e, d=dst, s=src: e.dma_start(
                out=d, in_=s, allow_slow_non_contiguous=True).then_inc(self.sX.h, 16))
            return sX.inc(16)

        colload(self.bqc[:], self.d_bqkv.rearrange("(j p) -> p j", p=128))
        colload(self.boc[:], self.d_bo.rearrange("(j p) -> p j", p=128))
        for l in range(2):
            colload(self.gcol[:, l * 8:l * 8 + 8], self.d_flng[l].rearrange("(j p) -> p j", p=128))
            colload(self.bcol[:, l * 8:l * 8 + 8], self.d_flnb[l].rearrange("(j p) -> p j", p=128))
            colload(self.b2col[:, l * 8:l * 8 + 8], self.d_fb2[l].rearrange("(j p) -> p j", p=128))
            colload(self.b1c[:, l * 44:l * 44 + 42], self.d_fb1[l, 0:5376].rearrange("(j p) -> p j", p=128))
            colload(self.b1c[0:42, l * 44 + 42:l * 44 + 43], self.d_fb1[l, 5376:5418].rearrange("(p o) -> p o", o=1))
            colload(self.b1c[0:42, l * 44 + 43:l * 44 + 44], self.d_fb1[l, 5418:5460].rearrange("(p o) -> p o", o=1))
            for j, dten in ((0, self.d_fqs), (1, self.d_fks)):
                colload(self.qsc[0:64, l * 4 + j:l * 4 + j + 1], dten[l].rearrange("(p o) -> p o", o=1))
                colload(self.qsc[64:128, l * 4 + j:l * 4 + j + 1], dten[l].rearrange("(p o) -> p o", o=1))
                for hp in range(4):
                    p0 = hp * 32
                    s0 = (p0 + 32) % 64
                    src32 = dten[l, s0:s0 + 32].rearrange("(p o) -> p o", o=1)
                    colload(self.qsc[p0:p0 + 32, l * 4 + 2 + j:l * 4 + 3 + j], src32)
        self.x_bias = sX.n

        iot = self.svcol()
        self.iot = iot
        self.G(lambda e: e.iota(iot, [[0, 1]], base=0, channel_multiplier=1,
                                allow_small_or_imprecise_dtypes=True).then_inc(self.sG.h, 1))
        self.g_iota = sG.inc()
        self.G(lambda e: e.memset(self.ident[:], 1.0).then_inc(self.sG.h, 1))
        m1 = sG.inc()
        self.wait('g', sG, m1)
        self.G(lambda e: e.affine_select(self.ident[:], self.ident[:], [[-1, 128]],
                                         ALU.is_equal, 0.0, base=0,
                                         channel_multiplier=1).then_inc(self.sG.h, 1))
        self.g_ident = sG.inc()
        for b in range(2):
            p0 = b * 64
            hi = self.r01[p0:p0 + 32, :]
            lo = self.r01[p0 + 32:p0 + 64, :]
            self.G(lambda e, a=hi: e.memset(a, 1.0).then_inc(self.sG.h, 1))
            self.G(lambda e, a=lo: e.memset(a, -1.0).then_inc(self.sG.h, 1))
        m2 = sG.inc(4)
        self.wait('g', sG, m2)
        for b in range(2):
            p0 = b * 64
            hi = self.r01[p0:p0 + 32, :]
            lo = self.r01[p0 + 32:p0 + 64, :]
            self.G(lambda e, a=hi, base=p0 + 32: e.affine_select(
                a, a, [[-1, 128]], ALU.is_equal, 0.0, base=base,
                channel_multiplier=1).then_inc(self.sG.h, 1))
            self.G(lambda e, a=lo, base=p0: e.affine_select(
                a, a, [[-1, 128]], ALU.is_equal, 0.0, base=base,
                channel_multiplier=1).then_inc(self.sG.h, 1))
        self.g_r01 = sG.inc(4)
        ob = self.svb[:, 0:1]
        self.ones128b = ob
        self.G(lambda e: e.memset(ob, 1.0).then_inc(self.sG.h, 1))
        self.G(lambda e: e.memset(self.onesrow[:], 1.0).then_inc(self.sG.h, 1))
        self.G(lambda e: e.memset(self.bd2[:], 0.0).then_inc(self.sG.h, 1))
        self.G(lambda e: e.memset(self.exp2m[:], 0.0).then_inc(self.sG.h, 1))
        m3 = sG.inc(4)
        self.wait('g', sG, m3)
        self.G(lambda e: e.memset(self.bd2[0:64, 0:1], 1.0).then_inc(self.sG.h, 1))
        self.G(lambda e: e.memset(self.bd2[64:128, 1:2], 1.0).then_inc(self.sG.h, 1))
        self.G(lambda e: e.memset(self.exp2m[:], 1.0).then_inc(self.sG.h, 1))
        mepre = sG.inc(3)
        self.wait('g', sG, mepre)
        self.G(lambda e: e.affine_select(self.exp2m[:], self.exp2m[:], [[1, 128]],
                                         ALU.is_ge, 0.0, base=0,
                                         channel_multiplier=-64).then_inc(self.sG.h, 1))
        me2 = sG.inc()
        self.wait('g', sG, me2)
        self.G(lambda e: e.affine_select(self.exp2m[:], self.exp2m[:], [[-1, 128]],
                                         ALU.is_ge, 0.0, base=63,
                                         channel_multiplier=64).then_inc(self.sG.h, 1))
        self.g_const = sG.inc()

        def mkmask(W, lo, hi):
            t = self.ab.buf(W)
            self.G(lambda e, a=t: e.memset(a, 0.0).then_inc(self.sG.h, 1))
            mset = sG.inc()
            self.wait('g', sG, mset)
            if lo is not None:
                self.G(lambda e, a=t, b=-lo: e.affine_select(
                    a, a, [[1, W]], ALU.is_ge, -1.0e9, base=b,
                    channel_multiplier=-1).then_inc(self.sG.h, 1))
                self.wait('g', sG, sG.inc())
            if hi is not None:
                self.G(lambda e, a=t, b=hi: e.affine_select(
                    a, a, [[-1, W]], ALU.is_ge, -1.0e9, base=b,
                    channel_multiplier=1).then_inc(self.sG.h, 1))
                self.wait('g', sG, sG.inc())
            return t
        self.am_int = mkmask(384, 1, 256)
        self.am_r0 = mkmask(128, None, 0)
        self.am_r1 = mkmask(256, None, 128)
        self.fm_int = mkmask(640, 0, 512)
        self.fm_r = [mkmask(128 * (r + 1), None, 128 * r) for r in range(4)]
        self.g_masks = sG.n

        self.wait('v', sG, self.g_iota)
        self.npib = self.svcol()
        self.V(lambda e, a=self.npib: e.memset(a, -PI))
        self.c64 = self.svcol()
        self.V(lambda e, a=self.c64: e.memset(a, 64.0))
        self.c0125 = self.svcol()
        self.V(lambda e, a=self.c0125: e.memset(a, 0.125))
        self.cln8 = self.svcol()
        self.V(lambda e, a=self.cln8: e.memset(a, float(np.log(8.0))))
        self.cm05 = self.svcol()
        self.V(lambda e, a=self.cm05: e.memset(a, -0.5))
        self.bqsc = self.svcol(8)
        self.wait('v', self.sX, self.x_bias)
        self.V(lambda e, a=self.bqsc: e.tensor_scalar(a, self.bqc[:, 0:8], 0.125, None, ALU.mult))
        self.v_vecs = self.vmark()

    def ln_rows(self, psS, psQ, t_stats):
        krow = int(os.environ.get('KBROW', '99'))
        sV, sS = self.sV, self.sS
        self.wait('v', self.sT, t_stats)
        for tc in range(4):
            mu = self.mu_(tc * 512, 512)
            rs = self.rs_(tc * 512, 512)
            c0 = self.c0_(tc * 512, 512)
            self.V(lambda e, a=mu, b=psS[tc]: e.tensor_scalar(a, b, 1.0 / D, None, ALU.mult))
            self.V(lambda e, a=rs, b=psQ[tc]: e.tensor_scalar(a, b, 1.0 / D, 1e-5, ALU.mult, ALU.add))
            self.vsync()
            self.V(lambda e, a=c0, b=mu: e.tensor_mul(a, b, b))
            self.vsync()
            self.V(lambda e, a=rs, b=c0: e.tensor_sub(a, a, b))
            self.vsync()
            self.V(lambda e, a=rs: e.reciprocal(a, a))
        v1 = self.vmark()
        if krow < 2:
            return v1
        self.wait('s', sV, v1)
        for tc in range(4):
            rs = self.rs_(tc * 512, 512)
            self.S(lambda e, a=rs: e.sqrt(a, a))
        s1 = self.smark()
        if krow < 3:
            return v1
        self.wait('v', sS, s1)
        for tc in range(4):
            mu = self.mu_(tc * 512, 512)
            rs = self.rs_(tc * 512, 512)
            c0 = self.c0_(tc * 512, 512)
            self.V(lambda e, a=c0, b=mu, r=rs: e.scalar_tensor_tensor(a, b, -1.0, r, ALU.mult, ALU.mult))
        return self.vmark()

    def ln1(self):
        sub = int(os.environ.get('KBSUB', '99'))
        sT, sV, sS, sX = self.sT, self.sV, self.sS, self.sX
        for c in range(CH):
            for buf in (self.xf, self.xb, self.xfb):
                g = buf[:, c * TG:c * TG + 1]
                self.V(lambda e, a=g: e.memset(a, 0.0))
        self.v_guard = self.vmark()

        if sub < 1:
            return
        am = self.ab.mark()
        sq = [self.ab.buf(T), self.ab.buf(T)]
        s_cast = [self.sS.n + 2 * c + 2 for c in range(CH)]
        t_sqmm = [self.sT.n + c + 1 for c in range(CH)]
        self.wait('s', sX, self.x_loaded)
        for c in range(CH):
            xbch, xfch = self.xbc(c, 0, T), self.xfc(c, 0, T)
            if c >= 2:
                self.wait('s', sT, t_sqmm[c - 2])
            self.S(lambda e, a=xbch, b=xfch: e.copy(a, b).then_inc(self.sS.h, 1))
            sS.inc()
            self.S(lambda e, a=sq[c % 2], b=xbch: e.square(a, b).then_inc(self.sS.h, 1))
            sS.inc()
        assert sS.n == s_cast[-1]

        if sub < 2:
            return
        psS = [self.pA[0:1, 0:512], self.pA[0:1, 512:1024],
               self.pB[0:1, 0:512], self.pB[0:1, 512:1024]]
        psQ = [self.pC[0:1, 0:512], self.pC[0:1, 512:1024],
               self.pD[0:1, 0:512], self.pD[0:1, 512:1024]]
        ob = self.ones128b
        self.wait('t', self.sG, self.g_const)
        for c in range(CH):
            self.wait('t', sS, s_cast[c])
            for tc in range(4):
                xbch = self.xbc(c, tc * 512, 512)
                sqch = sq[c % 2][:, tc * 512:tc * 512 + 512]
                st, sp = (c == 0), (c == CH - 1)
                self.T(lambda e, o=psS[tc], r=xbch, st=st, sp=sp:
                       e.matmul(o, ob, r, start=st, stop=sp))
                self.T(lambda e, o=psQ[tc], r=sqch, st=st, sp=sp:
                       e.matmul(o, ob, r, start=st, stop=sp))
            self.tmark()
        assert sT.n == t_sqmm[-1]
        self.ab.free(am)

        if sub < 3:
            return
        v_rows = self.ln_rows(psS, psQ, sT.n)

        if sub < 4:
            return
        orow = self.onesrow[:]
        for tc in range(4):
            rs = self.rs_(tc * 512, 512)
            c0 = self.c0_(tc * 512, 512)
            rb = self.pA[:, (tc % 2) * 512:(tc % 2) * 512 + 512]
            cb = self.pB[:, (tc % 2) * 512:(tc % 2) * 512 + 512]
            self.wait('t', sV, v_rows if tc < 2 else sV.n)
            self.T(lambda e, o=rb, r=rs: e.matmul(o, orow, r, start=True, stop=True))
            self.T(lambda e, o=cb, r=c0: e.matmul(o, orow, r, start=True, stop=True))
            t_b = self.tmark()
            self.wait('v', sT, t_b)
            for c in range(CH):
                xfch = self.xfc(c, tc * 512, 512)
                xbch = self.xbc(c, tc * 512, 512)
                self.V(lambda e, a=xfch, r=rb: e.tensor_mul(a, a, r))
                self.V(lambda e, a=xfch, r=cb: e.tensor_add(a, a, r))
                self.V(lambda e, a=xbch, b=xfch: e.tensor_copy(a, b))
            self.vmark()
        self.v_ln1 = sV.n

    def ffn_tables(self):
        sT, sV, sS = self.sT, self.sV, self.sS
        self.tab_cx = self.ab.buf(T)
        self.tab_sx = self.ab.buf(T)
        self.tab_cdx = self.ab.buf(T)
        self.tab_sdx = self.ab.buf(T)
        trow = self.arf[0:1, 0:T]
        prow = self.arf[0:1, 2048:4096]
        invf = self.arf[0:1, 4096:4224]
        lnb = self.arf[0:1, 4224:4352]
        hrow = self.arf[0:1, 4480:4608]
        xs = self.arf[:, 4864:5376]
        mm = self.arf[:, 5376:5888]
        sn = self.arf[:, 4352:4864]
        rxs = self.arf[:, 5888:6400]
        self.wait('g', self.sV, self.sV.n)
        self.wait('g', self.sT, self.sT.n)
        self.G(lambda e: e.iota(hrow, [[0, 4], [2, 32]], base=0, channel_multiplier=0,
                                allow_small_or_imprecise_dtypes=True).then_inc(self.sG.h, 1))
        self.G(lambda e: e.iota(trow, [[1, T]], base=0, channel_multiplier=0,
                                allow_small_or_imprecise_dtypes=True).then_inc(self.sG.h, 1))
        g_rows = self.sG.inc(2)
        self.wait('v', self.sG, g_rows)
        self.wait('v', sV, self.v_ln1)
        LN1E4 = float(np.log(10000.0))
        self.V(lambda e, s=-LN1E4 / 64.0: e.tensor_scalar(invf, hrow, s, None, ALU.mult))
        self.V(lambda e: e.tensor_scalar(lnb, hrow, 25.6, 1.0 / 89.6, ALU.add, ALU.mult))
        self.V(lambda e: e.tensor_scalar(prow, trow, -1024.0, 1.0 / 256.0, ALU.add, ALU.mult))
        v_pre = self.vmark()
        self.wait('s', sV, v_pre)
        self.S(lambda e: e.activation(invf, invf, ACT.Exp))
        self.S(lambda e: e.activation(lnb, lnb, ACT.Ln))
        s_pre = self.smark()
        if "arfrow0" in self.probes and "arfrow0" not in self.dbg_tensors:
            self.probe("arfrow0", self.arf[0:1, 0:4608])
        self.wait('t', sS, s_pre)
        self.wait('t', sV, sV.n)
        self.wait('t', sS, sS.n)
        for tc in range(4):
            fr = self.pA[:, 0:512]
            lx = self.pB[:, 0:512]
            trch = trow[:, tc * 512:tc * 512 + 512]
            prch = prow[:, tc * 512:tc * 512 + 512]
            if tc > 0:
                self.wait('t', sV, sV.n)
            self.T(lambda e, o=fr, r=trch: e.matmul(o, invf, r, start=True, stop=True))
            self.T(lambda e, o=lx, r=prch: e.matmul(o, lnb, r, start=True, stop=True))
            t_o = self.tmark()
            cxc = self.tab_cx[:, tc * 512:tc * 512 + 512]
            sxc = self.tab_sx[:, tc * 512:tc * 512 + 512]
            cdc = self.tab_cdx[:, tc * 512:tc * 512 + 512]
            sdc = self.tab_sdx[:, tc * 512:tc * 512 + 512]
            self.wait('v', sT, t_o)
            I2PI = 1.0 / (2 * PI)
            ki = self.si32[:]
            def rr(buf, extra_turn, extra_add):
                self.V(lambda e, a=buf, b=fr: e.tensor_scalar(a, b, I2PI, extra_turn, ALU.mult, ALU.add)
                       if extra_turn is not None else e.tensor_scalar(a, b, I2PI, None, ALU.mult))
                self.vsync()
                self.V(lambda e, a=ki, b=buf: e.tensor_copy(a, b))
                self.vsync()
                self.V(lambda e, a=buf, b=ki: e.tensor_copy(a, b))
                self.vsync()
                self.V(lambda e, a=buf, b=fr: e.scalar_tensor_tensor(a, a, -2 * PI, b, ALU.mult, ALU.add))
                if extra_add:
                    self.vsync()
                    self.V(lambda e, a=buf: e.tensor_scalar(a, a, extra_add, None, ALU.add))
                self.vsync()
                self.V(lambda e, a=rxs, b=buf: e.tensor_scalar(a, b, PI, None, ALU.is_gt))
                self.vsync()
                self.V(lambda e, a=buf, b=rxs: e.scalar_tensor_tensor(a, b, -2 * PI, a, ALU.mult, ALU.add))
                self.vsync()
                self.V(lambda e, a=buf: e.tensor_scalar(a, a, PI, -PI, ALU.min, ALU.max))
            rr(sn, None, 0.0)
            v1 = self.vmark()
            self.wait('s', sV, v1)
            self.S(lambda e, a=sn: e.activation(a, a, ACT.Sin))
            s1 = self.smark()
            self.wait('v', sS, s1)
            rr(mm, 0.25, PI / 2)
            v2 = self.vmark()
            self.wait('s', sV, v2)
            self.S(lambda e, a=mm: e.activation(a, a, ACT.Sin))
            self.S(lambda e, a=xs, b=lx: e.activation(a, b, ACT.Exp))
            s2 = self.smark()
            self.wait('v', sS, s2)
            self.V(lambda e, a=rxs, b=xs: e.reciprocal(a, b))
            self.V(lambda e, a=cxc: e.tensor_mul(a, mm, xs))
            self.V(lambda e, a=sxc: e.tensor_mul(a, sn, xs))
            self.V(lambda e, a=cdc: e.tensor_mul(a, mm, rxs))
            self.V(lambda e, a=sdc: e.tensor_mul(a, sn, rxs))
            self.vmark()
            if tc == 0 and "tb0_sn" in self.probes and "tb0_sn" not in self.dbg_tensors:
                self.probe("tb0_sn", sn)
                self.probe("tb0_mm", mm)
                self.probe("tb0_xs", xs)
                self.probe("tb0_rxs", rxs)
        self.v_tables = sV.n
        if "tcx" in self.probes and "tcx" not in self.dbg_tensors:
            self.probe("tcx", self.tab_cx)
            self.probe("tsx", self.tab_sx)
            self.probe("tcdx", self.tab_cdx)
            self.probe("tsdx", self.tab_sdx)

    def banded_attention(self, cfg):
        sT, sV, sS = self.sT, self.sV, self.sS
        tiles = cfg['tiles']
        LB = cfg['lookback']
        inst = [(r, b) for r in tiles for b in range(2)]
        n = len(inst)
        t_sa = [0] * n
        t_tp = [0] * n
        t_pv = [0] * n
        v_nmx = [0] * n
        v_b2 = [0] * n
        v_pte = [0] * n
        v_ote = {}
        s_ln = [0] * n
        s_e2 = [0] * n
        NSA, NPT, NPOT = cfg['nsa'], cfg['npt'], cfg['npot']
        NPTS = cfg.get('npts', 2)
        self.wait('t', sV, sV.n)
        self.wait('t', sS, sS.n)

        def kwW(r):
            kw = max(0, 128 * r - LB)
            return kw, 128 * r + 128 - kw

        for w in range(n + 4):
            j = w - 1
            if 0 <= j < n:
                r, b = inst[j]
                kw, W = kwW(r)
                self.wait('v', sT, t_sa[j])
                psa = cfg['psa'](j % NSA)
                nmx = cfg['nmx'](j % 2)
                mtile = cfg['mask'](r)
                self.V(lambda e, o=psa[:, 0:W], m=mtile: e.tensor_tensor(o, o, m, ALU.add))
                self.V(lambda e, a=nmx, b=psa[:, 0:W]:
                       e.reduce_max(a, b, axis=mybir.AxisListType.X, negate=True))
                v_nmx[j] = self.vmark()
            j = w - 2
            if 0 <= j < n:
                self.wait('v', sS, s_ln[j])
                nmx, lnsg, b2 = cfg['nmx'](j % 2), cfg['lnsg'](j % 2), cfg['b2'](j % 2)
                self.V(lambda e, a=b2, b=lnsg, c=nmx: e.scalar_tensor_tensor(a, b, -1.0, c, ALU.mult, ALU.add))
                v_b2[j] = self.vmark()
            j = w - 3
            if 0 <= j < n:
                r, b = inst[j]
                kw, W = kwW(r)
                self.wait('v', sT, t_tp[j])
                if j >= NPTS:
                    self.wait('v', sT, t_pv[j - NPTS])
                ppt = cfg['ppt'](j % NPT)
                pts = cfg['pts'](j % NPTS)
                self.V(lambda e, a=pts[:, 0:W], b=ppt[:, 0:W]: e.tensor_copy(a, b))
                v_pte[j] = self.vmark()
            j = w - 4
            if 0 <= j < n:
                r, b = inst[j]
                if b == 1:
                    self.wait('v', sT, t_pv[j])
                    slot = (j // 2) % NPOT
                    cfg['otevac'](r, cfg['pot'](slot))
                    v_ote[r] = self.vmark()
            j = w - 1
            if 0 <= j < n:
                r, b = inst[j]
                kw, W = kwW(r)
                self.wait('s', sV, v_nmx[j])
                if j >= 2:
                    self.wait('s', sT, t_tp[j - 2])
                psa = cfg['psa'](j % NSA)
                nmx, sg, lnsg = cfg['nmx'](j % 2), cfg['sg'](j % 2), cfg['lnsg'](j % 2)
                self.S(lambda e, o=cfg['garb'](j)[:, 0:W], i=psa[:, 0:W], b=nmx, a=sg:
                       e.activation(o, i, ACT.Exp, bias=b, accum_out=a))
                self.ssync()
                self.S(lambda e, a=lnsg, b=sg: e.activation(a, b, ACT.Ln))
                s_ln[j] = self.smark()
            j = w - 2
            if 0 <= j < n:
                r, b = inst[j]
                kw, W = kwW(r)
                self.wait('s', sV, v_b2[j])
                if j >= 2:
                    self.wait('s', sT, t_tp[j - 2])
                psa = cfg['psa'](j % NSA)
                b2 = cfg['b2'](j % 2)
                pr = cfg['probs'](j % 2)
                self.S(lambda e, o=pr[:, 0:W], i=psa[:, 0:W], b=b2:
                       e.activation(o, i, ACT.Exp, bias=b).then_inc(self.sS.h, 1))
                s_e2[j] = sS.inc()
            if w < n:
                r, b = inst[w]
                kw, W = kwW(r)
                if w >= NSA:
                    self.wait('t', sS, s_e2[w - NSA])
                psa = cfg['psa'](w % NSA)
                q = cfg['qbuf'](b, r)
                k = cfg['kbuf'](b, kw, W)
                for m0 in range(0, W, 512):
                    m1 = min(W, m0 + 512)
                    self.T(lambda e, o=psa[:, m0:m1], q=q, kk=k[:, m0:m1]:
                           e.matmul(o, q, kk, start=True, stop=True))
                t_sa[w] = self.tmark()
            j = w - 2
            if 0 <= j < n:
                r, b = inst[j]
                kw, W = kwW(r)
                self.wait('t', sS, s_e2[j])
                if j >= NPT:
                    self.wait('t', sV, v_pte[j - NPT])
                ppt = cfg['ppt'](j % NPT)
                pr = cfg['probs'](j % 2)
                for kb in range(W // 128):
                    self.T(lambda e, o=ppt[:, kb * 128:kb * 128 + 128],
                           l=pr[:, kb * 128:kb * 128 + 128]:
                           e.matmul(o, l, self.ident[:], start=True, stop=True))
                t_tp[j] = self.tmark()
            j = w - 3
            if 0 <= j < n:
                r, b = inst[j]
                kw, W = kwW(r)
                self.wait('t', sV, v_pte[j])
                slot = (j // 2) % NPOT
                if b == 0 and j >= 2 * NPOT:
                    self.wait('t', sV, v_ote[inst[j - 2 * NPOT][0]])
                pot = cfg['pot'](slot)
                pts = cfg['pts'](j % NPTS)
                nkb = W // 128
                for kb in range(nkb):
                    tt = kw // 128 + kb
                    self.T(lambda e, o=pot[64 * b:64 * b + 64, 0:128],
                           l=cfg['vblk'](tt, b), rr=pts[:, kb * 128:kb * 128 + 128],
                           st=(kb == 0), sp=(kb == nkb - 1), b=b:
                           e.matmul(o, l, rr, start=st, stop=sp, tile_position=(0, 64 * b)))
                t_pv[j] = self.tmark()

    def ffn_layer(self, layer):
        kbff = int(os.environ.get('KBFF', '99'))
        sT, sV, sS, sW = self.sT, self.sV, self.sS, self.sW
        srcc = (lambda c, t0, nn, sh=True: self.xbc(c, t0, nn, sh)) if layer == 0 else \
               (lambda c, t0, nn, sh=True: self.xfbc(c, t0, nn, sh))
        qs_col = self.qsc[:, layer * 4 + 0:layer * 4 + 1]
        ks_col = self.qsc[:, layer * 4 + 1:layer * 4 + 2]
        qsp_col = self.qsc[:, layer * 4 + 2:layer * 4 + 3]
        ksp_col = self.qsc[:, layer * 4 + 3:layer * 4 + 4]

        self.wait('v', self.sG, self.g_r01)
        self.wait('v', self.sX, self.x_bias)
        self.V(lambda e: e.tensor_scalar(self.rotq[:], self.r01[:], qs_col, None, ALU.mult))
        self.V(lambda e: e.tensor_scalar(self.rotk[:], self.r01[:], ks_col, None, ALU.mult))
        v_rot = self.vmark()

        am0 = self.ab.mark()
        self.ffn_tables()
        qb = self.ab.buf(1024)
        kbuf = self.ab.buf(1536)
        vbuf = self.ab.buf(1536)
        wsl0 = self.ab.buf(3 * 1024)
        wsl = [wsl0, wsl0]
        oTf = self.ab.buf(4 * 1024)
        prb = [self.ab.buf(640) for _ in range(2)]
        ptsb = [self.ab.buf(640) for _ in range(2)]
        mxc = [self.svcol() for _ in range(2)]
        nmxc = [self.svcol() for _ in range(2)]
        sgc = [self.svcol() for _ in range(2)]
        lnsgc = [self.svcol() for _ in range(2)]
        b2c = [self.svcol() for _ in range(2)]

        w_slot_free = [0, 0]
        wq_ready = [0, 0, 0, 0]
        for half in range(2):
            ht0 = half * 1024
            kv0 = max(0, ht0 - 512)
            klen = ht0 + 1024 - kv0

            def issue_qkvw(pf):
                slot = 0
                self.wait('g', sT, w_slot_free[slot])
                for part in range(3):
                    dst = wsl[slot][:, part * 1024:part * 1024 + 1024].rearrange(
                        "p (k f) -> p k f", f=128)
                    src = self.d_fwqkv[layer].rearrange("(k p) f -> p k f", p=128)[
                        :, :, part * 512 + 128 * pf: part * 512 + 128 * pf + 128]
                    self.G(lambda e, d=dst, s=src: e.dma_start(out=d, in_=s).then_inc(self.sW.h, 16))
                    sW.inc(16)
                wq_ready[pf] = sW.n

            issue_qkvw(0)
            for pf in range(4):
                slot = 0
                self.wait('t', sW, wq_ready[pf])
                self.wait('t', sV, sV.n)
                self.wait('t', sS, sS.n)
                if layer == 1:
                    self.wait('t', sS, self.s_xfb_next)
                pq2 = [self.pD[:, 0:512], self.pD[:, 512:1024]]
                ev = []
                ci = 0
                for part, dstbuf, t0r in ((0, qb, ht0), (1, kbuf, kv0), (2, vbuf, kv0)):
                    wsrc = wsl[slot][:, part * 1024:part * 1024 + 1024]
                    if part < 2:
                        cl = [(a0, min(512, ht0 + 1024 - a0)) for a0 in range(t0r, ht0 + 1024, 512)]
                    else:
                        cl = [(tt * 128, 128) for tt in range(kv0 // 128, (ht0 + 1024) // 128)]
                    for (a0, nn) in cl:
                        ps = pq2[ci % 2]
                        if ci >= 2:
                            eng, mk = ev[ci - 2]
                            self.wait('t', eng, mk)
                        if part < 2:
                            for kc in range(CH):
                                self.T(lambda e, o=ps[:, 0:nn], l=wsrc[:, kc * 128:kc * 128 + 128],
                                       r=srcc(kc, a0, nn), st=(kc == 0), sp=(kc == CH - 1):
                                       e.matmul(o, l, r, start=st, stop=sp))
                        else:
                            for kc in range(CH):
                                self.T(lambda e, o=ps[:, 0:128], l=srcc(kc, a0, 128),
                                       r=wsrc[:, kc * 128:kc * 128 + 128], st=(kc == 0), sp=(kc == CH - 1):
                                       e.matmul(o, l, r, start=st, stop=sp))
                        t_q = self.tmark()
                        dst = dstbuf[:, a0 - t0r:a0 - t0r + nn]
                        if ci % 2 == 0:
                            self.wait('v', sT, t_q)
                            self.V(lambda e, a=dst, b=ps[:, 0:nn]: e.tensor_copy(a, b))
                            ev.append((sV, self.vmark()))
                        else:
                            self.wait('s', sT, t_q)
                            self.S(lambda e, a=dst, b=ps[:, 0:nn]: e.copy(a, b).then_inc(self.sS.h, 1))
                            ev.append((sS, sS.inc()))
                        ci += 1
                v_qkv = sV.n
                w_slot_free[slot] = sT.n
                if pf < 3:
                    issue_qkvw(pf + 1)

                if kbff < 1:
                    continue
                kbl2 = int(os.environ.get('KBL2', '99'))
                chunks = []
                for which, buf, t0r, rot, tcx, tsx, biasc in (
                        (0, qb, ht0, self.rotq, self.tab_cx, self.tab_sx, self.cln8),
                        (1, kbuf, kv0, self.rotk, self.tab_cdx, self.tab_sdx, None)):
                    nch = (ht0 + 1024 - t0r + 511) // 512
                    for i in range(nch):
                        chunks.append((which, buf, t0r, rot, tcx, tsx, biasc, i))
                nck = len(chunks)
                qsq2 = [ptsb[0][:, 0:512], ptsb[1][:, 0:512]]
                s_sq = [0] * nck
                t_st = [0] * nck
                v_cl = [0] * nck
                s_isd = [0] * nck

                def rrof(idx):
                    wh, _, _, _, _, _, _, i = chunks[idx]
                    c0 = (0 if wh == 0 else 1024) + i * 512
                    return self.arf[0:2, c0:c0 + 512]

                self.wait('s', sV, v_qkv)
                for idx, (which, buf, t0r, rot, tcx, tsx, biasc, i) in enumerate(chunks):
                    ch = buf[:, i * 512:i * 512 + 512]
                    pr = (self.pC[0:2, 0:512], self.pC[0:2, 512:1024])[idx % 2]
                    rr = rrof(idx)
                    if idx >= 2:
                        self.wait('s', sT, t_st[idx - 2])
                    self.S(lambda e, a=qsq2[idx % 2], b=ch: e.square(a, b).then_inc(self.sS.h, 1))
                    s_sq[idx] = sS.inc()
                    self.wait('t', sS, s_sq[idx])
                    if idx >= 2:
                        self.wait('t', sV, v_cl[idx - 2])
                    self.T(lambda e, o=pr, r=qsq2[idx % 2]: e.matmul(o, self.bd2[:], r, start=True, stop=True))
                    t_st[idx] = self.tmark()
                    self.wait('v', sT, t_st[idx])
                    self.V(lambda e, a=rr, b=pr: e.tensor_scalar(a, b, 1e-24, None, ALU.max))
                    v_cl[idx] = self.vmark()
                for idx in range(nck):
                    if kbl2 < 2:
                        break
                    biasc = chunks[idx][6]
                    rr = rrof(idx)
                    self.wait('s', sV, v_cl[idx])
                    self.S(lambda e, a=rr: e.activation(a, a, ACT.Ln))
                    self.ssync()
                    if biasc is not None:
                        self.S(lambda e, a=rr, b=biasc: e.activation(
                            a, a, ACT.Exp, scale=self.cm05[0:2, :], bias=b[0:2, :]).then_inc(self.sS.h, 1))
                    else:
                        self.S(lambda e, a=rr: e.activation(
                            a, a, ACT.Exp, scale=self.cm05[0:2, :]).then_inc(self.sS.h, 1))
                    s_isd[idx] = sS.inc()
                t_rot = [0] * nck
                v_ap = [0] * nck
                for idx in range(nck):
                    if kbl2 < 3:
                        break
                    (which, buf, t0r, rot, tcx, tsx, biasc, i) = chunks[idx]
                    a0 = t0r + i * 512
                    ch = buf[:, i * 512:i * 512 + 512]
                    rr = rrof(idx)
                    prot = (self.pA[:, 0:512], self.pA[:, 512:1024])[idx % 2]
                    prb_ = (self.pB[:, 0:512], self.pB[:, 512:1024])[idx % 2]
                    self.wait('t', sS, s_isd[idx])
                    if idx == 0:
                        self.wait('t', sV, v_rot)
                    if idx >= 2:
                        self.wait('t', sV, v_ap[idx - 2])
                    self.T(lambda e, o=prot, l=rot, r=ch: e.matmul(o, l[:], r, start=True, stop=True))
                    self.T(lambda e, o=prb_, r=rr: e.matmul(o, self.exp2m[:], r, start=True, stop=True))
                    t_rot[idx] = self.tmark()
                    t1f, t2f = ((self.arf[:, 4352:4864], self.arf[:, 4864:5376]),
                                (self.arf[:, 5376:5888], self.arf[:, 5888:6400]))[idx % 2]
                    scol = qs_col if which == 0 else ks_col
                    self.wait('v', sT, t_rot[idx])
                    self.V(lambda e, a=t1f, b=ch, s=scol, cc=tcx[:, a0:a0 + 512]:
                           e.scalar_tensor_tensor(a, b, s, cc, ALU.mult, ALU.mult))
                    self.V(lambda e, a=t2f, b=prot, cc=tsx[:, a0:a0 + 512]:
                           e.tensor_mul(a, b, cc))
                    self.V(lambda e, a=t1f, b=t2f: e.tensor_add(a, a, b))
                    self.V(lambda e, a=ch, b=t1f, cc=prb_: e.tensor_mul(a, b, cc))
                    v_ap[idx] = self.vmark()

                if half == 0 and pf == 0:
                    self.probe(f"qb{layer}", qb)
                    self.probe(f"kb{layer}", kbuf[:, 0:1024])
                    self.probe(f"vb{layer}", vbuf[:, 0:1024])
                def mk_ote(pf, half):
                    def f(r, pot):
                        dst = oTf[:, pf * 1024 + (r - half * 8) * 128: pf * 1024 + (r - half * 8) * 128 + 128]
                        self.V(lambda e, a=dst, b=pot[:, 0:128]: e.tensor_copy(a, b))
                    return f
                cfg = dict(
                    tiles=list(range(half * 8, half * 8 + 8)),
                    lookback=512,
                    qbuf=lambda b, r: qb[64 * b:64 * b + 64, (r * 128 - ht0):(r * 128 - ht0) + 128],
                    kbuf=lambda b, kw, W: kbuf[64 * b:64 * b + 64, kw - kv0:kw - kv0 + W],
                    vblk=lambda tt, b: vbuf[:, (tt * 128 - kv0) + 64 * b:(tt * 128 - kv0) + 64 * b + 64],
                    mask=lambda r: (self.fm_r[r] if r < 4 else self.fm_int),
                    psa=lambda s: (self.pA, self.pB)[s][:, 0:640],
                    nsa=2,
                    ppt=lambda s: self.pC[:, 0:640],
                    npt=1,
                    pot=lambda s: self.pD[:, 0:128],
                    npot=1,
                    probs=lambda s: prb[s], pts=lambda s: ptsb[s],
                    garb=lambda j: prb[j % 2],
                    mx=lambda s: mxc[s], nmx=lambda s: nmxc[s], sg=lambda s: sgc[s],
                    lnsg=lambda s: lnsgc[s], b2=lambda s: b2c[s],
                    otevac=mk_ote(pf, half),
                )
                if kbff >= 2:
                    self.banded_attention(cfg)
                if half == 0 and pf == 0:
                    self.wait('y', self.sV, self.sV.n)
                    self.probe(f"oTf{layer}", oTf[:, 0:1024])

            if half == 0:
                self.wait('y', self.sV, self.sV.n)
                self.probe(f"oTfall{layer}", oTf[:])
            if kbff < 3:
                self.ab.free(am0 if False else self.ab.mark())
                continue
            am1 = self.ab.mark()
            wo_sl = [wsl[0][:, 0:512], wsl[0][:, 512:1024]]
            wo_free = [max(w_slot_free), max(w_slot_free)]
            self.wait('t', sV, sV.n)
            self.wait('t', sS, sS.n)
            for fc in range(CH):
                ws = fc % 2
                self.wait('g', sT, wo_free[ws])
                dstw = wo_sl[ws].rearrange("p (k f) -> p k f", f=128)
                srcw = self.d_fwo[layer].rearrange("(k p) f -> p k f", p=128)[:, :, fc * 128:fc * 128 + 128]
                self.G(lambda e, d=dstw, s=srcw: e.dma_start(out=d, in_=s).then_inc(self.sW.h, 16))
                w_wo = sW.inc(16)
                self.wait('t', sW, w_wo)
                for tc in range(2):
                    po = (self.pA, self.pB)[tc % 2][:, 0:512]
                    for kc in range(4):
                        self.T(lambda e, o=po, l=wo_sl[ws][:, kc * 128:kc * 128 + 128],
                               r=oTf[:, kc * 1024 + tc * 512:kc * 1024 + tc * 512 + 512],
                               st=(kc == 0), sp=(kc == 3): e.matmul(o, l, r, start=st, stop=sp))
                    t_wo = self.tmark()
                    self.wait('v', sT, t_wo)
                    xfch = self.xfc(fc, ht0 + tc * 512, 512)
                    self.V(lambda e, a=xfch, b=po: e.tensor_add(a, a, b))
                    v_wo = self.vmark()
                    self.wait('t', sV, v_wo)
                wo_free[ws] = sT.n
            self.ab.free(am1)
            w_slot_free[0] = max(w_slot_free[0], sT.n)
        v_wo_all = sV.n
        self.ab.free(am0)

        self.probe(f"xfwo{layer}", self.xf[:])
        if kbff < 4:
            self.v_ffn = sV.n
            self.s_ffn = sS.n
            if layer == 0:
                self.s_xfb_next = 0
            return
        self.wait('s', sV, v_wo_all)
        for c in range(CH):
            self.S(lambda e, a=self.xfbc(c, 0, T), b=self.xfc(c, 0, T): e.copy(a, b))
        s_xfb = self.smark()

        am2 = self.ab.mark()
        sq = [self.ab.buf(T), self.ab.buf(T)]
        s_sq2 = [sS.n + c + 1 for c in range(CH)]
        t_mm2 = [sT.n + c + 1 for c in range(CH)]
        for c in range(CH):
            if c >= 2:
                self.wait('s', sT, t_mm2[c - 2])
            xfsh = self.xfc(c, 0, T, True)
            self.S(lambda e, a=sq[c % 2], b=xfsh: e.square(a, b).then_inc(self.sS.h, 1))
            sS.inc()
        assert sS.n == s_sq2[-1]
        psS = [self.pA[0:1, 0:512], self.pA[0:1, 512:1024],
               self.pB[0:1, 0:512], self.pB[0:1, 512:1024]]
        psQ = [self.pC[0:1, 0:512], self.pC[0:1, 512:1024],
               self.pD[0:1, 0:512], self.pD[0:1, 512:1024]]
        self.wait('t', sS, s_xfb)
        for c in range(CH):
            self.wait('t', sS, s_sq2[c])
            for tc in range(4):
                xbs = self.xfbc(c, tc * 512, 512, True)
                sqch = sq[c % 2][:, tc * 512:tc * 512 + 512]
                st, sp = (c == 0), (c == CH - 1)
                self.T(lambda e, o=psS[tc], r=xbs, st=st, sp=sp:
                       e.matmul(o, self.ones128b, r, start=st, stop=sp))
                self.T(lambda e, o=psQ[tc], r=sqch, st=st, sp=sp:
                       e.matmul(o, self.ones128b, r, start=st, stop=sp))
            self.tmark()
        assert sT.n == t_mm2[-1]
        self.ab.free(am2)
        v_rows = self.ln_rows(psS, psQ, sT.n)

        snap = self.svcol(12)
        self.wait('v', self.sV, sV.n)
        for c in range(4, CH):
            for k in range(1, 4):
                dst = snap[:, (c - 4) * 3 + (k - 1):(c - 4) * 3 + k]
                srcb = self.xfc(c, 512 * k - 1, 1)
                self.V(lambda e, a=dst, b=srcb: e.tensor_copy(a, b))
        v_snap = self.vmark()
        self.probe(f"snap{layer}", snap)

        am3 = self.ab.mark()
        region = self.ab.buf(8192)
        h2n = region[:, 0:4096]
        w1sl = [region[:, 4096:6144], region[:, 6144:8192]]
        w2sl = [region[:, 0:2816], region[:, 2816:5632]]
        actb = self.ab.buf(22 * 512)
        gelb = [self.ab.buf(512), self.ab.buf(512)]
        gc = self.gcol[:, layer * 8:layer * 8 + 8]
        bc = self.bcol[:, layer * 8:layer * 8 + 8]
        b2cc = self.b2col[:, layer * 8:layer * 8 + 8]
        w1_free = [0, 0]
        w2_free = [0, 0]
        v_act = {}
        t_u_last, t_w2_last = 0, 0
        for tc in range(4):
            rs = self.rs_(tc * 512, 512)
            c0 = self.c0_(tc * 512, 512)
            rb = self.pA[:, 0:512]
            cb = self.pA[:, 512:1024]
            self.wait('t', sV, v_rows if tc == 0 else sV.n)
            self.T(lambda e, o=rb, r=rs: e.matmul(o, self.onesrow[:], r, start=True, stop=True))
            self.T(lambda e, o=cb, r=c0: e.matmul(o, self.onesrow[:], r, start=True, stop=True))
            t_bb = self.tmark()
            self.wait('v', sT, t_bb)
            tmpf2 = [self.arf[:, 4352:4864], self.arf[:, 4864:5376]]
            if tc > 0:
                self.wait('s', sT, t_u_last)
                self.wait('v', sT, t_w2_last)
            s_h2 = 0
            s_h2s = [0] * CH
            for c in range(CH):
                tmpf = tmpf2[c % 2]
                xfsh = self.xfc(c, tc * 512, 512, True)
                if c >= 2:
                    self.wait('v', sS, s_h2s[c - 2])
                self.V(lambda e, a=tmpf, b=xfsh, r=rb: e.tensor_mul(a, b, r))
                self.V(lambda e, a=tmpf, r=cb: e.tensor_add(a, a, r))
                if tc >= 1 and c >= 4:
                    sc = snap[:, (c - 4) * 3 + (tc - 1):(c - 4) * 3 + tc]
                    self.vsync()
                    self.V(lambda e, a=tmpf[:, 0:1], b=sc, r=rb[:, 0:1]: e.tensor_mul(a, b, r))
                    self.vsync()
                    self.V(lambda e, a=tmpf[:, 0:1], r=cb[:, 0:1]: e.tensor_add(a, a, r))
                v_t = self.vmark()
                self.wait('s', sV, v_t)
                self.S(lambda e, a=h2n[:, c * 512:c * 512 + 512], b=tmpf,
                       g=gc[:, c:c + 1], bb=bc[:, c:c + 1]:
                       e.activation(a, b, ACT.Identity, bias=bb, scale=g).then_inc(self.sS.h, 1))
                s_h2 = sS.inc()
                s_h2s[c] = s_h2
            s_h2n = s_h2
            if tc == 0:
                self.probe(f"h2n{layer}", h2n)
            if tc == 1:
                self.probe(f"h2nb{layer}", h2n)
            for ci in range(22):
                wslot = ci % 2
                wdt = min(128, 2730 - 128 * ci)
                self.wait('g', sT, w1_free[wslot])
                srcall = self.d_fw1[layer].rearrange("(k p) f -> p k f", p=128)
                for ag in range(2):
                    dstw = w1sl[wslot][:, ag * 1024:ag * 1024 + 1024].rearrange(
                        "p (k f) -> p k f", f=128)[:, :, 0:wdt]
                    s_ = srcall[:, :, 256 * ci + ag * wdt:256 * ci + (ag + 1) * wdt]
                    self.G(lambda e, d=dstw, s=s_:
                           e.dma_start(out=d, in_=s).then_inc(self.sW.h, 16))
                    sW.inc(16)
                w1_ready = sW.n
                ppair = (self.pB, self.pC)[ci % 2]
                pa = ppair[0:wdt, 0:512]
                pg = ppair[0:wdt, 512:1024]
                self.wait('t', sW, w1_ready)
                if ci == 0:
                    self.wait('t', sS, s_h2n)
                if ci >= 2:
                    self.wait('t', sV, v_act[ci - 2])
                for kc in range(CH):
                    self.T(lambda e, o=pa, l=w1sl[wslot][:, kc * 128:kc * 128 + wdt],
                           r=h2n[:, kc * 512:kc * 512 + 512], st=(kc == 0), sp=(kc == CH - 1):
                           e.matmul(o, l, r, start=st, stop=sp))
                for kc in range(CH):
                    self.T(lambda e, o=pg, l=w1sl[wslot][:, 1024 + kc * 128:1024 + kc * 128 + wdt],
                           r=h2n[:, kc * 512:kc * 512 + 512], st=(kc == 0), sp=(kc == CH - 1):
                           e.matmul(o, l, r, start=st, stop=sp))
                t_u = self.tmark()
                w1_free[wslot] = t_u
                t_u_last = t_u
                b1a = self.b1c[:, layer * 44 + 2 * ci:layer * 44 + 2 * ci + 1]
                b1g = self.b1c[:, layer * 44 + 2 * ci + 1:layer * 44 + 2 * ci + 2]
                self.wait('s', sT, t_u)
                if ci >= 2:
                    self.wait('s', sV, v_act[ci - 2])
                gbuf = gelb[ci % 2]
                self.S(lambda e, o=gbuf[0:wdt, 0:512], i=pg, b=b1g[0:wdt, :]:
                       e.activation(o, i, ACT.Gelu, bias=b).then_inc(self.sS.h, 1))
                s_gel = sS.inc()
                self.wait('v', sS, s_gel)
                dst = actb[0:wdt, ci * 512:ci * 512 + 512]
                self.V(lambda e, a=dst, b=pa, s=b1a[0:wdt, :], g=gbuf[0:wdt, 0:512]:
                       e.scalar_tensor_tensor(a, b, s, g, ALU.add, ALU.mult))
                v_act[ci] = self.vmark()
            v_act_all = sV.n
            if tc == 0:
                self.probe(f"act{layer}", actb)
            if tc == 1:
                self.probe(f"actb{layer}", actb)
            for fc in range(CH):
                wslot = fc % 2
                self.wait('g', sT, w2_free[wslot])
                self.wait('g', sT, t_u_last)
                dstw = w2sl[wslot].rearrange("p (k f) -> p k f", f=128)
                src1 = self.d_fw2[layer][0:2688, :].rearrange("(k p) f -> p k f", p=128)[
                    :, :, fc * 128:fc * 128 + 128]
                self.G(lambda e, d=dstw[:, 0:21, :], s=src1:
                       e.dma_start(out=d, in_=s).then_inc(self.sW.h, 16))
                sW.inc(16)
                src2 = self.d_fw2[layer][2688:2730, fc * 128:fc * 128 + 128].rearrange(
                    "(o p) f -> p o f", o=1)
                self.G(lambda e, d=dstw[0:42, 21:22, :], s=src2:
                       e.dma_start(out=d, in_=s).then_inc(self.sW.h, 16))
                w2_ready = sW.inc(16)
                po = (self.pC, self.pD)[fc % 2][:, 0:512]
                if fc < 2:
                    self.wait('t', sV, v_act_all)
                self.wait('t', sW, w2_ready)
                for kc in range(22):
                    wdt = min(128, 2730 - 128 * kc)
                    self.T(lambda e, o=po, l=w2sl[wslot][0:wdt, kc * 128:kc * 128 + 128],
                           r=actb[0:wdt, kc * 512:kc * 512 + 512], st=(kc == 0), sp=(kc == 21):
                           e.matmul(o, l, r, start=st, stop=sp))
                t_w2 = self.tmark()
                w2_free[wslot] = t_w2
                t_w2_last = t_w2
                self.wait('v', sT, t_w2)
                xfch = self.xfc(fc, tc * 512, 512)
                self.V(lambda e, a=xfch, b=po, s=b2cc[:, fc:fc + 1]:
                       e.scalar_tensor_tensor(a, b, s, a, ALU.add, ALU.add))
                self.vmark()
            w1_free[0] = w1_free[1] = t_w2_last
        self.ab.free(am3)
        if layer == 0:
            self.wait('s', sV, sV.n)
            self.wait('s', sT, sT.n)
            for c in range(CH):
                self.S(lambda e, a=self.xfbc(c, 0, T), b=self.xfc(c, 0, T): e.copy(a, b))
            self.s_xfb_next = self.smark()
        self.v_ffn = sV.n
        self.s_ffn = sS.n

    def attn(self):
        sT, sV, sS, sW = self.sT, self.sV, self.sS, self.sW
        am0 = self.ab.mark()
        qb = self.ab.buf(1024)
        kbuf = self.ab.buf(1280)
        vbuf = self.ab.buf(1280)
        wsl = [self.ab.buf(3 * 1024) for _ in range(2)]
        oT = self.ab.buf(CH * 1024)
        wot_sl = [self.ab.buf(1024), self.ab.buf(1024)]
        prb = [self.ab.buf(384) for _ in range(2)]
        ptsb = [self.ab.buf(384) for _ in range(2)]
        garb = self.ab.buf(384)
        mxc = [self.svcol() for _ in range(2)]
        nmxc = [self.svcol() for _ in range(2)]
        sgc = [self.svcol() for _ in range(2)]
        lnsgc = [self.svcol() for _ in range(2)]
        b2c = [self.svcol() for _ in range(2)]

        self.wait('g', sT, sT.n)
        self.wait('g', sV, sV.n)
        self.wait('g', sS, sS.n)
        w_slot_free = [0, 0]
        wot_free = [0, 0]
        wq_ready = [0] * CH
        for half in range(2):
            ht0 = half * 1024
            kv0 = max(0, ht0 - 256)

            def issue_aw(p):
                slot = p % 2
                self.wait('g', sT, w_slot_free[slot])
                for part in range(3):
                    dst = wsl[slot][:, part * 1024:part * 1024 + 1024].rearrange(
                        "p (k f) -> p k f", f=128)
                    src = self.d_wqkvT.rearrange("(k p) f -> p k f", p=128)[
                        :, :, part * 1024 + 128 * p: part * 1024 + 128 * p + 128]
                    self.G(lambda e, d=dst, s=src: e.dma_start(out=d, in_=s).then_inc(self.sW.h, 16))
                    sW.inc(16)
                wq_ready[p] = sW.n

            issue_aw(0)
            for p in range(CH):
                slot = p % 2
                self.wait('t', sW, wq_ready[p])
                self.wait('t', sV, sV.n)
                self.wait('t', sS, sS.n)
                pq2 = [self.pD[:, 0:512], self.pD[:, 512:1024]]
                ev = []
                ci = 0
                for part, dstbuf, t0r in ((0, qb, ht0), (1, kbuf, kv0), (2, vbuf, kv0)):
                    wsrc = wsl[slot][:, part * 1024:part * 1024 + 1024]
                    if part < 2:
                        cl = [(a0, min(512, ht0 + 1024 - a0)) for a0 in range(t0r, ht0 + 1024, 512)]
                    else:
                        cl = [(tt * 128, 128) for tt in range(kv0 // 128, (ht0 + 1024) // 128)]
                    for (a0, nn) in cl:
                        ps = pq2[ci % 2]
                        if ci >= 2:
                            eng, mk = ev[ci - 2]
                            self.wait('t', eng, mk)
                        if part < 2:
                            for kc in range(CH):
                                self.T(lambda e, o=ps[:, 0:nn], l=wsrc[:, kc * 128:kc * 128 + 128],
                                       r=self.xbc(kc, a0, nn), st=(kc == 0), sp=(kc == CH - 1):
                                       e.matmul(o, l, r, start=st, stop=sp))
                        else:
                            for kc in range(CH):
                                self.T(lambda e, o=ps[:, 0:128], l=self.xbc(kc, a0, 128),
                                       r=wsrc[:, kc * 128:kc * 128 + 128], st=(kc == 0), sp=(kc == CH - 1):
                                       e.matmul(o, l, r, start=st, stop=sp))
                        t_q = self.tmark()
                        dst = dstbuf[:, a0 - t0r:a0 - t0r + nn]
                        if ci % 2 == 0:
                            self.wait('v', sT, t_q)
                            if part == 0:
                                bcl = self.bqc[:, p:p + 1]
                                self.V(lambda e, a=dst, b=ps[:, 0:nn], s=bcl:
                                       e.tensor_scalar(a, b, s, 0.125, ALU.add, ALU.mult))
                            elif part == 1:
                                bcl = self.bqc[:, 8 + p:8 + p + 1]
                                self.V(lambda e, a=dst, b=ps[:, 0:nn], s=bcl:
                                       e.tensor_scalar(a, b, s, None, ALU.add))
                            else:
                                self.V(lambda e, a=dst, b=ps[:, 0:128]: e.tensor_copy(a, b))
                            ev.append((sV, self.vmark()))
                        else:
                            self.wait('s', sT, t_q)
                            if part == 0:
                                bcl = self.bqsc[:, p:p + 1]
                                self.S(lambda e, a=dst, b=ps[:, 0:nn], s=bcl:
                                       e.activation(a, b, ACT.Identity, bias=s,
                                                    scale=self.c0125).then_inc(self.sS.h, 1))
                            elif part == 1:
                                bcl = self.bqc[:, 8 + p:8 + p + 1]
                                self.S(lambda e, a=dst, b=ps[:, 0:nn], s=bcl:
                                       e.activation(a, b, ACT.Identity, bias=s).then_inc(self.sS.h, 1))
                            else:
                                self.S(lambda e, a=dst, b=ps[:, 0:128]: e.copy(a, b).then_inc(self.sS.h, 1))
                            ev.append((sS, sS.inc()))
                        ci += 1
                w_slot_free[slot] = sT.n
                if p < CH - 1:
                    issue_aw(p + 1)
                else:
                    self.wait('t', sV, sV.n)
                    self.wait('t', sS, sS.n)

                if half == 0 and p == 0:
                    self.probe("aqb", qb)
                    self.probe("akb", kbuf[:, 0:1024])
                    self.probe("avb", vbuf[:, 0:1024])
                def mk_ote(p, half):
                    def f(r, pot):
                        dst = oT[:, p * 1024 + (r - half * 8) * 128: p * 1024 + (r - half * 8) * 128 + 128]
                        bv = self.bqc[:, 16 + p:16 + p + 1]
                        self.V(lambda e, a=dst, b=pot[:, 0:128], s=bv:
                               e.tensor_scalar(a, b, s, None, ALU.add))
                    return f
                cfg = dict(
                    tiles=list(range(half * 8, half * 8 + 8)),
                    lookback=256,
                    qbuf=lambda b, r: qb[64 * b:64 * b + 64, (r * 128 - ht0):(r * 128 - ht0) + 128],
                    kbuf=lambda b, kw, W: kbuf[64 * b:64 * b + 64, kw - kv0:kw - kv0 + W],
                    vblk=lambda tt, b: vbuf[:, (tt * 128 - kv0) + 64 * b:(tt * 128 - kv0) + 64 * b + 64],
                    mask=lambda r: (self.am_r0 if r == 0 else (self.am_r1 if r == 1 else self.am_int)),
                    psa=lambda s: (self.pA[:, 0:384], self.pA[:, 512:896], self.pB[:, 0:384])[s],
                    nsa=3,
                    ppt=lambda s: (self.pB[:, 512:896], self.pC[:, 0:384])[s],
                    npt=2,
                    pot=lambda s: (self.pC[:, 512:640], self.pD[:, 0:128])[s],
                    npot=2,
                    probs=lambda s: prb[s], pts=lambda s: ptsb[s],
                    garb=lambda j, g=garb: g,
                    mx=lambda s: mxc[s], nmx=lambda s: nmxc[s], sg=lambda s: sgc[s],
                    lnsg=lambda s: lnsgc[s], b2=lambda s: b2c[s],
                    otevac=mk_ote(p, half),
                )
                self.banded_attention(cfg)
                if half == 0 and p == 0:
                    self.wait('y', sV, sV.n)
                    self.probe("aoT", oT[:, 0:1024])

            self.wait('t', sV, sV.n)
            self.wait('t', sS, sS.n)
            if half == 1:
                self.probe("aoTall", oT[:])
            for fc in range(CH):
                ws = fc % 2
                self.wait('g', sT, wot_free[ws])
                dstw = wot_sl[ws].rearrange("p (k f) -> p k f", f=128)
                srcw = self.d_woT.rearrange("(k p) f -> p k f", p=128)[:, :, fc * 128:fc * 128 + 128]
                self.G(lambda e, d=dstw, s=srcw: e.dma_start(out=d, in_=s).then_inc(self.sW.h, 16))
                w_wo = sW.inc(16)
                self.wait('t', sW, w_wo)
                if half == 1 and fc == 7:
                    self.wait('y', sW, w_wo)
                    self.probe("wotfc7", wot_sl[ws])
                for tc in range(2):
                    po = (self.pA, self.pB)[tc % 2][:, 0:512]
                    for kc in range(CH):
                        self.T(lambda e, o=po, l=wot_sl[ws][:, kc * 128:kc * 128 + 128],
                               r=oT[:, kc * 1024 + tc * 512:kc * 1024 + tc * 512 + 512],
                               st=(kc == 0), sp=(kc == CH - 1): e.matmul(o, l, r, start=st, stop=sp))
                    t_wo = self.tmark()
                    self.wait('v', sT, t_wo)
                    xfch = self.xfc(fc, ht0 + tc * 512, 512)
                    self.V(lambda e, a=xfch, b=po, s=self.boc[:, fc:fc + 1]:
                           e.scalar_tensor_tensor(a, b, s, a, ALU.add, ALU.add))
                    v_wo = self.vmark()
                    self.wait('t', sV, v_wo)
                wot_free[ws] = sT.n
            if half == 1:
                self.wait('y', sV, sV.n)
                self.probe("aoT1", oT[:, 7 * 1024:8 * 1024])
                self.probe("awo", self.xf[:])
        self.ab.free(am0)

    def finish(self):
        sV, sX, sW = self.sV, self.sX, self.sW
        self.probe("xff", self.xf[:])
        self.wait('g', sV, sV.n)
        self.wait('g', self.sS, self.sS.n)
        self.wait('g', self.sT, self.sT.n)
        for c in range(CH):
            dst = self.xfc(c, 0, T)
            src = self.d_x[c * 128:(c + 1) * 128, :]
            self.G(lambda e, d=dst, s=src: e.dma_start(out=d, in_=s, accum_op=ALU.add).then_inc(self.sW.h, 16))
            sW.inc(16)
        g_acc = sW.n
        self.wait('y', sW, g_acc)
        for c in range(CH):
            dst = self.d_out[c * 128:(c + 1) * 128, :]
            src = self.xfc(c, 0, T)
            self.Y(lambda e, d=dst, s=src: e.dma_start(out=d, in_=s).then_inc(self.sX.h, 16))
            sX.inc(16)
        self.wait('y', sX, sX.n)


def build(stage=99, probes=()):
    return Builder(stage, probes).nc


def _reorder_w1(w1, b1):
    w1o = np.empty_like(w1)
    b1o = np.empty_like(b1)
    a_w, g_w = w1[:, :, :2730], w1[:, :, 2730:]
    a_b, g_b = b1[:, :2730], b1[:, 2730:]
    o = 0
    for c in range(22):
        wdt = min(128, 2730 - 128 * c)
        w1o[:, :, o:o + wdt] = a_w[:, :, 128 * c:128 * c + wdt]
        b1o[:, o:o + wdt] = a_b[:, 128 * c:128 * c + wdt]
        o += wdt
        w1o[:, :, o:o + wdt] = g_w[:, :, 128 * c:128 * c + wdt]
        b1o[:, o:o + wdt] = g_b[:, 128 * c:128 * c + wdt]
        o += wdt
    assert o == 5460
    return w1o, b1o


def kernel(L0_attn_x, L0_attn_Wqkv, L0_attn_bqkv, L0_attn_Wo, L0_attn_bo,
           L0_ffn_wqkv, L0_ffn_q_scale, L0_ffn_k_scale, L0_ffn_wo,
           L0_ffn_ln_g, L0_ffn_ln_b, L0_ffn_w1, L0_ffn_b1, L0_ffn_w2,
           L0_ffn_b2):
    from concourse.bass_utils import run_bass_kernel_spmd
    f32 = lambda a: np.ascontiguousarray(np.asarray(a, dtype=np.float32))
    x = f32(L0_attn_x)
    w1r, b1r = _reorder_w1(f32(L0_ffn_w1), f32(L0_ffn_b1))
    shared = {
        "wqkvT": np.ascontiguousarray(f32(L0_attn_Wqkv).T),
        "bqkv": f32(L0_attn_bqkv),
        "woT": np.ascontiguousarray(f32(L0_attn_Wo).T),
        "bo": f32(L0_attn_bo),
        "fwqkv": f32(L0_ffn_wqkv),
        "fqs": f32(L0_ffn_q_scale),
        "fks": f32(L0_ffn_k_scale),
        "fwo": f32(L0_ffn_wo),
        "flng": f32(L0_ffn_ln_g),
        "flnb": f32(L0_ffn_ln_b),
        "fw1r": np.ascontiguousarray(w1r),
        "fb1r": np.ascontiguousarray(b1r),
        "fw2": f32(L0_ffn_w2),
        "fb2": f32(L0_ffn_b2),
    }
    in_maps = [dict(shared, xT=np.ascontiguousarray(x[i].T)) for i in range(8)]
    nc = Builder(99).nc
    res = run_bass_kernel_spmd(nc, in_maps, core_ids=list(range(8)))
    out = np.stack([np.asarray(res.results[i]["outT"], dtype=np.float32).T
                    for i in range(8)])
    return out


# version 12 (final)
# speedup vs baseline: 1.2358x; 1.0502x over previous
import sys, os
if '/opt/trn_rl_repo' not in sys.path:
    sys.path.insert(0, '/opt/trn_rl_repo')
import numpy as np
from contextlib import ExitStack
import concourse.bass as bass
import concourse.mybir as mybir

F32 = mybir.dt.float32
BF16 = mybir.dt.bfloat16
ALU = mybir.AluOpType
ACT = mybir.ActivationFunctionType

D, T = 1024, 2048
TG = T + 1
CH, NT = 8, 16
NEG = -3.0e38
PI = float(np.pi)


class Sem:
    def __init__(self, es, nc, name):
        self.h = es.enter_context(nc.semaphore(name))
        self.n = 0

    def inc(self, amt=1):
        self.n += amt
        return self.n


class Arena:
    def __init__(self, t, n):
        self.t, self.n, self.top = t, n, 0

    def buf(self, w):
        c = self.top
        self.top += w
        assert self.top <= self.n, f"arena overflow {self.top} > {self.n}"
        return self.t[:, c:c + w]

    def mark(self):
        return self.top

    def free(self, m):
        self.top = m


class Builder:
    def __init__(self, stage=99, probes=()):
        self.stage = stage
        self.probes = set(probes)
        self.dbg_tensors = {}
        nc = bass.Bass('TRN2', target_bir_lowering=False, detect_race_conditions=False)
        self.nc = nc
        self.es = ExitStack()
        es = self.es
        dt = lambda n, sh, dty=F32, kind="ExternalInput": nc.dram_tensor(n, sh, dty, kind=kind).ap()
        self.d_x = dt("xT", [D, T])
        self.d_wqkv2 = dt("wqkv2", [8, 3, 128, 1024])
        self.d_bqkv = dt("bqkv", [3 * D])
        self.d_wo2 = dt("wo2", [8, 128, 1024])
        self.d_bo = dt("bo", [D])
        self.d_fwqkv2 = dt("fwqkv2", [2, 4, 3, 128, 1024])
        self.d_fqs = dt("fqs", [2, 64])
        self.d_fks = dt("fks", [2, 64])
        self.d_fwo2 = dt("fwo2", [2, 8, 128, 512])
        self.d_flng = dt("flng", [2, D])
        self.d_flnb = dt("flnb", [2, D])
        self.d_fw1s = dt("fw1s", [2, 5591040])
        self.d_fb1 = dt("fb1r", [2, 5460])
        self.d_fw2s = dt("fw2s", [2, 2795520])
        self.d_fb2 = dt("fb2", [2, D])
        self.d_out = dt("outT", [D, T], F32, "ExternalOutput")

        sb = lambda n, sh, dty: es.enter_context(nc.sbuf_tensor(n, sh, dty))
        self.xf = sb("xf", [128, CH * TG], F32)
        self.xb = sb("xb", [128, CH * TG], BF16)
        self.xfb = sb("xfb", [128, CH * TG], BF16)
        self.sv = sb("sv", [128, 80], F32)
        self.svb = sb("svb", [128, 16], BF16)
        self.ident = sb("ident", [128, 128], BF16)
        self.r01 = sb("r01", [128, 128], BF16)
        self.rotq = sb("rotq", [128, 128], BF16)
        self.rotk = sb("rotk", [128, 128], BF16)
        self.onesrow = sb("onesrow", [1, 128], F32)
        self.bd2 = sb("bd2", [128, 2], BF16)
        self.exp2m = sb("exp2m", [2, 128], F32)
        self.bqc = sb("bqc", [128, 24], F32)
        self.boc = sb("boc", [128, 8], F32)
        self.gcol = sb("gcol", [128, 16], F32)
        self.bcol = sb("bcol", [128, 16], F32)
        self.b2col = sb("b2col", [128, 16], F32)
        self.b1c = sb("b1c", [128, 88], F32)
        self.qsc = sb("qsc", [128, 8], F32)
        self.AF = 6400
        self.arf = sb("arf", [128, self.AF], F32)
        self.si32 = sb("si32", [128, 512], mybir.dt.int32)
        rem0 = nc.sbuf_bytes_remaining
        ABB = (rem0 - 256) // 2 // 128 * 128
        self.AB = ABB
        self.arb = sb("arb", [128, self.AB], BF16)
        ps = lambda n: es.enter_context(nc.psum_tensor(n, [128, 1024], F32))
        self.pA, self.pB, self.pC, self.pD = ps("pA"), ps("pB"), ps("pC"), ps("pD")

        self.sW = Sem(es, nc, "sW")
        self.sX = Sem(es, nc, "sX")
        self.sT = Sem(es, nc, "sT")
        self.sV = Sem(es, nc, "sV")
        self.sS = Sem(es, nc, "sS")
        self.sG = Sem(es, nc, "sG")

        self.prog = {k: [] for k in "tvsgy"}
        self._svcol = 0
        self.ab = Arena(self.arb, self.AB)
        self.build()

    def T(self, f): self.prog['t'].append(f)
    def V(self, f): self.prog['v'].append(f)
    def S(self, f): self.prog['s'].append(f)
    def G(self, f): self.prog['g'].append(f)
    def Y(self, f): self.prog['y'].append(f)

    def wait(self, eng, sem, n):
        if n > 0:
            self.prog[eng].append(lambda e: e.wait_ge(sem.h, n))

    def probe(self, name, ap):
        if name not in self.probes:
            return
        sh = list(ap.shape)
        d = self.nc.dram_tensor("dbg_" + name, sh, ap.dtype, kind="ExternalOutput").ap()
        self.dbg_tensors[name] = sh
        self.wait('y', self.sT, self.sT.n)
        self.wait('y', self.sV, self.sV.n)
        self.wait('y', self.sS, self.sS.n)
        self.Y(lambda e, dd=d, s=ap: e.dma_start(out=dd, in_=s).then_inc(self.sX.h, 16))
        pd = self.sX.inc(16)
        for eng in 'tvsg':
            self.wait(eng, self.sX, pd)

    def _mark(self, key, sem):
        lst = self.prog[key]
        f = lst[-1]
        lst[-1] = (lambda e, f=f, s=sem.h: f(e).then_inc(s, 1))
        return sem.inc()

    def tmark(self):
        return self._mark('t', self.sT)

    def vsync(self):
        self.wait('v', self.sV, self.vmark())

    def ssync(self):
        self.wait('s', self.sS, self.smark())

    def vmark(self):
        return self._mark('v', self.sV)

    def smark(self):
        return self._mark('s', self.sS)

    def svcol(self, w=1):
        c = self._svcol
        self._svcol += w
        assert self._svcol <= 80
        return self.sv[:, c:c + w]

    def xfc(self, c, t0, n, shift=False):
        o = c * TG + 1 + t0 - (1 if (shift and c >= 4) else 0)
        return self.xf[:, o:o + n]

    def xbc(self, c, t0, n, shift=False):
        o = c * TG + 1 + t0 - (1 if (shift and c >= 4) else 0)
        return self.xb[:, o:o + n]

    def xfbc(self, c, t0, n, shift=False):
        o = c * TG + 1 + t0 - (1 if (shift and c >= 4) else 0)
        return self.xfb[:, o:o + n]

    def rs_(self, c0, n): return self.arf[0:1, c0:c0 + n]
    def c0_(self, c0, n): return self.arf[0:1, 2048 + c0:2048 + c0 + n]
    def mu_(self, c0, n): return self.arf[0:1, 4096 + c0:4096 + c0 + n]

    def build(self):
        self.consts()
        self.ln1()
        if self.stage >= 1:
            for layer in range(min(2, self.stage)):
                self.ffn_layer(layer)
        if self.stage >= 3:
            self.attn()
        self.finish()
        nc = self.nc
        with nc.Block() as blk:
            @blk.tensor
            def _(e):
                for f in self.prog['t']: f(e)

            @blk.vector
            def _(e):
                for f in self.prog['v']: f(e)

            @blk.scalar
            def _(e):
                for f in self.prog['s']: f(e)

            @blk.gpsimd
            def _(e):
                for f in self.prog['g']: f(e)

            @blk.sync
            def _(e):
                for f in self.prog['y']: f(e)
        self.es.close()

    def consts(self):
        sX, sG = self.sX, self.sG
        for c in range(CH):
            dst, src = self.xfc(c, 0, T), self.d_x[c * 128:(c + 1) * 128, :]
            self.Y(lambda e, d=dst, s=src: e.dma_start(out=d, in_=s).then_inc(self.sX.h, 16))
            sX.inc(16)
        self.x_loaded = sX.n

        def colload(dst, src):
            self.Y(lambda e, d=dst, s=src: e.dma_start(
                out=d, in_=s, allow_slow_non_contiguous=True).then_inc(self.sX.h, 16))
            return sX.inc(16)

        colload(self.bqc[:], self.d_bqkv.rearrange("(j p) -> p j", p=128))
        colload(self.boc[:], self.d_bo.rearrange("(j p) -> p j", p=128))
        for l in range(2):
            colload(self.gcol[:, l * 8:l * 8 + 8], self.d_flng[l].rearrange("(j p) -> p j", p=128))
            colload(self.bcol[:, l * 8:l * 8 + 8], self.d_flnb[l].rearrange("(j p) -> p j", p=128))
            colload(self.b2col[:, l * 8:l * 8 + 8], self.d_fb2[l].rearrange("(j p) -> p j", p=128))
            colload(self.b1c[:, l * 44:l * 44 + 42], self.d_fb1[l, 0:5376].rearrange("(j p) -> p j", p=128))
            colload(self.b1c[0:42, l * 44 + 42:l * 44 + 43], self.d_fb1[l, 5376:5418].rearrange("(p o) -> p o", o=1))
            colload(self.b1c[0:42, l * 44 + 43:l * 44 + 44], self.d_fb1[l, 5418:5460].rearrange("(p o) -> p o", o=1))
            for j, dten in ((0, self.d_fqs), (1, self.d_fks)):
                colload(self.qsc[0:64, l * 4 + j:l * 4 + j + 1], dten[l].rearrange("(p o) -> p o", o=1))
                colload(self.qsc[64:128, l * 4 + j:l * 4 + j + 1], dten[l].rearrange("(p o) -> p o", o=1))
                for hp in range(4):
                    p0 = hp * 32
                    s0 = (p0 + 32) % 64
                    src32 = dten[l, s0:s0 + 32].rearrange("(p o) -> p o", o=1)
                    colload(self.qsc[p0:p0 + 32, l * 4 + 2 + j:l * 4 + 3 + j], src32)
        self.x_bias = sX.n

        iot = self.svcol()
        self.iot = iot
        self.G(lambda e: e.iota(iot, [[0, 1]], base=0, channel_multiplier=1,
                                allow_small_or_imprecise_dtypes=True).then_inc(self.sG.h, 1))
        self.g_iota = sG.inc()
        self.G(lambda e: e.memset(self.ident[:], 1.0).then_inc(self.sG.h, 1))
        m1 = sG.inc()
        self.wait('g', sG, m1)
        self.G(lambda e: e.affine_select(self.ident[:], self.ident[:], [[-1, 128]],
                                         ALU.is_equal, 0.0, base=0,
                                         channel_multiplier=1).then_inc(self.sG.h, 1))
        self.g_ident = sG.inc()
        for b in range(2):
            p0 = b * 64
            hi = self.r01[p0:p0 + 32, :]
            lo = self.r01[p0 + 32:p0 + 64, :]
            self.G(lambda e, a=hi: e.memset(a, 1.0).then_inc(self.sG.h, 1))
            self.G(lambda e, a=lo: e.memset(a, -1.0).then_inc(self.sG.h, 1))
        m2 = sG.inc(4)
        self.wait('g', sG, m2)
        for b in range(2):
            p0 = b * 64
            hi = self.r01[p0:p0 + 32, :]
            lo = self.r01[p0 + 32:p0 + 64, :]
            self.G(lambda e, a=hi, base=p0 + 32: e.affine_select(
                a, a, [[-1, 128]], ALU.is_equal, 0.0, base=base,
                channel_multiplier=1).then_inc(self.sG.h, 1))
            self.G(lambda e, a=lo, base=p0: e.affine_select(
                a, a, [[-1, 128]], ALU.is_equal, 0.0, base=base,
                channel_multiplier=1).then_inc(self.sG.h, 1))
        self.g_r01 = sG.inc(4)
        ob = self.svb[:, 0:1]
        self.ones128b = ob
        self.G(lambda e: e.memset(ob, 1.0).then_inc(self.sG.h, 1))
        self.G(lambda e: e.memset(self.onesrow[:], 1.0).then_inc(self.sG.h, 1))
        self.G(lambda e: e.memset(self.bd2[:], 0.0).then_inc(self.sG.h, 1))
        self.G(lambda e: e.memset(self.exp2m[:], 0.0).then_inc(self.sG.h, 1))
        m3 = sG.inc(4)
        self.wait('g', sG, m3)
        self.G(lambda e: e.memset(self.bd2[0:64, 0:1], 1.0).then_inc(self.sG.h, 1))
        self.G(lambda e: e.memset(self.bd2[64:128, 1:2], 1.0).then_inc(self.sG.h, 1))
        self.G(lambda e: e.memset(self.exp2m[:], 1.0).then_inc(self.sG.h, 1))
        mepre = sG.inc(3)
        self.wait('g', sG, mepre)
        self.G(lambda e: e.affine_select(self.exp2m[:], self.exp2m[:], [[1, 128]],
                                         ALU.is_ge, 0.0, base=0,
                                         channel_multiplier=-64).then_inc(self.sG.h, 1))
        me2 = sG.inc()
        self.wait('g', sG, me2)
        self.G(lambda e: e.affine_select(self.exp2m[:], self.exp2m[:], [[-1, 128]],
                                         ALU.is_ge, 0.0, base=63,
                                         channel_multiplier=64).then_inc(self.sG.h, 1))
        self.g_const = sG.inc()

        def mkmask(W, lo, hi):
            t = self.ab.buf(W)
            self.G(lambda e, a=t: e.memset(a, 0.0).then_inc(self.sG.h, 1))
            mset = sG.inc()
            self.wait('g', sG, mset)
            if lo is not None:
                self.G(lambda e, a=t, b=-lo: e.affine_select(
                    a, a, [[1, W]], ALU.is_ge, -1.0e9, base=b,
                    channel_multiplier=-1).then_inc(self.sG.h, 1))
                self.wait('g', sG, sG.inc())
            if hi is not None:
                self.G(lambda e, a=t, b=hi: e.affine_select(
                    a, a, [[-1, W]], ALU.is_ge, -1.0e9, base=b,
                    channel_multiplier=1).then_inc(self.sG.h, 1))
                self.wait('g', sG, sG.inc())
            return t
        self.am_int = mkmask(384, 1, 256)
        self.am_r0 = mkmask(128, None, 0)
        self.am_r1 = mkmask(256, None, 128)
        self.fm_int = mkmask(640, 0, 512)
        self.fm_r = [mkmask(128 * (r + 1), None, 128 * r) for r in range(4)]
        self.g_masks = sG.n

        self.wait('v', sG, self.g_iota)
        self.npib = self.svcol()
        self.V(lambda e, a=self.npib: e.memset(a, -PI))
        self.c64 = self.svcol()
        self.V(lambda e, a=self.c64: e.memset(a, 64.0))
        self.c0125 = self.svcol()
        self.V(lambda e, a=self.c0125: e.memset(a, 0.125))
        self.cln8 = self.svcol()
        self.V(lambda e, a=self.cln8: e.memset(a, float(np.log(8.0))))
        self.cm05 = self.svcol()
        self.V(lambda e, a=self.cm05: e.memset(a, -0.5))
        self.bqsc = self.svcol(8)
        self.wait('v', self.sX, self.x_bias)
        self.V(lambda e, a=self.bqsc: e.tensor_scalar(a, self.bqc[:, 0:8], 0.125, None, ALU.mult))
        self.v_vecs = self.vmark()

    def ln_rows(self, psS, psQ, t_stats):
        krow = int(os.environ.get('KBROW', '99'))
        sV, sS = self.sV, self.sS
        self.wait('v', self.sT, t_stats)
        for tc in range(4):
            mu = self.mu_(tc * 512, 512)
            rs = self.rs_(tc * 512, 512)
            c0 = self.c0_(tc * 512, 512)
            self.V(lambda e, a=mu, b=psS[tc]: e.tensor_scalar(a, b, 1.0 / D, None, ALU.mult))
            self.V(lambda e, a=rs, b=psQ[tc]: e.tensor_scalar(a, b, 1.0 / D, 1e-5, ALU.mult, ALU.add))
            self.vsync()
            self.V(lambda e, a=c0, b=mu: e.tensor_mul(a, b, b))
            self.vsync()
            self.V(lambda e, a=rs, b=c0: e.tensor_sub(a, a, b))
            self.vsync()
            self.V(lambda e, a=rs: e.reciprocal(a, a))
        v1 = self.vmark()
        if krow < 2:
            return v1
        self.wait('s', sV, v1)
        for tc in range(4):
            rs = self.rs_(tc * 512, 512)
            self.S(lambda e, a=rs: e.sqrt(a, a))
        s1 = self.smark()
        if krow < 3:
            return v1
        self.wait('v', sS, s1)
        for tc in range(4):
            mu = self.mu_(tc * 512, 512)
            rs = self.rs_(tc * 512, 512)
            c0 = self.c0_(tc * 512, 512)
            self.V(lambda e, a=c0, b=mu, r=rs: e.scalar_tensor_tensor(a, b, -1.0, r, ALU.mult, ALU.mult))
        return self.vmark()

    def ln1(self):
        sub = int(os.environ.get('KBSUB', '99'))
        sT, sV, sS, sX = self.sT, self.sV, self.sS, self.sX
        for c in range(CH):
            for buf in (self.xf, self.xb, self.xfb):
                g = buf[:, c * TG:c * TG + 1]
                self.V(lambda e, a=g: e.memset(a, 0.0))
        self.v_guard = self.vmark()

        if sub < 1:
            return
        am = self.ab.mark()
        sq = [self.ab.buf(T), self.ab.buf(T)]
        s_cast = [self.sS.n + 2 * c + 2 for c in range(CH)]
        t_sqmm = [self.sT.n + c + 1 for c in range(CH)]
        self.wait('s', sX, self.x_loaded)
        for c in range(CH):
            xbch, xfch = self.xbc(c, 0, T), self.xfc(c, 0, T)
            if c >= 2:
                self.wait('s', sT, t_sqmm[c - 2])
            self.S(lambda e, a=xbch, b=xfch: e.copy(a, b).then_inc(self.sS.h, 1))
            sS.inc()
            self.S(lambda e, a=sq[c % 2], b=xbch: e.square(a, b).then_inc(self.sS.h, 1))
            sS.inc()
        assert sS.n == s_cast[-1]

        if sub < 2:
            return
        psS = [self.pA[0:1, 0:512], self.pA[0:1, 512:1024],
               self.pB[0:1, 0:512], self.pB[0:1, 512:1024]]
        psQ = [self.pC[0:1, 0:512], self.pC[0:1, 512:1024],
               self.pD[0:1, 0:512], self.pD[0:1, 512:1024]]
        ob = self.ones128b
        self.wait('t', self.sG, self.g_const)
        for c in range(CH):
            self.wait('t', sS, s_cast[c])
            for tc in range(4):
                xbch = self.xbc(c, tc * 512, 512)
                sqch = sq[c % 2][:, tc * 512:tc * 512 + 512]
                st, sp = (c == 0), (c == CH - 1)
                self.T(lambda e, o=psS[tc], r=xbch, st=st, sp=sp:
                       e.matmul(o, ob, r, start=st, stop=sp))
                self.T(lambda e, o=psQ[tc], r=sqch, st=st, sp=sp:
                       e.matmul(o, ob, r, start=st, stop=sp))
            self.tmark()
        assert sT.n == t_sqmm[-1]
        self.ab.free(am)

        if sub < 3:
            return
        v_rows = self.ln_rows(psS, psQ, sT.n)

        if sub < 4:
            return
        orow = self.onesrow[:]
        for tc in range(4):
            rs = self.rs_(tc * 512, 512)
            c0 = self.c0_(tc * 512, 512)
            rb = self.pA[:, (tc % 2) * 512:(tc % 2) * 512 + 512]
            cb = self.pB[:, (tc % 2) * 512:(tc % 2) * 512 + 512]
            self.wait('t', sV, v_rows if tc < 2 else sV.n)
            self.T(lambda e, o=rb, r=rs: e.matmul(o, orow, r, start=True, stop=True))
            self.T(lambda e, o=cb, r=c0: e.matmul(o, orow, r, start=True, stop=True))
            t_b = self.tmark()
            self.wait('v', sT, t_b)
            for c in range(CH):
                xfch = self.xfc(c, tc * 512, 512)
                xbch = self.xbc(c, tc * 512, 512)
                self.V(lambda e, a=xfch, r=rb: e.tensor_mul(a, a, r))
                self.V(lambda e, a=xfch, r=cb: e.tensor_add(a, a, r))
                self.V(lambda e, a=xbch, b=xfch: e.tensor_copy(a, b))
            self.vmark()
        self.v_ln1 = sV.n

    def ffn_tables(self):
        sT, sV, sS = self.sT, self.sV, self.sS
        self.tab_cx = self.ab.buf(T)
        self.tab_sx = self.ab.buf(T)
        self.tab_cdx = self.ab.buf(T)
        self.tab_sdx = self.ab.buf(T)
        trow = self.arf[0:1, 0:T]
        prow = self.arf[0:1, 2048:4096]
        invf = self.arf[0:1, 4096:4224]
        lnb = self.arf[0:1, 4224:4352]
        hrow = self.arf[0:1, 4480:4608]
        xs = self.arf[:, 4864:5376]
        mm = self.arf[:, 5376:5888]
        sn = self.arf[:, 4352:4864]
        rxs = self.arf[:, 5888:6400]
        self.wait('g', self.sV, self.sV.n)
        self.wait('g', self.sT, self.sT.n)
        self.G(lambda e: e.iota(hrow, [[0, 4], [2, 32]], base=0, channel_multiplier=0,
                                allow_small_or_imprecise_dtypes=True).then_inc(self.sG.h, 1))
        self.G(lambda e: e.iota(trow, [[1, T]], base=0, channel_multiplier=0,
                                allow_small_or_imprecise_dtypes=True).then_inc(self.sG.h, 1))
        g_rows = self.sG.inc(2)
        self.wait('v', self.sG, g_rows)
        self.wait('v', sV, self.v_ln1)
        LN1E4 = float(np.log(10000.0))
        self.V(lambda e, s=-LN1E4 / 64.0: e.tensor_scalar(invf, hrow, s, None, ALU.mult))
        self.V(lambda e: e.tensor_scalar(lnb, hrow, 25.6, 1.0 / 89.6, ALU.add, ALU.mult))
        self.V(lambda e: e.tensor_scalar(prow, trow, -1024.0, 1.0 / 256.0, ALU.add, ALU.mult))
        v_pre = self.vmark()
        self.wait('s', sV, v_pre)
        self.S(lambda e: e.activation(invf, invf, ACT.Exp))
        self.S(lambda e: e.activation(lnb, lnb, ACT.Ln))
        s_pre = self.smark()
        if "arfrow0" in self.probes and "arfrow0" not in self.dbg_tensors:
            self.probe("arfrow0", self.arf[0:1, 0:4608])
        self.wait('t', sS, s_pre)
        self.wait('t', sV, sV.n)
        self.wait('t', sS, sS.n)
        for tc in range(4):
            fr = self.pA[:, 0:512]
            lx = self.pB[:, 0:512]
            trch = trow[:, tc * 512:tc * 512 + 512]
            prch = prow[:, tc * 512:tc * 512 + 512]
            if tc > 0:
                self.wait('t', sV, sV.n)
            self.T(lambda e, o=fr, r=trch: e.matmul(o, invf, r, start=True, stop=True))
            self.T(lambda e, o=lx, r=prch: e.matmul(o, lnb, r, start=True, stop=True))
            t_o = self.tmark()
            cxc = self.tab_cx[:, tc * 512:tc * 512 + 512]
            sxc = self.tab_sx[:, tc * 512:tc * 512 + 512]
            cdc = self.tab_cdx[:, tc * 512:tc * 512 + 512]
            sdc = self.tab_sdx[:, tc * 512:tc * 512 + 512]
            self.wait('v', sT, t_o)
            I2PI = 1.0 / (2 * PI)
            ki = self.si32[:]
            def rr(buf, extra_turn, extra_add):
                self.V(lambda e, a=buf, b=fr: e.tensor_scalar(a, b, I2PI, extra_turn, ALU.mult, ALU.add)
                       if extra_turn is not None else e.tensor_scalar(a, b, I2PI, None, ALU.mult))
                self.vsync()
                self.V(lambda e, a=ki, b=buf: e.tensor_copy(a, b))
                self.vsync()
                self.V(lambda e, a=buf, b=ki: e.tensor_copy(a, b))
                self.vsync()
                self.V(lambda e, a=buf, b=fr: e.scalar_tensor_tensor(a, a, -2 * PI, b, ALU.mult, ALU.add))
                if extra_add:
                    self.vsync()
                    self.V(lambda e, a=buf: e.tensor_scalar(a, a, extra_add, None, ALU.add))
                self.vsync()
                self.V(lambda e, a=rxs, b=buf: e.tensor_scalar(a, b, PI, None, ALU.is_gt))
                self.vsync()
                self.V(lambda e, a=buf, b=rxs: e.scalar_tensor_tensor(a, b, -2 * PI, a, ALU.mult, ALU.add))
                self.vsync()
                self.V(lambda e, a=buf: e.tensor_scalar(a, a, PI, -PI, ALU.min, ALU.max))
            rr(sn, None, 0.0)
            v1 = self.vmark()
            self.wait('s', sV, v1)
            self.S(lambda e, a=sn: e.activation(a, a, ACT.Sin))
            s1 = self.smark()
            self.wait('v', sS, s1)
            rr(mm, 0.25, PI / 2)
            v2 = self.vmark()
            self.wait('s', sV, v2)
            self.S(lambda e, a=mm: e.activation(a, a, ACT.Sin))
            self.S(lambda e, a=xs, b=lx: e.activation(a, b, ACT.Exp))
            s2 = self.smark()
            self.wait('v', sS, s2)
            self.V(lambda e, a=rxs, b=xs: e.reciprocal(a, b))
            self.V(lambda e, a=cxc: e.tensor_mul(a, mm, xs))
            self.V(lambda e, a=sxc: e.tensor_mul(a, sn, xs))
            self.V(lambda e, a=cdc: e.tensor_mul(a, mm, rxs))
            self.V(lambda e, a=sdc: e.tensor_mul(a, sn, rxs))
            self.vmark()
            if tc == 0 and "tb0_sn" in self.probes and "tb0_sn" not in self.dbg_tensors:
                self.probe("tb0_sn", sn)
                self.probe("tb0_mm", mm)
                self.probe("tb0_xs", xs)
                self.probe("tb0_rxs", rxs)
        self.v_tables = sV.n
        if "tcx" in self.probes and "tcx" not in self.dbg_tensors:
            self.probe("tcx", self.tab_cx)
            self.probe("tsx", self.tab_sx)
            self.probe("tcdx", self.tab_cdx)
            self.probe("tsdx", self.tab_sdx)

    def banded_attention(self, cfg):
        sT, sV, sS = self.sT, self.sV, self.sS
        tiles = cfg['tiles']
        LB = cfg['lookback']
        inst = [(r, b) for r in tiles for b in range(2)]
        n = len(inst)
        t_sa = [0] * n
        t_tp = [0] * n
        t_pv = [0] * n
        v_nmx = [0] * n
        v_b2 = [0] * n
        v_pte = [0] * n
        v_ote = {}
        s_ln = [0] * n
        s_e2 = [0] * n
        NSA, NPT, NPOT = cfg['nsa'], cfg['npt'], cfg['npot']
        NPTS = cfg.get('npts', 2)
        self.wait('t', sV, sV.n)
        self.wait('t', sS, sS.n)

        def kwW(r):
            kw = max(0, 128 * r - LB)
            return kw, 128 * r + 128 - kw

        for w in range(n + 4):
            j = w - 1
            if 0 <= j < n:
                r, b = inst[j]
                kw, W = kwW(r)
                self.wait('v', sT, t_sa[j])
                psa = cfg['psa'](j % NSA)
                nmx = cfg['nmx'](j % 2)
                mtile = cfg['mask'](r)
                self.V(lambda e, o=psa[:, 0:W], m=mtile: e.tensor_tensor(o, o, m, ALU.add))
                self.V(lambda e, a=nmx, b=psa[:, 0:W]:
                       e.reduce_max(a, b, axis=mybir.AxisListType.X, negate=True))
                v_nmx[j] = self.vmark()
            j = w - 2
            if 0 <= j < n:
                self.wait('v', sS, s_ln[j])
                nmx, lnsg, b2 = cfg['nmx'](j % 2), cfg['lnsg'](j % 2), cfg['b2'](j % 2)
                self.V(lambda e, a=b2, b=lnsg, c=nmx: e.scalar_tensor_tensor(a, b, -1.0, c, ALU.mult, ALU.add))
                v_b2[j] = self.vmark()
            j = w - 3
            if 0 <= j < n:
                r, b = inst[j]
                kw, W = kwW(r)
                self.wait('v', sT, t_tp[j])
                if j >= NPTS:
                    self.wait('v', sT, t_pv[j - NPTS])
                ppt = cfg['ppt'](j % NPT)
                pts = cfg['pts'](j % NPTS)
                self.V(lambda e, a=pts[:, 0:W], b=ppt[:, 0:W]: e.tensor_copy(a, b))
                v_pte[j] = self.vmark()
            j = w - 4
            if 0 <= j < n:
                r, b = inst[j]
                if b == 1:
                    self.wait('v', sT, t_pv[j])
                    slot = (j // 2) % NPOT
                    cfg['otevac'](r, cfg['pot'](slot))
                    v_ote[r] = self.vmark()
            j = w - 1
            if 0 <= j < n:
                r, b = inst[j]
                kw, W = kwW(r)
                self.wait('s', sV, v_nmx[j])
                if j >= 2:
                    self.wait('s', sT, t_tp[j - 2])
                psa = cfg['psa'](j % NSA)
                nmx, sg, lnsg = cfg['nmx'](j % 2), cfg['sg'](j % 2), cfg['lnsg'](j % 2)
                self.S(lambda e, o=cfg['garb'](j)[:, 0:W], i=psa[:, 0:W], b=nmx, a=sg:
                       e.activation(o, i, ACT.Exp, bias=b, accum_out=a))
                self.ssync()
                self.S(lambda e, a=lnsg, b=sg: e.activation(a, b, ACT.Ln))
                s_ln[j] = self.smark()
            j = w - 2
            if 0 <= j < n:
                r, b = inst[j]
                kw, W = kwW(r)
                self.wait('s', sV, v_b2[j])
                if j >= 2:
                    self.wait('s', sT, t_tp[j - 2])
                psa = cfg['psa'](j % NSA)
                b2 = cfg['b2'](j % 2)
                pr = cfg['probs'](j % 2)
                self.S(lambda e, o=pr[:, 0:W], i=psa[:, 0:W], b=b2:
                       e.activation(o, i, ACT.Exp, bias=b).then_inc(self.sS.h, 1))
                s_e2[j] = sS.inc()
            if w < n:
                r, b = inst[w]
                kw, W = kwW(r)
                if w >= NSA:
                    self.wait('t', sS, s_e2[w - NSA])
                psa = cfg['psa'](w % NSA)
                q = cfg['qbuf'](b, r)
                k = cfg['kbuf'](b, kw, W)
                for m0 in range(0, W, 512):
                    m1 = min(W, m0 + 512)
                    self.T(lambda e, o=psa[:, m0:m1], q=q, kk=k[:, m0:m1]:
                           e.matmul(o, q, kk, start=True, stop=True))
                t_sa[w] = self.tmark()
            j = w - 2
            if 0 <= j < n:
                r, b = inst[j]
                kw, W = kwW(r)
                self.wait('t', sS, s_e2[j])
                if j >= NPT:
                    self.wait('t', sV, v_pte[j - NPT])
                ppt = cfg['ppt'](j % NPT)
                pr = cfg['probs'](j % 2)
                for kb in range(W // 128):
                    self.T(lambda e, o=ppt[:, kb * 128:kb * 128 + 128],
                           l=pr[:, kb * 128:kb * 128 + 128]:
                           e.matmul(o, l, self.ident[:], start=True, stop=True))
                t_tp[j] = self.tmark()
            j = w - 3
            if 0 <= j < n:
                r, b = inst[j]
                kw, W = kwW(r)
                self.wait('t', sV, v_pte[j])
                slot = (j // 2) % NPOT
                if b == 0 and j >= 2 * NPOT:
                    self.wait('t', sV, v_ote[inst[j - 2 * NPOT][0]])
                pot = cfg['pot'](slot)
                pts = cfg['pts'](j % NPTS)
                nkb = W // 128
                for kb in range(nkb):
                    tt = kw // 128 + kb
                    self.T(lambda e, o=pot[64 * b:64 * b + 64, 0:128],
                           l=cfg['vblk'](tt, b), rr=pts[:, kb * 128:kb * 128 + 128],
                           st=(kb == 0), sp=(kb == nkb - 1), b=b:
                           e.matmul(o, l, rr, start=st, stop=sp, tile_position=(0, 64 * b)))
                t_pv[j] = self.tmark()

    def ffn_layer(self, layer):
        kbff = int(os.environ.get('KBFF', '99'))
        sT, sV, sS, sW = self.sT, self.sV, self.sS, self.sW
        srcc = (lambda c, t0, nn, sh=True: self.xbc(c, t0, nn, sh)) if layer == 0 else \
               (lambda c, t0, nn, sh=True: self.xfbc(c, t0, nn, sh))
        qs_col = self.qsc[:, layer * 4 + 0:layer * 4 + 1]
        ks_col = self.qsc[:, layer * 4 + 1:layer * 4 + 2]
        qsp_col = self.qsc[:, layer * 4 + 2:layer * 4 + 3]
        ksp_col = self.qsc[:, layer * 4 + 3:layer * 4 + 4]

        self.wait('v', self.sG, self.g_r01)
        self.wait('v', self.sX, self.x_bias)
        self.V(lambda e: e.tensor_scalar(self.rotq[:], self.r01[:], qs_col, None, ALU.mult))
        self.V(lambda e: e.tensor_scalar(self.rotk[:], self.r01[:], ks_col, None, ALU.mult))
        v_rot = self.vmark()

        am0 = self.ab.mark()
        self.ffn_tables()
        qb = self.ab.buf(1024)
        kbuf = self.ab.buf(1536)
        vbuf = self.ab.buf(1536)
        wsl0 = self.ab.buf(3 * 1024)
        wsl = [wsl0, wsl0]
        oTf = self.ab.buf(4 * 1024)
        prb = [self.ab.buf(640) for _ in range(2)]
        ptsb = [self.ab.buf(640) for _ in range(2)]
        mxc = [self.svcol() for _ in range(2)]
        nmxc = [self.svcol() for _ in range(2)]
        sgc = [self.svcol() for _ in range(2)]
        lnsgc = [self.svcol() for _ in range(2)]
        b2c = [self.svcol() for _ in range(2)]

        w_slot_free = [0, 0]
        wq_ready = [0, 0, 0, 0]
        for half in range(2):
            ht0 = half * 1024
            kv0 = max(0, ht0 - 512)
            klen = ht0 + 1024 - kv0

            def issue_qkvw(pf):
                slot = 0
                self.wait('g', sT, w_slot_free[slot])
                for part in range(3):
                    dst = wsl[slot][:, part * 1024:part * 1024 + 1024]
                    src = self.d_fwqkv2[layer, pf, part]
                    self.G(lambda e, d=dst, s=src: e.dma_start(out=d, in_=s).then_inc(self.sW.h, 16))
                    sW.inc(16)
                wq_ready[pf] = sW.n

            issue_qkvw(0)
            for pf in range(4):
                slot = 0
                self.wait('t', sW, wq_ready[pf])
                self.wait('t', sV, sV.n)
                self.wait('t', sS, sS.n)
                if layer == 1:
                    self.wait('t', sS, self.s_xfb_next)
                pq2 = [self.pD[:, 0:512], self.pD[:, 512:1024]]
                ev = []
                ci = 0
                for part, dstbuf, t0r in ((0, qb, ht0), (1, kbuf, kv0), (2, vbuf, kv0)):
                    wsrc = wsl[slot][:, part * 1024:part * 1024 + 1024]
                    if part < 2:
                        cl = [(a0, min(512, ht0 + 1024 - a0)) for a0 in range(t0r, ht0 + 1024, 512)]
                    else:
                        cl = [(tt * 128, 128) for tt in range(kv0 // 128, (ht0 + 1024) // 128)]
                    for (a0, nn) in cl:
                        ps = pq2[ci % 2]
                        if ci >= 2:
                            eng, mk = ev[ci - 2]
                            self.wait('t', eng, mk)
                        if part < 2:
                            for kc in range(CH):
                                self.T(lambda e, o=ps[:, 0:nn], l=wsrc[:, kc * 128:kc * 128 + 128],
                                       r=srcc(kc, a0, nn), st=(kc == 0), sp=(kc == CH - 1):
                                       e.matmul(o, l, r, start=st, stop=sp))
                        else:
                            for kc in range(CH):
                                self.T(lambda e, o=ps[:, 0:128], l=srcc(kc, a0, 128),
                                       r=wsrc[:, kc * 128:kc * 128 + 128], st=(kc == 0), sp=(kc == CH - 1):
                                       e.matmul(o, l, r, start=st, stop=sp))
                        t_q = self.tmark()
                        dst = dstbuf[:, a0 - t0r:a0 - t0r + nn]
                        if ci % 2 == 0:
                            self.wait('v', sT, t_q)
                            self.V(lambda e, a=dst, b=ps[:, 0:nn]: e.tensor_copy(a, b))
                            ev.append((sV, self.vmark()))
                        else:
                            self.wait('s', sT, t_q)
                            self.S(lambda e, a=dst, b=ps[:, 0:nn]: e.copy(a, b).then_inc(self.sS.h, 1))
                            ev.append((sS, sS.inc()))
                        ci += 1
                v_qkv = sV.n
                w_slot_free[slot] = sT.n
                if pf < 3:
                    issue_qkvw(pf + 1)

                if kbff < 1:
                    continue
                kbl2 = int(os.environ.get('KBL2', '99'))
                chunks = []
                for which, buf, t0r, rot, tcx, tsx, biasc in (
                        (0, qb, ht0, self.rotq, self.tab_cx, self.tab_sx, self.cln8),
                        (1, kbuf, kv0, self.rotk, self.tab_cdx, self.tab_sdx, None)):
                    nch = (ht0 + 1024 - t0r + 511) // 512
                    for i in range(nch):
                        chunks.append((which, buf, t0r, rot, tcx, tsx, biasc, i))
                nck = len(chunks)
                qsq2 = [ptsb[0][:, 0:512], ptsb[1][:, 0:512]]
                s_sq = [0] * nck
                t_st = [0] * nck
                v_cl = [0] * nck
                s_isd = [0] * nck

                def rrof(idx):
                    wh, _, _, _, _, _, _, i = chunks[idx]
                    c0 = (0 if wh == 0 else 1024) + i * 512
                    return self.arf[0:2, c0:c0 + 512]

                self.wait('s', sV, v_qkv)
                for idx, (which, buf, t0r, rot, tcx, tsx, biasc, i) in enumerate(chunks):
                    ch = buf[:, i * 512:i * 512 + 512]
                    pr = (self.pC[0:2, 0:512], self.pC[0:2, 512:1024])[idx % 2]
                    rr = rrof(idx)
                    if idx >= 2:
                        self.wait('s', sT, t_st[idx - 2])
                    self.S(lambda e, a=qsq2[idx % 2], b=ch: e.square(a, b).then_inc(self.sS.h, 1))
                    s_sq[idx] = sS.inc()
                    self.wait('t', sS, s_sq[idx])
                    if idx >= 2:
                        self.wait('t', sV, v_cl[idx - 2])
                    self.T(lambda e, o=pr, r=qsq2[idx % 2]: e.matmul(o, self.bd2[:], r, start=True, stop=True))
                    t_st[idx] = self.tmark()
                    self.wait('v', sT, t_st[idx])
                    self.V(lambda e, a=rr, b=pr: e.tensor_scalar(a, b, 1e-24, None, ALU.max))
                    v_cl[idx] = self.vmark()
                for idx in range(nck):
                    if kbl2 < 2:
                        break
                    biasc = chunks[idx][6]
                    rr = rrof(idx)
                    self.wait('s', sV, v_cl[idx])
                    self.S(lambda e, a=rr: e.activation(a, a, ACT.Ln))
                    self.ssync()
                    if biasc is not None:
                        self.S(lambda e, a=rr, b=biasc: e.activation(
                            a, a, ACT.Exp, scale=self.cm05[0:2, :], bias=b[0:2, :]).then_inc(self.sS.h, 1))
                    else:
                        self.S(lambda e, a=rr: e.activation(
                            a, a, ACT.Exp, scale=self.cm05[0:2, :]).then_inc(self.sS.h, 1))
                    s_isd[idx] = sS.inc()
                t_rot = [0] * nck
                v_ap = [0] * nck
                for idx in range(nck):
                    if kbl2 < 3:
                        break
                    (which, buf, t0r, rot, tcx, tsx, biasc, i) = chunks[idx]
                    a0 = t0r + i * 512
                    ch = buf[:, i * 512:i * 512 + 512]
                    rr = rrof(idx)
                    prot = (self.pA[:, 0:512], self.pA[:, 512:1024])[idx % 2]
                    prb_ = (self.pB[:, 0:512], self.pB[:, 512:1024])[idx % 2]
                    self.wait('t', sS, s_isd[idx])
                    if idx == 0:
                        self.wait('t', sV, v_rot)
                    if idx >= 2:
                        self.wait('t', sV, v_ap[idx - 2])
                    self.T(lambda e, o=prot, l=rot, r=ch: e.matmul(o, l[:], r, start=True, stop=True))
                    self.T(lambda e, o=prb_, r=rr: e.matmul(o, self.exp2m[:], r, start=True, stop=True))
                    t_rot[idx] = self.tmark()
                    t1f, t2f = ((self.arf[:, 4352:4864], self.arf[:, 4864:5376]),
                                (self.arf[:, 5376:5888], self.arf[:, 5888:6400]))[idx % 2]
                    scol = qs_col if which == 0 else ks_col
                    self.wait('v', sT, t_rot[idx])
                    self.V(lambda e, a=t1f, b=ch, s=scol, cc=tcx[:, a0:a0 + 512]:
                           e.scalar_tensor_tensor(a, b, s, cc, ALU.mult, ALU.mult))
                    self.V(lambda e, a=t2f, b=prot, cc=tsx[:, a0:a0 + 512]:
                           e.tensor_mul(a, b, cc))
                    self.V(lambda e, a=t1f, b=t2f: e.tensor_add(a, a, b))
                    self.V(lambda e, a=ch, b=t1f, cc=prb_: e.tensor_mul(a, b, cc))
                    v_ap[idx] = self.vmark()

                if half == 0 and pf == 0:
                    self.probe(f"qb{layer}", qb)
                    self.probe(f"kb{layer}", kbuf[:, 0:1024])
                    self.probe(f"vb{layer}", vbuf[:, 0:1024])
                def mk_ote(pf, half):
                    def f(r, pot):
                        dst = oTf[:, pf * 1024 + (r - half * 8) * 128: pf * 1024 + (r - half * 8) * 128 + 128]
                        self.V(lambda e, a=dst, b=pot[:, 0:128]: e.tensor_copy(a, b))
                    return f
                cfg = dict(
                    tiles=list(range(half * 8, half * 8 + 8)),
                    lookback=512,
                    qbuf=lambda b, r: qb[64 * b:64 * b + 64, (r * 128 - ht0):(r * 128 - ht0) + 128],
                    kbuf=lambda b, kw, W: kbuf[64 * b:64 * b + 64, kw - kv0:kw - kv0 + W],
                    vblk=lambda tt, b: vbuf[:, (tt * 128 - kv0) + 64 * b:(tt * 128 - kv0) + 64 * b + 64],
                    mask=lambda r: (self.fm_r[r] if r < 4 else self.fm_int),
                    psa=lambda s: (self.pA, self.pB)[s][:, 0:640],
                    nsa=2,
                    ppt=lambda s: self.pC[:, 0:640],
                    npt=1,
                    pot=lambda s: self.pD[:, 0:128],
                    npot=1,
                    probs=lambda s: prb[s], pts=lambda s: ptsb[s],
                    garb=lambda j: prb[j % 2],
                    mx=lambda s: mxc[s], nmx=lambda s: nmxc[s], sg=lambda s: sgc[s],
                    lnsg=lambda s: lnsgc[s], b2=lambda s: b2c[s],
                    otevac=mk_ote(pf, half),
                )
                if kbff >= 2:
                    self.banded_attention(cfg)
                if half == 0 and pf == 0:
                    self.wait('y', self.sV, self.sV.n)
                    self.probe(f"oTf{layer}", oTf[:, 0:1024])

            if half == 0:
                self.wait('y', self.sV, self.sV.n)
                self.probe(f"oTfall{layer}", oTf[:])
            if kbff < 3:
                self.ab.free(am0 if False else self.ab.mark())
                continue
            am1 = self.ab.mark()
            wo_sl = [wsl[0][:, 0:512], wsl[0][:, 512:1024]]
            wo_free = [max(w_slot_free), max(w_slot_free)]
            self.wait('t', sV, sV.n)
            self.wait('t', sS, sS.n)
            for fc in range(CH):
                ws = fc % 2
                self.wait('g', sT, wo_free[ws])
                dstw = wo_sl[ws]
                srcw = self.d_fwo2[layer, fc]
                self.G(lambda e, d=dstw, s=srcw: e.dma_start(out=d, in_=s).then_inc(self.sW.h, 16))
                w_wo = sW.inc(16)
                self.wait('t', sW, w_wo)
                for tc in range(2):
                    po = (self.pA, self.pB)[tc % 2][:, 0:512]
                    for kc in range(4):
                        self.T(lambda e, o=po, l=wo_sl[ws][:, kc * 128:kc * 128 + 128],
                               r=oTf[:, kc * 1024 + tc * 512:kc * 1024 + tc * 512 + 512],
                               st=(kc == 0), sp=(kc == 3): e.matmul(o, l, r, start=st, stop=sp))
                    t_wo = self.tmark()
                    self.wait('v', sT, t_wo)
                    xfch = self.xfc(fc, ht0 + tc * 512, 512)
                    self.V(lambda e, a=xfch, b=po: e.tensor_add(a, a, b))
                    v_wo = self.vmark()
                    self.wait('t', sV, v_wo)
                wo_free[ws] = sT.n
            self.ab.free(am1)
            w_slot_free[0] = max(w_slot_free[0], sT.n)
        v_wo_all = sV.n
        self.ab.free(am0)

        self.probe(f"xfwo{layer}", self.xf[:])
        if kbff < 4:
            self.v_ffn = sV.n
            self.s_ffn = sS.n
            if layer == 0:
                self.s_xfb_next = 0
            return
        self.wait('s', sV, v_wo_all)
        for c in range(CH):
            self.S(lambda e, a=self.xfbc(c, 0, T), b=self.xfc(c, 0, T): e.copy(a, b))
        s_xfb = self.smark()

        am2 = self.ab.mark()
        sq = [self.ab.buf(T), self.ab.buf(T)]
        s_sq2 = [sS.n + c + 1 for c in range(CH)]
        t_mm2 = [sT.n + c + 1 for c in range(CH)]
        for c in range(CH):
            if c >= 2:
                self.wait('s', sT, t_mm2[c - 2])
            xfsh = self.xfc(c, 0, T, True)
            self.S(lambda e, a=sq[c % 2], b=xfsh: e.square(a, b).then_inc(self.sS.h, 1))
            sS.inc()
        assert sS.n == s_sq2[-1]
        psS = [self.pA[0:1, 0:512], self.pA[0:1, 512:1024],
               self.pB[0:1, 0:512], self.pB[0:1, 512:1024]]
        psQ = [self.pC[0:1, 0:512], self.pC[0:1, 512:1024],
               self.pD[0:1, 0:512], self.pD[0:1, 512:1024]]
        self.wait('t', sS, s_xfb)
        for c in range(CH):
            self.wait('t', sS, s_sq2[c])
            for tc in range(4):
                xbs = self.xfbc(c, tc * 512, 512, True)
                sqch = sq[c % 2][:, tc * 512:tc * 512 + 512]
                st, sp = (c == 0), (c == CH - 1)
                self.T(lambda e, o=psS[tc], r=xbs, st=st, sp=sp:
                       e.matmul(o, self.ones128b, r, start=st, stop=sp))
                self.T(lambda e, o=psQ[tc], r=sqch, st=st, sp=sp:
                       e.matmul(o, self.ones128b, r, start=st, stop=sp))
            self.tmark()
        assert sT.n == t_mm2[-1]
        self.ab.free(am2)
        v_rows = self.ln_rows(psS, psQ, sT.n)

        snap = self.svcol(12)
        self.wait('v', self.sV, sV.n)
        for c in range(4, CH):
            for k in range(1, 4):
                dst = snap[:, (c - 4) * 3 + (k - 1):(c - 4) * 3 + k]
                srcb = self.xfc(c, 512 * k - 1, 1)
                self.V(lambda e, a=dst, b=srcb: e.tensor_copy(a, b))
        v_snap = self.vmark()
        self.probe(f"snap{layer}", snap)

        am3 = self.ab.mark()
        region = self.ab.buf(8192)
        h2n = region[:, 0:4096]
        w1sl = [region[:, 4096:6144], region[:, 6144:8192]]
        w2sl = [region[:, 0:2816], region[:, 2816:5632]]
        actb = self.ab.buf(22 * 512)
        gelb = [self.ab.buf(512), self.ab.buf(512)]
        gc = self.gcol[:, layer * 8:layer * 8 + 8]
        bc = self.bcol[:, layer * 8:layer * 8 + 8]
        b2cc = self.b2col[:, layer * 8:layer * 8 + 8]
        w1_free = [0, 0]
        w2_free = [0, 0]
        v_act = {}
        t_u_last, t_w2_last = 0, 0
        for tc in range(4):
            rs = self.rs_(tc * 512, 512)
            c0 = self.c0_(tc * 512, 512)
            rb = self.pA[:, 0:512]
            cb = self.pA[:, 512:1024]
            self.wait('t', sV, v_rows if tc == 0 else sV.n)
            self.T(lambda e, o=rb, r=rs: e.matmul(o, self.onesrow[:], r, start=True, stop=True))
            self.T(lambda e, o=cb, r=c0: e.matmul(o, self.onesrow[:], r, start=True, stop=True))
            t_bb = self.tmark()
            self.wait('v', sT, t_bb)
            tmpf2 = [self.arf[:, 4352:4864], self.arf[:, 4864:5376]]
            if tc > 0:
                self.wait('s', sT, t_u_last)
                self.wait('v', sT, t_w2_last)
            s_h2 = 0
            s_h2s = [0] * CH
            for c in range(CH):
                tmpf = tmpf2[c % 2]
                xfsh = self.xfc(c, tc * 512, 512, True)
                if c >= 2:
                    self.wait('v', sS, s_h2s[c - 2])
                self.V(lambda e, a=tmpf, b=xfsh, r=rb: e.tensor_mul(a, b, r))
                self.V(lambda e, a=tmpf, r=cb: e.tensor_add(a, a, r))
                if tc >= 1 and c >= 4:
                    sc = snap[:, (c - 4) * 3 + (tc - 1):(c - 4) * 3 + tc]
                    self.vsync()
                    self.V(lambda e, a=tmpf[:, 0:1], b=sc, r=rb[:, 0:1]: e.tensor_mul(a, b, r))
                    self.vsync()
                    self.V(lambda e, a=tmpf[:, 0:1], r=cb[:, 0:1]: e.tensor_add(a, a, r))
                v_t = self.vmark()
                self.wait('s', sV, v_t)
                self.S(lambda e, a=h2n[:, c * 512:c * 512 + 512], b=tmpf,
                       g=gc[:, c:c + 1], bb=bc[:, c:c + 1]:
                       e.activation(a, b, ACT.Identity, bias=bb, scale=g).then_inc(self.sS.h, 1))
                s_h2 = sS.inc()
                s_h2s[c] = s_h2
            s_h2n = s_h2
            if tc == 0:
                self.probe(f"h2n{layer}", h2n)
            if tc == 1:
                self.probe(f"h2nb{layer}", h2n)
            for ci in range(22):
                wslot = ci % 2
                wdt = min(128, 2730 - 128 * ci)
                self.wait('g', sT, w1_free[wslot])
                for ag in range(2):
                    o = 262144 * ci + ag * 1024 * wdt
                    if wdt == 128:
                        dstw = w1sl[wslot][:, ag * 1024:ag * 1024 + 1024]
                        s_ = self.d_fw1s[layer, o:o + 131072].rearrange(
                            "(p f) -> p f", p=128)
                    else:
                        dstw = w1sl[wslot][:, ag * 1024:ag * 1024 + 1024].rearrange(
                            "p (k f) -> p k f", f=128)[:, :, 0:wdt]
                        s_ = self.d_fw1s[layer, o:o + 1024 * wdt].rearrange(
                            "(p k f) -> p k f", p=128, k=8)
                    self.G(lambda e, d=dstw, s=s_:
                           e.dma_start(out=d, in_=s).then_inc(self.sW.h, 16))
                    sW.inc(16)
                w1_ready = sW.n
                ppair = (self.pB, self.pC)[ci % 2]
                pa = ppair[0:wdt, 0:512]
                pg = ppair[0:wdt, 512:1024]
                self.wait('t', sW, w1_ready)
                if ci == 0:
                    self.wait('t', sS, s_h2n)
                if ci >= 2:
                    self.wait('t', sV, v_act[ci - 2])
                for kc in range(CH):
                    self.T(lambda e, o=pa, l=w1sl[wslot][:, kc * 128:kc * 128 + wdt],
                           r=h2n[:, kc * 512:kc * 512 + 512], st=(kc == 0), sp=(kc == CH - 1):
                           e.matmul(o, l, r, start=st, stop=sp))
                for kc in range(CH):
                    self.T(lambda e, o=pg, l=w1sl[wslot][:, 1024 + kc * 128:1024 + kc * 128 + wdt],
                           r=h2n[:, kc * 512:kc * 512 + 512], st=(kc == 0), sp=(kc == CH - 1):
                           e.matmul(o, l, r, start=st, stop=sp))
                t_u = self.tmark()
                w1_free[wslot] = t_u
                t_u_last = t_u
                b1a = self.b1c[:, layer * 44 + 2 * ci:layer * 44 + 2 * ci + 1]
                b1g = self.b1c[:, layer * 44 + 2 * ci + 1:layer * 44 + 2 * ci + 2]
                self.wait('s', sT, t_u)
                if ci >= 2:
                    self.wait('s', sV, v_act[ci - 2])
                gbuf = gelb[ci % 2]
                self.S(lambda e, o=gbuf[0:wdt, 0:512], i=pg, b=b1g[0:wdt, :]:
                       e.activation(o, i, ACT.Gelu, bias=b).then_inc(self.sS.h, 1))
                s_gel = sS.inc()
                self.wait('v', sS, s_gel)
                dst = actb[0:wdt, ci * 512:ci * 512 + 512]
                self.V(lambda e, a=dst, b=pa, s=b1a[0:wdt, :], g=gbuf[0:wdt, 0:512]:
                       e.scalar_tensor_tensor(a, b, s, g, ALU.add, ALU.mult))
                v_act[ci] = self.vmark()
            v_act_all = sV.n
            if tc == 0:
                self.probe(f"act{layer}", actb)
            if tc == 1:
                self.probe(f"actb{layer}", actb)
            for fc in range(CH):
                wslot = fc % 2
                self.wait('g', sT, w2_free[wslot])
                self.wait('g', sT, t_u_last)
                off = fc * 349440
                src1 = self.d_fw2s[layer, off:off + 344064].rearrange(
                    "(p f) -> p f", p=128)
                self.G(lambda e, d=w2sl[wslot][:, 0:2688], s=src1:
                       e.dma_start(out=d, in_=s).then_inc(self.sW.h, 16))
                sW.inc(16)
                src2 = self.d_fw2s[layer, off + 344064:off + 349440].rearrange(
                    "(p f) -> p f", p=42)
                self.G(lambda e, d=w2sl[wslot][0:42, 2688:2816], s=src2:
                       e.dma_start(out=d, in_=s).then_inc(self.sW.h, 16))
                w2_ready = sW.inc(16)
                po = (self.pC, self.pD)[fc % 2][:, 0:512]
                if fc < 2:
                    self.wait('t', sV, v_act_all)
                self.wait('t', sW, w2_ready)
                for kc in range(22):
                    wdt = min(128, 2730 - 128 * kc)
                    self.T(lambda e, o=po, l=w2sl[wslot][0:wdt, kc * 128:kc * 128 + 128],
                           r=actb[0:wdt, kc * 512:kc * 512 + 512], st=(kc == 0), sp=(kc == 21):
                           e.matmul(o, l, r, start=st, stop=sp))
                t_w2 = self.tmark()
                w2_free[wslot] = t_w2
                t_w2_last = t_w2
                self.wait('v', sT, t_w2)
                xfch = self.xfc(fc, tc * 512, 512)
                self.V(lambda e, a=xfch, b=po, s=b2cc[:, fc:fc + 1]:
                       e.scalar_tensor_tensor(a, b, s, a, ALU.add, ALU.add))
                self.vmark()
            w1_free[0] = w1_free[1] = t_w2_last
        self.ab.free(am3)
        if layer == 0:
            self.wait('s', sV, sV.n)
            self.wait('s', sT, sT.n)
            for c in range(CH):
                self.S(lambda e, a=self.xfbc(c, 0, T), b=self.xfc(c, 0, T): e.copy(a, b))
            self.s_xfb_next = self.smark()
        self.v_ffn = sV.n
        self.s_ffn = sS.n

    def attn(self):
        sT, sV, sS, sW = self.sT, self.sV, self.sS, self.sW
        am0 = self.ab.mark()
        qb = self.ab.buf(1024)
        kbuf = self.ab.buf(1280)
        vbuf = self.ab.buf(1280)
        wsl = [self.ab.buf(3 * 1024) for _ in range(2)]
        oT = self.ab.buf(CH * 1024)
        wot_sl = [self.ab.buf(1024), self.ab.buf(1024)]
        prb = [self.ab.buf(384) for _ in range(2)]
        ptsb = [self.ab.buf(384) for _ in range(2)]
        garb = self.ab.buf(384)
        mxc = [self.svcol() for _ in range(2)]
        nmxc = [self.svcol() for _ in range(2)]
        sgc = [self.svcol() for _ in range(2)]
        lnsgc = [self.svcol() for _ in range(2)]
        b2c = [self.svcol() for _ in range(2)]

        self.wait('g', sT, sT.n)
        self.wait('g', sV, sV.n)
        self.wait('g', sS, sS.n)
        w_slot_free = [0, 0]
        wot_free = [0, 0]
        wq_ready = [0] * CH
        for half in range(2):
            ht0 = half * 1024
            kv0 = max(0, ht0 - 256)

            def issue_aw(p):
                slot = p % 2
                self.wait('g', sT, w_slot_free[slot])
                for part in range(3):
                    dst = wsl[slot][:, part * 1024:part * 1024 + 1024]
                    src = self.d_wqkv2[p, part]
                    self.G(lambda e, d=dst, s=src: e.dma_start(out=d, in_=s).then_inc(self.sW.h, 16))
                    sW.inc(16)
                wq_ready[p] = sW.n

            issue_aw(0)
            for p in range(CH):
                slot = p % 2
                self.wait('t', sW, wq_ready[p])
                self.wait('t', sV, sV.n)
                self.wait('t', sS, sS.n)
                pq2 = [self.pD[:, 0:512], self.pD[:, 512:1024]]
                ev = []
                ci = 0
                for part, dstbuf, t0r in ((0, qb, ht0), (1, kbuf, kv0), (2, vbuf, kv0)):
                    wsrc = wsl[slot][:, part * 1024:part * 1024 + 1024]
                    if part < 2:
                        cl = [(a0, min(512, ht0 + 1024 - a0)) for a0 in range(t0r, ht0 + 1024, 512)]
                    else:
                        cl = [(tt * 128, 128) for tt in range(kv0 // 128, (ht0 + 1024) // 128)]
                    for (a0, nn) in cl:
                        ps = pq2[ci % 2]
                        if ci >= 2:
                            eng, mk = ev[ci - 2]
                            self.wait('t', eng, mk)
                        if part < 2:
                            for kc in range(CH):
                                self.T(lambda e, o=ps[:, 0:nn], l=wsrc[:, kc * 128:kc * 128 + 128],
                                       r=self.xbc(kc, a0, nn), st=(kc == 0), sp=(kc == CH - 1):
                                       e.matmul(o, l, r, start=st, stop=sp))
                        else:
                            for kc in range(CH):
                                self.T(lambda e, o=ps[:, 0:128], l=self.xbc(kc, a0, 128),
                                       r=wsrc[:, kc * 128:kc * 128 + 128], st=(kc == 0), sp=(kc == CH - 1):
                                       e.matmul(o, l, r, start=st, stop=sp))
                        t_q = self.tmark()
                        dst = dstbuf[:, a0 - t0r:a0 - t0r + nn]
                        if ci % 2 == 0:
                            self.wait('v', sT, t_q)
                            if part == 0:
                                bcl = self.bqc[:, p:p + 1]
                                self.V(lambda e, a=dst, b=ps[:, 0:nn], s=bcl:
                                       e.tensor_scalar(a, b, s, 0.125, ALU.add, ALU.mult))
                            elif part == 1:
                                bcl = self.bqc[:, 8 + p:8 + p + 1]
                                self.V(lambda e, a=dst, b=ps[:, 0:nn], s=bcl:
                                       e.tensor_scalar(a, b, s, None, ALU.add))
                            else:
                                self.V(lambda e, a=dst, b=ps[:, 0:128]: e.tensor_copy(a, b))
                            ev.append((sV, self.vmark()))
                        else:
                            self.wait('s', sT, t_q)
                            if part == 0:
                                bcl = self.bqsc[:, p:p + 1]
                                self.S(lambda e, a=dst, b=ps[:, 0:nn], s=bcl:
                                       e.activation(a, b, ACT.Identity, bias=s,
                                                    scale=self.c0125).then_inc(self.sS.h, 1))
                            elif part == 1:
                                bcl = self.bqc[:, 8 + p:8 + p + 1]
                                self.S(lambda e, a=dst, b=ps[:, 0:nn], s=bcl:
                                       e.activation(a, b, ACT.Identity, bias=s).then_inc(self.sS.h, 1))
                            else:
                                self.S(lambda e, a=dst, b=ps[:, 0:128]: e.copy(a, b).then_inc(self.sS.h, 1))
                            ev.append((sS, sS.inc()))
                        ci += 1
                w_slot_free[slot] = sT.n
                if p < CH - 1:
                    issue_aw(p + 1)
                else:
                    self.wait('t', sV, sV.n)
                    self.wait('t', sS, sS.n)

                if half == 0 and p == 0:
                    self.probe("aqb", qb)
                    self.probe("akb", kbuf[:, 0:1024])
                    self.probe("avb", vbuf[:, 0:1024])
                def mk_ote(p, half):
                    def f(r, pot):
                        dst = oT[:, p * 1024 + (r - half * 8) * 128: p * 1024 + (r - half * 8) * 128 + 128]
                        bv = self.bqc[:, 16 + p:16 + p + 1]
                        self.V(lambda e, a=dst, b=pot[:, 0:128], s=bv:
                               e.tensor_scalar(a, b, s, None, ALU.add))
                    return f
                cfg = dict(
                    tiles=list(range(half * 8, half * 8 + 8)),
                    lookback=256,
                    qbuf=lambda b, r: qb[64 * b:64 * b + 64, (r * 128 - ht0):(r * 128 - ht0) + 128],
                    kbuf=lambda b, kw, W: kbuf[64 * b:64 * b + 64, kw - kv0:kw - kv0 + W],
                    vblk=lambda tt, b: vbuf[:, (tt * 128 - kv0) + 64 * b:(tt * 128 - kv0) + 64 * b + 64],
                    mask=lambda r: (self.am_r0 if r == 0 else (self.am_r1 if r == 1 else self.am_int)),
                    psa=lambda s: (self.pA[:, 0:384], self.pA[:, 512:896], self.pB[:, 0:384])[s],
                    nsa=3,
                    ppt=lambda s: (self.pB[:, 512:896], self.pC[:, 0:384])[s],
                    npt=2,
                    pot=lambda s: (self.pC[:, 512:640], self.pD[:, 0:128])[s],
                    npot=2,
                    probs=lambda s: prb[s], pts=lambda s: ptsb[s],
                    garb=lambda j, g=garb: g,
                    mx=lambda s: mxc[s], nmx=lambda s: nmxc[s], sg=lambda s: sgc[s],
                    lnsg=lambda s: lnsgc[s], b2=lambda s: b2c[s],
                    otevac=mk_ote(p, half),
                )
                self.banded_attention(cfg)
                if half == 0 and p == 0:
                    self.wait('y', sV, sV.n)
                    self.probe("aoT", oT[:, 0:1024])

            self.wait('t', sV, sV.n)
            self.wait('t', sS, sS.n)
            if half == 1:
                self.probe("aoTall", oT[:])
            for fc in range(CH):
                ws = fc % 2
                self.wait('g', sT, wot_free[ws])
                dstw = wot_sl[ws]
                srcw = self.d_wo2[fc]
                self.G(lambda e, d=dstw, s=srcw: e.dma_start(out=d, in_=s).then_inc(self.sW.h, 16))
                w_wo = sW.inc(16)
                self.wait('t', sW, w_wo)
                if half == 1 and fc == 7:
                    self.wait('y', sW, w_wo)
                    self.probe("wotfc7", wot_sl[ws])
                for tc in range(2):
                    po = (self.pA, self.pB)[tc % 2][:, 0:512]
                    for kc in range(CH):
                        self.T(lambda e, o=po, l=wot_sl[ws][:, kc * 128:kc * 128 + 128],
                               r=oT[:, kc * 1024 + tc * 512:kc * 1024 + tc * 512 + 512],
                               st=(kc == 0), sp=(kc == CH - 1): e.matmul(o, l, r, start=st, stop=sp))
                    t_wo = self.tmark()
                    self.wait('v', sT, t_wo)
                    xfch = self.xfc(fc, ht0 + tc * 512, 512)
                    self.V(lambda e, a=xfch, b=po, s=self.boc[:, fc:fc + 1]:
                           e.scalar_tensor_tensor(a, b, s, a, ALU.add, ALU.add))
                    v_wo = self.vmark()
                    self.wait('t', sV, v_wo)
                wot_free[ws] = sT.n
            if half == 1:
                self.wait('y', sV, sV.n)
                self.probe("aoT1", oT[:, 7 * 1024:8 * 1024])
                self.probe("awo", self.xf[:])
        self.ab.free(am0)

    def finish(self):
        sV, sX, sW = self.sV, self.sX, self.sW
        self.probe("xff", self.xf[:])
        self.wait('g', sV, sV.n)
        self.wait('g', self.sS, self.sS.n)
        self.wait('g', self.sT, self.sT.n)
        for c in range(CH):
            dst = self.xfc(c, 0, T)
            src = self.d_x[c * 128:(c + 1) * 128, :]
            self.G(lambda e, d=dst, s=src: e.dma_start(out=d, in_=s, accum_op=ALU.add).then_inc(self.sW.h, 16))
            sW.inc(16)
        g_acc = sW.n
        self.wait('y', sW, g_acc)
        for c in range(CH):
            dst = self.d_out[c * 128:(c + 1) * 128, :]
            src = self.xfc(c, 0, T)
            self.Y(lambda e, d=dst, s=src: e.dma_start(out=d, in_=s).then_inc(self.sX.h, 16))
            sX.inc(16)
        self.wait('y', sX, sX.n)


def build(stage=99, probes=()):
    return Builder(stage, probes).nc


def _reorder_w1(w1, b1):
    w1o = np.empty_like(w1)
    b1o = np.empty_like(b1)
    a_w, g_w = w1[:, :, :2730], w1[:, :, 2730:]
    a_b, g_b = b1[:, :2730], b1[:, 2730:]
    o = 0
    for c in range(22):
        wdt = min(128, 2730 - 128 * c)
        w1o[:, :, o:o + wdt] = a_w[:, :, 128 * c:128 * c + wdt]
        b1o[:, o:o + wdt] = a_b[:, 128 * c:128 * c + wdt]
        o += wdt
        w1o[:, :, o:o + wdt] = g_w[:, :, 128 * c:128 * c + wdt]
        b1o[:, o:o + wdt] = g_b[:, 128 * c:128 * c + wdt]
        o += wdt
    assert o == 5460
    return w1o, b1o


def _stream_w1(w1r):
    out = np.empty((2, 1024 * 5460), np.float32)
    for l in range(2):
        w = w1r[l].reshape(8, 128, 5460)
        o = 0
        for ci in range(22):
            wdt = min(128, 2730 - 128 * ci)
            for ag in range(2):
                blk = w[:, :, 256 * ci + ag * wdt:256 * ci + (ag + 1) * wdt]
                n = 1024 * wdt
                out[l, o:o + n] = blk.transpose(1, 0, 2).reshape(-1)
                o += n
        assert o == 1024 * 5460
    return out


def _stream_w2(w2):
    out = np.empty((2, 2730 * 1024), np.float32)
    for l in range(2):
        o = 0
        for fc in range(8):
            blk = w2[l, 0:2688, fc * 128:fc * 128 + 128].reshape(21, 128, 128)
            out[l, o:o + 344064] = blk.transpose(1, 0, 2).reshape(-1)
            o += 344064
            out[l, o:o + 5376] = w2[l, 2688:2730, fc * 128:fc * 128 + 128].reshape(-1)
            o += 5376
    return out


def _stream_kpf(wT, nf, nblk):
    k = wT.shape[0] // 128
    w = wT.reshape(k, 128, wT.shape[1])
    nchunk = nf // 128
    out = np.empty((nchunk, nblk, 128, k * 128), np.float32)
    for c in range(nchunk):
        for b in range(nblk):
            out[c, b] = w[:, :, b * nf + c * 128:b * nf + c * 128 + 128].transpose(
                1, 0, 2).reshape(128, k * 128)
    return out


def kernel(L0_attn_x, L0_attn_Wqkv, L0_attn_bqkv, L0_attn_Wo, L0_attn_bo,
           L0_ffn_wqkv, L0_ffn_q_scale, L0_ffn_k_scale, L0_ffn_wo,
           L0_ffn_ln_g, L0_ffn_ln_b, L0_ffn_w1, L0_ffn_b1, L0_ffn_w2,
           L0_ffn_b2):
    from concourse.bass_utils import run_bass_kernel_spmd
    f32 = lambda a: np.ascontiguousarray(np.asarray(a, dtype=np.float32))
    x = f32(L0_attn_x)
    w1r, b1r = _reorder_w1(f32(L0_ffn_w1), f32(L0_ffn_b1))
    fwqkv = f32(L0_ffn_wqkv)
    fwo = f32(L0_ffn_wo)
    shared = {
        "wqkv2": _stream_kpf(f32(L0_attn_Wqkv).T, 1024, 3),
        "bqkv": f32(L0_attn_bqkv),
        "wo2": _stream_kpf(f32(L0_attn_Wo).T, 1024, 1)[:, 0],
        "bo": f32(L0_attn_bo),
        "fwqkv2": np.stack([_stream_kpf(fwqkv[l], 512, 3) for l in range(2)]),
        "fqs": f32(L0_ffn_q_scale),
        "fks": f32(L0_ffn_k_scale),
        "fwo2": np.stack([_stream_kpf(fwo[l], 1024, 1)[:, 0] for l in range(2)]),
        "flng": f32(L0_ffn_ln_g),
        "flnb": f32(L0_ffn_ln_b),
        "fw1s": _stream_w1(w1r),
        "fb1r": np.ascontiguousarray(b1r),
        "fw2s": _stream_w2(f32(L0_ffn_w2)),
        "fb2": f32(L0_ffn_b2),
    }
    in_maps = [dict(shared, xT=np.ascontiguousarray(x[i].T)) for i in range(8)]
    nc = Builder(99).nc
    res = run_bass_kernel_spmd(nc, in_maps, core_ids=list(range(8)))
    out = np.stack([np.asarray(res.results[i]["outT"], dtype=np.float32).T
                    for i in range(8)])
    return out


# version 15
# speedup vs baseline: 1.2424x; 1.0053x over previous
import sys, os
if '/opt/trn_rl_repo' not in sys.path:
    sys.path.insert(0, '/opt/trn_rl_repo')
import numpy as np
from contextlib import ExitStack
import concourse.bass as bass
import concourse.mybir as mybir

F32 = mybir.dt.float32
BF16 = mybir.dt.bfloat16
ALU = mybir.AluOpType
ACT = mybir.ActivationFunctionType

USE_SCRATCH = False
D, T = 1024, 2048
TG = T + 1
CH, NT = 8, 16
NEG = -3.0e38
PI = float(np.pi)


class Sem:
    def __init__(self, es, nc, name):
        self.h = es.enter_context(nc.semaphore(name))
        self.n = 0

    def inc(self, amt=1):
        self.n += amt
        return self.n


class Arena:
    def __init__(self, t, n):
        self.t, self.n, self.top = t, n, 0

    def buf(self, w):
        c = self.top
        self.top += w
        assert self.top <= self.n, f"arena overflow {self.top} > {self.n}"
        return self.t[:, c:c + w]

    def mark(self):
        return self.top

    def free(self, m):
        self.top = m


class Builder:
    def __init__(self, stage=99, probes=()):
        self.stage = stage
        self.probes = set(probes)
        self.dbg_tensors = {}
        nc = bass.Bass('TRN2', target_bir_lowering=False, detect_race_conditions=False)
        self.nc = nc
        self.es = ExitStack()
        es = self.es
        dt = lambda n, sh, dty=F32, kind="ExternalInput": nc.dram_tensor(n, sh, dty, kind=kind).ap()
        self.d_x = dt("xT", [D, T])
        self.d_wqkv2 = dt("wqkv2", [8, 128, 3072])
        self.d_bqkv = dt("bqkv", [3 * D])
        self.d_wo2 = dt("wo2", [8, 128, 1024])
        self.d_bo = dt("bo", [D])
        self.d_fwqkv2 = dt("fwqkv2", [2, 4, 128, 3072])
        self.d_fqs = dt("fqs", [2, 64])
        self.d_fks = dt("fks", [2, 64])
        self.d_fwo2 = dt("fwo2", [2, 8, 128, 512])
        self.d_flng = dt("flng", [2, D])
        self.d_flnb = dt("flnb", [2, D])
        self.d_fw1s = dt("fw1s", [2, 5591040])
        self.d_fb1 = dt("fb1r", [2, 5460])
        self.d_fw2s = dt("fw2s", [2, 2795520])
        self.d_fb2 = dt("fb2", [2, D])
        self.d_out = dt("outT", [D, T], F32, "ExternalOutput")
        self.d_fw1b = dt("fw1b", [2, 5591040], BF16, "Internal")
        self.d_fw2b = dt("fw2b", [2, 2795520], BF16, "Internal")
        self.conv_all = 0
        self.conv_chunks = []
        CCH = 1397760
        for l in range(2):
            for j in range(4):
                self.conv_chunks.append((self.d_fw1b[l, j * CCH:(j + 1) * CCH],
                                         self.d_fw1s[l, j * CCH:(j + 1) * CCH]))
            for j in range(2):
                self.conv_chunks.append((self.d_fw2b[l, j * CCH:(j + 1) * CCH],
                                         self.d_fw2s[l, j * CCH:(j + 1) * CCH]))

        sb = lambda n, sh, dty: es.enter_context(nc.sbuf_tensor(n, sh, dty))
        self.xf = sb("xf", [128, CH * TG], F32)
        self.xb = sb("xb", [128, CH * TG], BF16)
        self.xfb = sb("xfb", [128, CH * TG], BF16)
        self.sv = sb("sv", [128, 80], F32)
        self.svb = sb("svb", [128, 16], BF16)
        self.ident = sb("ident", [128, 128], BF16)
        self.r01 = sb("r01", [128, 128], BF16)
        self.rotq = sb("rotq", [128, 128], BF16)
        self.rotk = sb("rotk", [128, 128], BF16)
        self.onesrow = sb("onesrow", [1, 128], F32)
        self.bd2 = sb("bd2", [128, 2], BF16)
        self.exp2m = sb("exp2m", [2, 128], F32)
        self.bqc = sb("bqc", [128, 24], F32)
        self.boc = sb("boc", [128, 8], F32)
        self.gcol = sb("gcol", [128, 16], F32)
        self.bcol = sb("bcol", [128, 16], F32)
        self.b2col = sb("b2col", [128, 16], F32)
        self.b1c = sb("b1c", [128, 88], F32)
        self.qsc = sb("qsc", [128, 8], F32)
        self.AF = 6400
        self.arf = sb("arf", [128, self.AF], F32)
        self.si32 = sb("si32", [128, 512], mybir.dt.int32)
        rem0 = nc.sbuf_bytes_remaining
        ABB = (rem0 - 256) // 2 // 128 * 128
        self.AB = ABB
        self.arb = sb("arb", [128, self.AB], BF16)
        ps = lambda n: es.enter_context(nc.psum_tensor(n, [128, 1024], F32))
        self.pA, self.pB, self.pC, self.pD = ps("pA"), ps("pB"), ps("pC"), ps("pD")

        self.sW = Sem(es, nc, "sW")
        self.sU = Sem(es, nc, "sU")
        self.sZ = Sem(es, nc, "sZ")
        self.sX = Sem(es, nc, "sX")
        self.sT = Sem(es, nc, "sT")
        self.sV = Sem(es, nc, "sV")
        self.sS = Sem(es, nc, "sS")
        self.sG = Sem(es, nc, "sG")

        self.prog = {k: [] for k in "tvsgy"}
        self._svcol = 0
        self.ab = Arena(self.arb, self.AB)
        self.build()

    def T(self, f): self.prog['t'].append(f)
    def V(self, f): self.prog['v'].append(f)
    def S(self, f): self.prog['s'].append(f)
    def G(self, f): self.prog['g'].append(f)
    def Y(self, f): self.prog['y'].append(f)

    def wait(self, eng, sem, n):
        if n > 0:
            self.prog[eng].append(lambda e: e.wait_ge(sem.h, n))

    def probe(self, name, ap):
        if name not in self.probes:
            return
        sh = list(ap.shape)
        d = self.nc.dram_tensor("dbg_" + name, sh, ap.dtype, kind="ExternalOutput").ap()
        self.dbg_tensors[name] = sh
        self.wait('y', self.sT, self.sT.n)
        self.wait('y', self.sV, self.sV.n)
        self.wait('y', self.sS, self.sS.n)
        self.Y(lambda e, dd=d, s=ap: e.dma_start(out=dd, in_=s).then_inc(self.sX.h, 16))
        pd = self.sX.inc(16)
        for eng in 'tvsg':
            self.wait(eng, self.sX, pd)

    def _mark(self, key, sem):
        lst = self.prog[key]
        f = lst[-1]
        lst[-1] = (lambda e, f=f, s=sem.h: f(e).then_inc(s, 1))
        return sem.inc()

    def tmark(self):
        return self._mark('t', self.sT)

    def conv_emit(self, n):
        for _ in range(n):
            if not self.conv_chunks:
                return
            dd, ss = self.conv_chunks.pop(0)
            d2 = dd.rearrange("(a b) -> a b", b=1024)
            s2 = ss.rearrange("(a b) -> a b", b=1024)
            self.G(lambda e, d=d2, s=s2: e.dma_start(out=d, in_=s).then_inc(self.sW.h, 16))
            self.conv_all = self.sW.inc(16)

    def vsync(self):
        self.wait('v', self.sV, self.vmark())

    def ssync(self):
        self.wait('s', self.sS, self.smark())

    def vmark(self):
        return self._mark('v', self.sV)

    def smark(self):
        return self._mark('s', self.sS)

    def svcol(self, w=1):
        c = self._svcol
        self._svcol += w
        assert self._svcol <= 80
        return self.sv[:, c:c + w]

    def xfc(self, c, t0, n, shift=False):
        o = c * TG + 1 + t0 - (1 if (shift and c >= 4) else 0)
        return self.xf[:, o:o + n]

    def xbc(self, c, t0, n, shift=False):
        o = c * TG + 1 + t0 - (1 if (shift and c >= 4) else 0)
        return self.xb[:, o:o + n]

    def xfbc(self, c, t0, n, shift=False):
        o = c * TG + 1 + t0 - (1 if (shift and c >= 4) else 0)
        return self.xfb[:, o:o + n]

    def rs_(self, c0, n): return self.arf[0:1, c0:c0 + n]
    def c0_(self, c0, n): return self.arf[0:1, 2048 + c0:2048 + c0 + n]
    def mu_(self, c0, n): return self.arf[0:1, 4096 + c0:4096 + c0 + n]

    def build(self):
        self.consts()
        self.ln1()
        if self.stage >= 1:
            for layer in range(min(2, self.stage)):
                self.ffn_layer(layer)
        if self.stage >= 3:
            self.attn()
        self.finish()
        nc = self.nc
        with nc.Block() as blk:
            @blk.tensor
            def _(e):
                for f in self.prog['t']: f(e)

            @blk.vector
            def _(e):
                for f in self.prog['v']: f(e)

            @blk.scalar
            def _(e):
                for f in self.prog['s']: f(e)

            @blk.gpsimd
            def _(e):
                for f in self.prog['g']: f(e)

            @blk.sync
            def _(e):
                for f in self.prog['y']: f(e)
        self.es.close()

    def consts(self):
        sX, sG = self.sX, self.sG
        for c in range(CH):
            dst, src = self.xfc(c, 0, T), self.d_x[c * 128:(c + 1) * 128, :]
            self.Y(lambda e, d=dst, s=src: e.dma_start(out=d, in_=s).then_inc(self.sX.h, 16))
            sX.inc(16)
        self.x_loaded = sX.n

        def colload(dst, src):
            self.Y(lambda e, d=dst, s=src: e.dma_start(
                out=d, in_=s, allow_slow_non_contiguous=True).then_inc(self.sX.h, 16))
            return sX.inc(16)

        colload(self.bqc[:], self.d_bqkv.rearrange("(j p) -> p j", p=128))
        colload(self.boc[:], self.d_bo.rearrange("(j p) -> p j", p=128))
        for l in range(2):
            colload(self.gcol[:, l * 8:l * 8 + 8], self.d_flng[l].rearrange("(j p) -> p j", p=128))
            colload(self.bcol[:, l * 8:l * 8 + 8], self.d_flnb[l].rearrange("(j p) -> p j", p=128))
            colload(self.b2col[:, l * 8:l * 8 + 8], self.d_fb2[l].rearrange("(j p) -> p j", p=128))
            colload(self.b1c[:, l * 44:l * 44 + 42], self.d_fb1[l, 0:5376].rearrange("(j p) -> p j", p=128))
            colload(self.b1c[0:42, l * 44 + 42:l * 44 + 43], self.d_fb1[l, 5376:5418].rearrange("(p o) -> p o", o=1))
            colload(self.b1c[0:42, l * 44 + 43:l * 44 + 44], self.d_fb1[l, 5418:5460].rearrange("(p o) -> p o", o=1))
            for j, dten in ((0, self.d_fqs), (1, self.d_fks)):
                colload(self.qsc[0:64, l * 4 + j:l * 4 + j + 1], dten[l].rearrange("(p o) -> p o", o=1))
                colload(self.qsc[64:128, l * 4 + j:l * 4 + j + 1], dten[l].rearrange("(p o) -> p o", o=1))
                for hp in range(4):
                    p0 = hp * 32
                    s0 = (p0 + 32) % 64
                    src32 = dten[l, s0:s0 + 32].rearrange("(p o) -> p o", o=1)
                    colload(self.qsc[p0:p0 + 32, l * 4 + 2 + j:l * 4 + 3 + j], src32)
        self.x_bias = sX.n

        iot = self.svcol()
        self.iot = iot
        self.G(lambda e: e.iota(iot, [[0, 1]], base=0, channel_multiplier=1,
                                allow_small_or_imprecise_dtypes=True).then_inc(self.sG.h, 1))
        self.g_iota = sG.inc()
        self.G(lambda e: e.memset(self.ident[:], 1.0).then_inc(self.sG.h, 1))
        m1 = sG.inc()
        self.wait('g', sG, m1)
        self.G(lambda e: e.affine_select(self.ident[:], self.ident[:], [[-1, 128]],
                                         ALU.is_equal, 0.0, base=0,
                                         channel_multiplier=1).then_inc(self.sG.h, 1))
        self.g_ident = sG.inc()
        for b in range(2):
            p0 = b * 64
            hi = self.r01[p0:p0 + 32, :]
            lo = self.r01[p0 + 32:p0 + 64, :]
            self.G(lambda e, a=hi: e.memset(a, 1.0).then_inc(self.sG.h, 1))
            self.G(lambda e, a=lo: e.memset(a, -1.0).then_inc(self.sG.h, 1))
        m2 = sG.inc(4)
        self.wait('g', sG, m2)
        for b in range(2):
            p0 = b * 64
            hi = self.r01[p0:p0 + 32, :]
            lo = self.r01[p0 + 32:p0 + 64, :]
            self.G(lambda e, a=hi, base=p0 + 32: e.affine_select(
                a, a, [[-1, 128]], ALU.is_equal, 0.0, base=base,
                channel_multiplier=1).then_inc(self.sG.h, 1))
            self.G(lambda e, a=lo, base=p0: e.affine_select(
                a, a, [[-1, 128]], ALU.is_equal, 0.0, base=base,
                channel_multiplier=1).then_inc(self.sG.h, 1))
        self.g_r01 = sG.inc(4)
        ob = self.svb[:, 0:1]
        self.ones128b = ob
        self.G(lambda e: e.memset(ob, 1.0).then_inc(self.sG.h, 1))
        self.G(lambda e: e.memset(self.onesrow[:], 1.0).then_inc(self.sG.h, 1))
        self.G(lambda e: e.memset(self.bd2[:], 0.0).then_inc(self.sG.h, 1))
        self.G(lambda e: e.memset(self.exp2m[:], 0.0).then_inc(self.sG.h, 1))
        m3 = sG.inc(4)
        self.wait('g', sG, m3)
        self.G(lambda e: e.memset(self.bd2[0:64, 0:1], 1.0).then_inc(self.sG.h, 1))
        self.G(lambda e: e.memset(self.bd2[64:128, 1:2], 1.0).then_inc(self.sG.h, 1))
        self.G(lambda e: e.memset(self.exp2m[:], 1.0).then_inc(self.sG.h, 1))
        mepre = sG.inc(3)
        self.wait('g', sG, mepre)
        self.G(lambda e: e.affine_select(self.exp2m[:], self.exp2m[:], [[1, 128]],
                                         ALU.is_ge, 0.0, base=0,
                                         channel_multiplier=-64).then_inc(self.sG.h, 1))
        me2 = sG.inc()
        self.wait('g', sG, me2)
        self.G(lambda e: e.affine_select(self.exp2m[:], self.exp2m[:], [[-1, 128]],
                                         ALU.is_ge, 0.0, base=63,
                                         channel_multiplier=64).then_inc(self.sG.h, 1))
        self.g_const = sG.inc()

        def mkmask(W, lo, hi):
            t = self.ab.buf(W)
            self.G(lambda e, a=t: e.memset(a, 0.0).then_inc(self.sG.h, 1))
            mset = sG.inc()
            self.wait('g', sG, mset)
            if lo is not None:
                self.G(lambda e, a=t, b=-lo: e.affine_select(
                    a, a, [[1, W]], ALU.is_ge, -1.0e9, base=b,
                    channel_multiplier=-1).then_inc(self.sG.h, 1))
                self.wait('g', sG, sG.inc())
            if hi is not None:
                self.G(lambda e, a=t, b=hi: e.affine_select(
                    a, a, [[-1, W]], ALU.is_ge, -1.0e9, base=b,
                    channel_multiplier=1).then_inc(self.sG.h, 1))
                self.wait('g', sG, sG.inc())
            return t
        self.am_int = mkmask(384, 1, 256)
        self.am_r0 = mkmask(128, None, 0)
        self.am_r1 = mkmask(256, None, 128)
        self.fm_int = mkmask(640, 0, 512)
        self.fm_r = [mkmask(128 * (r + 1), None, 128 * r) for r in range(4)]
        self.g_masks = sG.n

        self.wait('v', sG, self.g_iota)
        self.npib = self.svcol()
        self.V(lambda e, a=self.npib: e.memset(a, -PI))
        self.c64 = self.svcol()
        self.V(lambda e, a=self.c64: e.memset(a, 64.0))
        self.c0125 = self.svcol()
        self.V(lambda e, a=self.c0125: e.memset(a, 0.125))
        self.cln8 = self.svcol()
        self.V(lambda e, a=self.cln8: e.memset(a, float(np.log(8.0))))
        self.cm05 = self.svcol()
        self.V(lambda e, a=self.cm05: e.memset(a, -0.5))
        self.bqsc = self.svcol(8)
        self.wait('v', self.sX, self.x_bias)
        self.V(lambda e, a=self.bqsc: e.tensor_scalar(a, self.bqc[:, 0:8], 0.125, None, ALU.mult))
        self.v_vecs = self.vmark()

    def ln_rows(self, psS, psQ, t_stats):
        krow = int(os.environ.get('KBROW', '99'))
        sV, sS = self.sV, self.sS
        self.wait('v', self.sT, t_stats)
        for tc in range(4):
            mu = self.mu_(tc * 512, 512)
            rs = self.rs_(tc * 512, 512)
            c0 = self.c0_(tc * 512, 512)
            self.V(lambda e, a=mu, b=psS[tc]: e.tensor_scalar(a, b, 1.0 / D, None, ALU.mult))
            self.V(lambda e, a=rs, b=psQ[tc]: e.tensor_scalar(a, b, 1.0 / D, 1e-5, ALU.mult, ALU.add))
            self.vsync()
            self.V(lambda e, a=c0, b=mu: e.tensor_mul(a, b, b))
            self.vsync()
            self.V(lambda e, a=rs, b=c0: e.tensor_sub(a, a, b))
            self.vsync()
            self.V(lambda e, a=rs: e.reciprocal(a, a))
        v1 = self.vmark()
        if krow < 2:
            return v1
        self.wait('s', sV, v1)
        for tc in range(4):
            rs = self.rs_(tc * 512, 512)
            self.S(lambda e, a=rs: e.sqrt(a, a))
        s1 = self.smark()
        if krow < 3:
            return v1
        self.wait('v', sS, s1)
        for tc in range(4):
            mu = self.mu_(tc * 512, 512)
            rs = self.rs_(tc * 512, 512)
            c0 = self.c0_(tc * 512, 512)
            self.V(lambda e, a=c0, b=mu, r=rs: e.scalar_tensor_tensor(a, b, -1.0, r, ALU.mult, ALU.mult))
        return self.vmark()

    def ln1(self):
        sub = int(os.environ.get('KBSUB', '99'))
        sT, sV, sS, sX = self.sT, self.sV, self.sS, self.sX
        for c in range(CH):
            for buf in (self.xf, self.xb, self.xfb):
                g = buf[:, c * TG:c * TG + 1]
                self.V(lambda e, a=g: e.memset(a, 0.0))
        self.v_guard = self.vmark()

        if sub < 1:
            return
        am = self.ab.mark()
        sq = [self.ab.buf(T), self.ab.buf(T)]
        s_cast = [self.sS.n + 2 * c + 2 for c in range(CH)]
        t_sqmm = [self.sT.n + c + 1 for c in range(CH)]
        self.wait('s', sX, self.x_loaded)
        for c in range(CH):
            xbch, xfch = self.xbc(c, 0, T), self.xfc(c, 0, T)
            if c >= 2:
                self.wait('s', sT, t_sqmm[c - 2])
            self.S(lambda e, a=xbch, b=xfch: e.copy(a, b).then_inc(self.sS.h, 1))
            sS.inc()
            self.S(lambda e, a=sq[c % 2], b=xbch: e.square(a, b).then_inc(self.sS.h, 1))
            sS.inc()
        assert sS.n == s_cast[-1]

        if sub < 2:
            return
        psS = [self.pA[0:1, 0:512], self.pA[0:1, 512:1024],
               self.pB[0:1, 0:512], self.pB[0:1, 512:1024]]
        psQ = [self.pC[0:1, 0:512], self.pC[0:1, 512:1024],
               self.pD[0:1, 0:512], self.pD[0:1, 512:1024]]
        ob = self.ones128b
        self.wait('t', self.sG, self.g_const)
        for c in range(CH):
            self.wait('t', sS, s_cast[c])
            for tc in range(4):
                xbch = self.xbc(c, tc * 512, 512)
                sqch = sq[c % 2][:, tc * 512:tc * 512 + 512]
                st, sp = (c == 0), (c == CH - 1)
                self.T(lambda e, o=psS[tc], r=xbch, st=st, sp=sp:
                       e.matmul(o, ob, r, start=st, stop=sp))
                self.T(lambda e, o=psQ[tc], r=sqch, st=st, sp=sp:
                       e.matmul(o, ob, r, start=st, stop=sp))
            self.tmark()
        assert sT.n == t_sqmm[-1]
        self.ab.free(am)

        if sub < 3:
            return
        v_rows = self.ln_rows(psS, psQ, sT.n)

        if sub < 4:
            return
        orow = self.onesrow[:]
        for tc in range(4):
            rs = self.rs_(tc * 512, 512)
            c0 = self.c0_(tc * 512, 512)
            rb = self.pA[:, (tc % 2) * 512:(tc % 2) * 512 + 512]
            cb = self.pB[:, (tc % 2) * 512:(tc % 2) * 512 + 512]
            self.wait('t', sV, v_rows if tc < 2 else sV.n)
            self.T(lambda e, o=rb, r=rs: e.matmul(o, orow, r, start=True, stop=True))
            self.T(lambda e, o=cb, r=c0: e.matmul(o, orow, r, start=True, stop=True))
            t_b = self.tmark()
            self.wait('v', sT, t_b)
            for c in range(CH):
                xfch = self.xfc(c, tc * 512, 512)
                xbch = self.xbc(c, tc * 512, 512)
                self.V(lambda e, a=xfch, r=rb: e.tensor_mul(a, a, r))
                self.V(lambda e, a=xfch, r=cb: e.tensor_add(a, a, r))
                self.V(lambda e, a=xbch, b=xfch: e.tensor_copy(a, b))
            self.vmark()
        self.v_ln1 = sV.n

    def ffn_tables(self):
        sT, sV, sS = self.sT, self.sV, self.sS
        self.tab_cx = self.ab.buf(T)
        self.tab_sx = self.ab.buf(T)
        self.tab_cdx = self.ab.buf(T)
        self.tab_sdx = self.ab.buf(T)
        trow = self.arf[0:1, 0:T]
        prow = self.arf[0:1, 2048:4096]
        invf = self.arf[0:1, 4096:4224]
        lnb = self.arf[0:1, 4224:4352]
        hrow = self.arf[0:1, 4480:4608]
        xs = self.arf[:, 4864:5376]
        mm = self.arf[:, 5376:5888]
        sn = self.arf[:, 4352:4864]
        rxs = self.arf[:, 5888:6400]
        self.wait('g', self.sV, self.sV.n)
        self.wait('g', self.sT, self.sT.n)
        self.G(lambda e: e.iota(hrow, [[0, 4], [2, 32]], base=0, channel_multiplier=0,
                                allow_small_or_imprecise_dtypes=True).then_inc(self.sG.h, 1))
        self.G(lambda e: e.iota(trow, [[1, T]], base=0, channel_multiplier=0,
                                allow_small_or_imprecise_dtypes=True).then_inc(self.sG.h, 1))
        g_rows = self.sG.inc(2)
        self.wait('v', self.sG, g_rows)
        self.wait('v', sV, self.v_ln1)
        LN1E4 = float(np.log(10000.0))
        self.V(lambda e, s=-LN1E4 / 64.0: e.tensor_scalar(invf, hrow, s, None, ALU.mult))
        self.V(lambda e: e.tensor_scalar(lnb, hrow, 25.6, 1.0 / 89.6, ALU.add, ALU.mult))
        self.V(lambda e: e.tensor_scalar(prow, trow, -1024.0, 1.0 / 256.0, ALU.add, ALU.mult))
        v_pre = self.vmark()
        self.wait('s', sV, v_pre)
        self.S(lambda e: e.activation(invf, invf, ACT.Exp))
        self.S(lambda e: e.activation(lnb, lnb, ACT.Ln))
        s_pre = self.smark()
        if "arfrow0" in self.probes and "arfrow0" not in self.dbg_tensors:
            self.probe("arfrow0", self.arf[0:1, 0:4608])
        self.wait('t', sS, s_pre)
        self.wait('t', sV, sV.n)
        self.wait('t', sS, sS.n)
        for tc in range(4):
            fr = self.pA[:, 0:512]
            lx = self.pB[:, 0:512]
            trch = trow[:, tc * 512:tc * 512 + 512]
            prch = prow[:, tc * 512:tc * 512 + 512]
            if tc > 0:
                self.wait('t', sV, sV.n)
            self.T(lambda e, o=fr, r=trch: e.matmul(o, invf, r, start=True, stop=True))
            self.T(lambda e, o=lx, r=prch: e.matmul(o, lnb, r, start=True, stop=True))
            t_o = self.tmark()
            cxc = self.tab_cx[:, tc * 512:tc * 512 + 512]
            sxc = self.tab_sx[:, tc * 512:tc * 512 + 512]
            cdc = self.tab_cdx[:, tc * 512:tc * 512 + 512]
            sdc = self.tab_sdx[:, tc * 512:tc * 512 + 512]
            self.wait('v', sT, t_o)
            I2PI = 1.0 / (2 * PI)
            ki = self.si32[:]
            def rr(buf, extra_turn, extra_add):
                self.V(lambda e, a=buf, b=fr: e.tensor_scalar(a, b, I2PI, extra_turn, ALU.mult, ALU.add)
                       if extra_turn is not None else e.tensor_scalar(a, b, I2PI, None, ALU.mult))
                self.vsync()
                self.V(lambda e, a=ki, b=buf: e.tensor_copy(a, b))
                self.vsync()
                self.V(lambda e, a=buf, b=ki: e.tensor_copy(a, b))
                self.vsync()
                self.V(lambda e, a=buf, b=fr: e.scalar_tensor_tensor(a, a, -2 * PI, b, ALU.mult, ALU.add))
                if extra_add:
                    self.vsync()
                    self.V(lambda e, a=buf: e.tensor_scalar(a, a, extra_add, None, ALU.add))
                self.vsync()
                self.V(lambda e, a=rxs, b=buf: e.tensor_scalar(a, b, PI, None, ALU.is_gt))
                self.vsync()
                self.V(lambda e, a=buf, b=rxs: e.scalar_tensor_tensor(a, b, -2 * PI, a, ALU.mult, ALU.add))
                self.vsync()
                self.V(lambda e, a=buf: e.tensor_scalar(a, a, PI, -PI, ALU.min, ALU.max))
            rr(sn, None, 0.0)
            v1 = self.vmark()
            self.wait('s', sV, v1)
            self.S(lambda e, a=sn: e.activation(a, a, ACT.Sin))
            s1 = self.smark()
            self.wait('v', sS, s1)
            rr(mm, 0.25, PI / 2)
            v2 = self.vmark()
            self.wait('s', sV, v2)
            self.S(lambda e, a=mm: e.activation(a, a, ACT.Sin))
            self.S(lambda e, a=xs, b=lx: e.activation(a, b, ACT.Exp))
            s2 = self.smark()
            self.wait('v', sS, s2)
            self.V(lambda e, a=rxs, b=xs: e.reciprocal(a, b))
            self.V(lambda e, a=cxc: e.tensor_mul(a, mm, xs))
            self.V(lambda e, a=sxc: e.tensor_mul(a, sn, xs))
            self.V(lambda e, a=cdc: e.tensor_mul(a, mm, rxs))
            self.V(lambda e, a=sdc: e.tensor_mul(a, sn, rxs))
            self.vmark()
            if tc == 0 and "tb0_sn" in self.probes and "tb0_sn" not in self.dbg_tensors:
                self.probe("tb0_sn", sn)
                self.probe("tb0_mm", mm)
                self.probe("tb0_xs", xs)
                self.probe("tb0_rxs", rxs)
        self.v_tables = sV.n
        if "tcx" in self.probes and "tcx" not in self.dbg_tensors:
            self.probe("tcx", self.tab_cx)
            self.probe("tsx", self.tab_sx)
            self.probe("tcdx", self.tab_cdx)
            self.probe("tsdx", self.tab_sdx)

    def banded_attention(self, cfg):
        sT, sV, sS = self.sT, self.sV, self.sS
        tiles = cfg['tiles']
        LB = cfg['lookback']
        inst = [(r, b) for r in tiles for b in range(2)]
        n = len(inst)
        t_sa = [0] * n
        t_tp = [0] * n
        t_pv = [0] * n
        v_nmx = [0] * n
        v_b2 = [0] * n
        v_pte = [0] * n
        v_ote = {}
        s_ln = [0] * n
        s_e2 = [0] * n
        NSA, NPT, NPOT = cfg['nsa'], cfg['npt'], cfg['npot']
        NPTS = cfg.get('npts', 2)
        self.wait('t', sV, sV.n)
        self.wait('t', sS, sS.n)

        def kwW(r):
            kw = max(0, 128 * r - LB)
            return kw, 128 * r + 128 - kw

        for w in range(n + 4):
            j = w - 1
            if 0 <= j < n:
                r, b = inst[j]
                kw, W = kwW(r)
                self.wait('v', sT, t_sa[j])
                psa = cfg['psa'](j % NSA)
                nmx = cfg['nmx'](j % 2)
                mtile = cfg['mask'](r)
                self.V(lambda e, o=psa[:, 0:W], m=mtile: e.tensor_tensor(o, o, m, ALU.add))
                self.V(lambda e, a=nmx, b=psa[:, 0:W]:
                       e.reduce_max(a, b, axis=mybir.AxisListType.X, negate=True))
                v_nmx[j] = self.vmark()
            j = w - 2
            if 0 <= j < n:
                self.wait('v', sS, s_ln[j])
                nmx, lnsg, b2 = cfg['nmx'](j % 2), cfg['lnsg'](j % 2), cfg['b2'](j % 2)
                self.V(lambda e, a=b2, b=lnsg, c=nmx: e.scalar_tensor_tensor(a, b, -1.0, c, ALU.mult, ALU.add))
                v_b2[j] = self.vmark()
            j = w - 3
            if 0 <= j < n:
                r, b = inst[j]
                kw, W = kwW(r)
                self.wait('v', sT, t_tp[j])
                if j >= NPTS:
                    self.wait('v', sT, t_pv[j - NPTS])
                ppt = cfg['ppt'](j % NPT)
                pts = cfg['pts'](j % NPTS)
                self.V(lambda e, a=pts[:, 0:W], b=ppt[:, 0:W]: e.tensor_copy(a, b))
                v_pte[j] = self.vmark()
            j = w - 4
            if 0 <= j < n:
                r, b = inst[j]
                if b == 1:
                    self.wait('v', sT, t_pv[j])
                    slot = (j // 2) % NPOT
                    cfg['otevac'](r, cfg['pot'](slot))
                    v_ote[r] = self.vmark()
            j = w - 1
            if 0 <= j < n:
                r, b = inst[j]
                kw, W = kwW(r)
                self.wait('s', sV, v_nmx[j])
                if j >= 2:
                    self.wait('s', sT, t_tp[j - 2])
                psa = cfg['psa'](j % NSA)
                nmx, sg, lnsg = cfg['nmx'](j % 2), cfg['sg'](j % 2), cfg['lnsg'](j % 2)
                self.S(lambda e, o=cfg['garb'](j)[:, 0:W], i=psa[:, 0:W], b=nmx, a=sg:
                       e.activation(o, i, ACT.Exp, bias=b, accum_out=a))
                self.ssync()
                self.S(lambda e, a=lnsg, b=sg: e.activation(a, b, ACT.Ln))
                s_ln[j] = self.smark()
            j = w - 2
            if 0 <= j < n:
                r, b = inst[j]
                kw, W = kwW(r)
                self.wait('s', sV, v_b2[j])
                if j >= 2:
                    self.wait('s', sT, t_tp[j - 2])
                psa = cfg['psa'](j % NSA)
                b2 = cfg['b2'](j % 2)
                pr = cfg['probs'](j % 2)
                self.S(lambda e, o=pr[:, 0:W], i=psa[:, 0:W], b=b2:
                       e.activation(o, i, ACT.Exp, bias=b).then_inc(self.sS.h, 1))
                s_e2[j] = sS.inc()
            if w < n:
                r, b = inst[w]
                kw, W = kwW(r)
                if w >= NSA:
                    self.wait('t', sS, s_e2[w - NSA])
                psa = cfg['psa'](w % NSA)
                q = cfg['qbuf'](b, r)
                k = cfg['kbuf'](b, kw, W)
                for m0 in range(0, W, 512):
                    m1 = min(W, m0 + 512)
                    self.T(lambda e, o=psa[:, m0:m1], q=q, kk=k[:, m0:m1]:
                           e.matmul(o, q, kk, start=True, stop=True))
                t_sa[w] = self.tmark()
            j = w - 2
            if 0 <= j < n:
                r, b = inst[j]
                kw, W = kwW(r)
                self.wait('t', sS, s_e2[j])
                if j >= NPT:
                    self.wait('t', sV, v_pte[j - NPT])
                ppt = cfg['ppt'](j % NPT)
                pr = cfg['probs'](j % 2)
                for kb in range(W // 128):
                    self.T(lambda e, o=ppt[:, kb * 128:kb * 128 + 128],
                           l=pr[:, kb * 128:kb * 128 + 128]:
                           e.matmul(o, l, self.ident[:], start=True, stop=True))
                t_tp[j] = self.tmark()
            j = w - 3
            if 0 <= j < n:
                r, b = inst[j]
                kw, W = kwW(r)
                self.wait('t', sV, v_pte[j])
                slot = (j // 2) % NPOT
                if b == 0 and j >= 2 * NPOT:
                    self.wait('t', sV, v_ote[inst[j - 2 * NPOT][0]])
                pot = cfg['pot'](slot)
                pts = cfg['pts'](j % NPTS)
                nkb = W // 128
                for kb in range(nkb):
                    tt = kw // 128 + kb
                    self.T(lambda e, o=pot[64 * b:64 * b + 64, 0:128],
                           l=cfg['vblk'](tt, b), rr=pts[:, kb * 128:kb * 128 + 128],
                           st=(kb == 0), sp=(kb == nkb - 1), b=b:
                           e.matmul(o, l, rr, start=st, stop=sp, tile_position=(0, 64 * b)))
                t_pv[j] = self.tmark()

    def ffn_layer(self, layer):
        kbff = int(os.environ.get('KBFF', '99'))
        sT, sV, sS, sW = self.sT, self.sV, self.sS, self.sW
        srcc = (lambda c, t0, nn, sh=True: self.xbc(c, t0, nn, sh)) if layer == 0 else \
               (lambda c, t0, nn, sh=True: self.xfbc(c, t0, nn, sh))
        qs_col = self.qsc[:, layer * 4 + 0:layer * 4 + 1]
        ks_col = self.qsc[:, layer * 4 + 1:layer * 4 + 2]
        qsp_col = self.qsc[:, layer * 4 + 2:layer * 4 + 3]
        ksp_col = self.qsc[:, layer * 4 + 3:layer * 4 + 4]

        self.wait('v', self.sG, self.g_r01)
        self.wait('v', self.sX, self.x_bias)
        self.V(lambda e: e.tensor_scalar(self.rotq[:], self.r01[:], qs_col, None, ALU.mult))
        self.V(lambda e: e.tensor_scalar(self.rotk[:], self.r01[:], ks_col, None, ALU.mult))
        v_rot = self.vmark()

        am0 = self.ab.mark()
        self.ffn_tables()
        qb = self.ab.buf(1024)
        kbuf = self.ab.buf(1536)
        vbuf = self.ab.buf(1536)
        wsl0 = self.ab.buf(3 * 1024)
        wsl = [wsl0, wsl0]
        oTf = self.ab.buf(4 * 1024)
        prb = [self.ab.buf(640) for _ in range(2)]
        ptsb = [self.ab.buf(640) for _ in range(2)]
        mxc = [self.svcol() for _ in range(2)]
        nmxc = [self.svcol() for _ in range(2)]
        sgc = [self.svcol() for _ in range(2)]
        lnsgc = [self.svcol() for _ in range(2)]
        b2c = [self.svcol() for _ in range(2)]

        w_slot_free = [0, 0]
        wq_ready = [0, 0, 0, 0]
        for half in range(2):
            ht0 = half * 1024
            kv0 = max(0, ht0 - 512)
            klen = ht0 + 1024 - kv0

            def issue_qkvw(pf):
                slot = 0
                self.wait('g', sT, w_slot_free[slot])
                dst = wsl[slot][:, 0:3072]
                src = self.d_fwqkv2[layer, pf]
                self.G(lambda e, d=dst, s=src: e.dma_start(out=d, in_=s).then_inc(self.sW.h, 16))
                sW.inc(16)
                wq_ready[pf] = sW.n

            issue_qkvw(0)
            for pf in range(4):
                slot = 0
                self.wait('t', sW, wq_ready[pf])
                self.wait('t', sV, sV.n)
                self.wait('t', sS, sS.n)
                if layer == 1:
                    self.wait('t', sS, self.s_xfb_next)
                pq2 = [self.pD[:, 0:512], self.pD[:, 512:1024]]
                ev = []
                ci = 0
                for part, dstbuf, t0r in ((0, qb, ht0), (1, kbuf, kv0), (2, vbuf, kv0)):
                    wsrc = wsl[slot][:, part * 1024:part * 1024 + 1024]
                    if part < 2:
                        cl = [(a0, min(512, ht0 + 1024 - a0)) for a0 in range(t0r, ht0 + 1024, 512)]
                    else:
                        cl = [(tt * 128, 128) for tt in range(kv0 // 128, (ht0 + 1024) // 128)]
                    for (a0, nn) in cl:
                        ps = pq2[ci % 2]
                        if ci >= 2:
                            eng, mk = ev[ci - 2]
                            self.wait('t', eng, mk)
                        if part < 2:
                            for kc in range(CH):
                                self.T(lambda e, o=ps[:, 0:nn], l=wsrc[:, kc * 128:kc * 128 + 128],
                                       r=srcc(kc, a0, nn), st=(kc == 0), sp=(kc == CH - 1):
                                       e.matmul(o, l, r, start=st, stop=sp))
                        else:
                            for kc in range(CH):
                                self.T(lambda e, o=ps[:, 0:128], l=srcc(kc, a0, 128),
                                       r=wsrc[:, kc * 128:kc * 128 + 128], st=(kc == 0), sp=(kc == CH - 1):
                                       e.matmul(o, l, r, start=st, stop=sp))
                        t_q = self.tmark()
                        dst = dstbuf[:, a0 - t0r:a0 - t0r + nn]
                        if ci % 2 == 0:
                            self.wait('v', sT, t_q)
                            self.V(lambda e, a=dst, b=ps[:, 0:nn]: e.tensor_copy(a, b))
                            ev.append((sV, self.vmark()))
                        else:
                            self.wait('s', sT, t_q)
                            self.S(lambda e, a=dst, b=ps[:, 0:nn]: e.copy(a, b).then_inc(self.sS.h, 1))
                            ev.append((sS, sS.inc()))
                        ci += 1
                v_qkv = sV.n
                w_slot_free[slot] = sT.n
                if pf < 3:
                    issue_qkvw(pf + 1)
                if layer == 0 and USE_SCRATCH:
                    self.conv_emit(2)

                if kbff < 1:
                    continue
                kbl2 = int(os.environ.get('KBL2', '99'))
                chunks = []
                for which, buf, t0r, rot, tcx, tsx, biasc in (
                        (0, qb, ht0, self.rotq, self.tab_cx, self.tab_sx, self.cln8),
                        (1, kbuf, kv0, self.rotk, self.tab_cdx, self.tab_sdx, None)):
                    nch = (ht0 + 1024 - t0r + 511) // 512
                    for i in range(nch):
                        chunks.append((which, buf, t0r, rot, tcx, tsx, biasc, i))
                nck = len(chunks)
                qsq2 = [ptsb[0][:, 0:512], ptsb[1][:, 0:512]]
                s_sq = [0] * nck
                t_st = [0] * nck
                v_cl = [0] * nck
                s_isd = [0] * nck

                def rrof(idx):
                    wh, _, _, _, _, _, _, i = chunks[idx]
                    c0 = (0 if wh == 0 else 1024) + i * 512
                    return self.arf[0:2, c0:c0 + 512]

                self.wait('s', sV, v_qkv)
                for idx, (which, buf, t0r, rot, tcx, tsx, biasc, i) in enumerate(chunks):
                    ch = buf[:, i * 512:i * 512 + 512]
                    pr = (self.pC[0:2, 0:512], self.pC[0:2, 512:1024])[idx % 2]
                    rr = rrof(idx)
                    if idx >= 2:
                        self.wait('s', sT, t_st[idx - 2])
                    self.S(lambda e, a=qsq2[idx % 2], b=ch: e.square(a, b).then_inc(self.sS.h, 1))
                    s_sq[idx] = sS.inc()
                    self.wait('t', sS, s_sq[idx])
                    if idx >= 2:
                        self.wait('t', sV, v_cl[idx - 2])
                    self.T(lambda e, o=pr, r=qsq2[idx % 2]: e.matmul(o, self.bd2[:], r, start=True, stop=True))
                    t_st[idx] = self.tmark()
                    self.wait('v', sT, t_st[idx])
                    self.V(lambda e, a=rr, b=pr: e.tensor_scalar(a, b, 1e-24, None, ALU.max))
                    v_cl[idx] = self.vmark()
                for idx in range(nck):
                    if kbl2 < 2:
                        break
                    biasc = chunks[idx][6]
                    rr = rrof(idx)
                    self.wait('s', sV, v_cl[idx])
                    self.S(lambda e, a=rr: e.activation(a, a, ACT.Ln))
                    self.ssync()
                    if biasc is not None:
                        self.S(lambda e, a=rr, b=biasc: e.activation(
                            a, a, ACT.Exp, scale=self.cm05[0:2, :], bias=b[0:2, :]).then_inc(self.sS.h, 1))
                    else:
                        self.S(lambda e, a=rr: e.activation(
                            a, a, ACT.Exp, scale=self.cm05[0:2, :]).then_inc(self.sS.h, 1))
                    s_isd[idx] = sS.inc()
                t_rot = [0] * nck
                v_ap = [0] * nck
                for idx in range(nck):
                    if kbl2 < 3:
                        break
                    (which, buf, t0r, rot, tcx, tsx, biasc, i) = chunks[idx]
                    a0 = t0r + i * 512
                    ch = buf[:, i * 512:i * 512 + 512]
                    rr = rrof(idx)
                    prot = (self.pA[:, 0:512], self.pA[:, 512:1024])[idx % 2]
                    prb_ = (self.pB[:, 0:512], self.pB[:, 512:1024])[idx % 2]
                    self.wait('t', sS, s_isd[idx])
                    if idx == 0:
                        self.wait('t', sV, v_rot)
                    if idx >= 2:
                        self.wait('t', sV, v_ap[idx - 2])
                    self.T(lambda e, o=prot, l=rot, r=ch: e.matmul(o, l[:], r, start=True, stop=True))
                    self.T(lambda e, o=prb_, r=rr: e.matmul(o, self.exp2m[:], r, start=True, stop=True))
                    t_rot[idx] = self.tmark()
                    t1f, t2f = ((self.arf[:, 4352:4864], self.arf[:, 4864:5376]),
                                (self.arf[:, 5376:5888], self.arf[:, 5888:6400]))[idx % 2]
                    scol = qs_col if which == 0 else ks_col
                    self.wait('v', sT, t_rot[idx])
                    self.V(lambda e, a=t1f, b=ch, s=scol, cc=tcx[:, a0:a0 + 512]:
                           e.scalar_tensor_tensor(a, b, s, cc, ALU.mult, ALU.mult))
                    self.V(lambda e, a=t2f, b=prot, cc=tsx[:, a0:a0 + 512]:
                           e.tensor_mul(a, b, cc))
                    self.V(lambda e, a=t1f, b=t2f: e.tensor_add(a, a, b))
                    self.V(lambda e, a=ch, b=t1f, cc=prb_: e.tensor_mul(a, b, cc))
                    v_ap[idx] = self.vmark()

                if half == 0 and pf == 0:
                    self.probe(f"qb{layer}", qb)
                    self.probe(f"kb{layer}", kbuf[:, 0:1024])
                    self.probe(f"vb{layer}", vbuf[:, 0:1024])
                def mk_ote(pf, half):
                    def f(r, pot):
                        dst = oTf[:, pf * 1024 + (r - half * 8) * 128: pf * 1024 + (r - half * 8) * 128 + 128]
                        self.V(lambda e, a=dst, b=pot[:, 0:128]: e.tensor_copy(a, b))
                    return f
                cfg = dict(
                    tiles=list(range(half * 8, half * 8 + 8)),
                    lookback=512,
                    qbuf=lambda b, r: qb[64 * b:64 * b + 64, (r * 128 - ht0):(r * 128 - ht0) + 128],
                    kbuf=lambda b, kw, W: kbuf[64 * b:64 * b + 64, kw - kv0:kw - kv0 + W],
                    vblk=lambda tt, b: vbuf[:, (tt * 128 - kv0) + 64 * b:(tt * 128 - kv0) + 64 * b + 64],
                    mask=lambda r: (self.fm_r[r] if r < 4 else self.fm_int),
                    psa=lambda s: (self.pA, self.pB)[s][:, 0:640],
                    nsa=2,
                    ppt=lambda s: self.pC[:, 0:640],
                    npt=1,
                    pot=lambda s: self.pD[:, 0:128],
                    npot=1,
                    probs=lambda s: prb[s], pts=lambda s: ptsb[s],
                    garb=lambda j: prb[j % 2],
                    mx=lambda s: mxc[s], nmx=lambda s: nmxc[s], sg=lambda s: sgc[s],
                    lnsg=lambda s: lnsgc[s], b2=lambda s: b2c[s],
                    otevac=mk_ote(pf, half),
                )
                if kbff >= 2:
                    self.banded_attention(cfg)
                if half == 0 and pf == 0:
                    self.wait('y', self.sV, self.sV.n)
                    self.probe(f"oTf{layer}", oTf[:, 0:1024])

            if half == 0:
                self.wait('y', self.sV, self.sV.n)
                self.probe(f"oTfall{layer}", oTf[:])
            if kbff < 3:
                self.ab.free(am0 if False else self.ab.mark())
                continue
            am1 = self.ab.mark()
            wo_sl = [wsl[0][:, 0:512], wsl[0][:, 512:1024]]
            wo_free = [max(w_slot_free), max(w_slot_free)]
            self.wait('t', sV, sV.n)
            self.wait('t', sS, sS.n)
            for fc in range(CH):
                ws = fc % 2
                self.wait('g', sT, wo_free[ws])
                dstw = wo_sl[ws]
                srcw = self.d_fwo2[layer, fc]
                self.G(lambda e, d=dstw, s=srcw: e.dma_start(out=d, in_=s).then_inc(self.sW.h, 16))
                w_wo = sW.inc(16)
                self.wait('t', sW, w_wo)
                for tc in range(2):
                    po = (self.pA, self.pB)[tc % 2][:, 0:512]
                    for kc in range(4):
                        self.T(lambda e, o=po, l=wo_sl[ws][:, kc * 128:kc * 128 + 128],
                               r=oTf[:, kc * 1024 + tc * 512:kc * 1024 + tc * 512 + 512],
                               st=(kc == 0), sp=(kc == 3): e.matmul(o, l, r, start=st, stop=sp))
                    t_wo = self.tmark()
                    self.wait('v', sT, t_wo)
                    xfch = self.xfc(fc, ht0 + tc * 512, 512)
                    self.V(lambda e, a=xfch, b=po: e.tensor_add(a, a, b))
                    v_wo = self.vmark()
                    self.wait('t', sV, v_wo)
                wo_free[ws] = sT.n
            self.ab.free(am1)
            w_slot_free[0] = max(w_slot_free[0], sT.n)
        v_wo_all = sV.n
        self.ab.free(am0)

        self.probe(f"xfwo{layer}", self.xf[:])
        if kbff < 4:
            self.v_ffn = sV.n
            self.s_ffn = sS.n
            if layer == 0:
                self.s_xfb_next = 0
            return
        self.wait('s', sV, v_wo_all)
        for c in range(CH):
            self.S(lambda e, a=self.xfbc(c, 0, T), b=self.xfc(c, 0, T): e.copy(a, b))
        s_xfb = self.smark()

        am2 = self.ab.mark()
        sq = [self.ab.buf(T), self.ab.buf(T)]
        s_sq2 = [sS.n + c + 1 for c in range(CH)]
        t_mm2 = [sT.n + c + 1 for c in range(CH)]
        for c in range(CH):
            if c >= 2:
                self.wait('s', sT, t_mm2[c - 2])
            xfsh = self.xfc(c, 0, T, True)
            self.S(lambda e, a=sq[c % 2], b=xfsh: e.square(a, b).then_inc(self.sS.h, 1))
            sS.inc()
        assert sS.n == s_sq2[-1]
        psS = [self.pA[0:1, 0:512], self.pA[0:1, 512:1024],
               self.pB[0:1, 0:512], self.pB[0:1, 512:1024]]
        psQ = [self.pC[0:1, 0:512], self.pC[0:1, 512:1024],
               self.pD[0:1, 0:512], self.pD[0:1, 512:1024]]
        self.wait('t', sS, s_xfb)
        for c in range(CH):
            self.wait('t', sS, s_sq2[c])
            for tc in range(4):
                xbs = self.xfbc(c, tc * 512, 512, True)
                sqch = sq[c % 2][:, tc * 512:tc * 512 + 512]
                st, sp = (c == 0), (c == CH - 1)
                self.T(lambda e, o=psS[tc], r=xbs, st=st, sp=sp:
                       e.matmul(o, self.ones128b, r, start=st, stop=sp))
                self.T(lambda e, o=psQ[tc], r=sqch, st=st, sp=sp:
                       e.matmul(o, self.ones128b, r, start=st, stop=sp))
            self.tmark()
        assert sT.n == t_mm2[-1]
        self.ab.free(am2)
        v_rows = self.ln_rows(psS, psQ, sT.n)

        snap = self.svcol(12)
        self.wait('v', self.sV, sV.n)
        for c in range(4, CH):
            for k in range(1, 4):
                dst = snap[:, (c - 4) * 3 + (k - 1):(c - 4) * 3 + k]
                srcb = self.xfc(c, 512 * k - 1, 1)
                self.V(lambda e, a=dst, b=srcb: e.tensor_copy(a, b))
        v_snap = self.vmark()
        self.probe(f"snap{layer}", snap)

        self.wait('y', sW, self.conv_all)
        self.wait('g', sW, self.conv_all)
        am3 = self.ab.mark()
        region = self.ab.buf(8192)
        h2n = region[:, 0:4096]
        w1sl = [region[:, 4096:6144], region[:, 6144:8192]]
        w2sl = [region[:, 0:2816], region[:, 2816:5632]]
        actb = self.ab.buf(22 * 512)
        gelb = [self.ab.buf(512), self.ab.buf(512)]
        gc = self.gcol[:, layer * 8:layer * 8 + 8]
        bc = self.bcol[:, layer * 8:layer * 8 + 8]
        b2cc = self.b2col[:, layer * 8:layer * 8 + 8]
        w1_free = [0, 0]
        w2_free = [0, 0]
        v_act = {}
        t_u_last, t_w2_last = 0, 0
        for tc in range(4):
            rs = self.rs_(tc * 512, 512)
            c0 = self.c0_(tc * 512, 512)
            rb = self.pA[:, 0:512]
            cb = self.pA[:, 512:1024]
            self.wait('t', sV, v_rows if tc == 0 else sV.n)
            self.T(lambda e, o=rb, r=rs: e.matmul(o, self.onesrow[:], r, start=True, stop=True))
            self.T(lambda e, o=cb, r=c0: e.matmul(o, self.onesrow[:], r, start=True, stop=True))
            t_bb = self.tmark()
            self.wait('v', sT, t_bb)
            tmpf2 = [self.arf[:, 4352:4864], self.arf[:, 4864:5376]]
            if tc > 0:
                self.wait('s', sT, t_u_last)
                self.wait('v', sT, t_w2_last)
            s_h2 = 0
            s_h2s = [0] * CH
            for c in range(CH):
                tmpf = tmpf2[c % 2]
                xfsh = self.xfc(c, tc * 512, 512, True)
                if c >= 2:
                    self.wait('v', sS, s_h2s[c - 2])
                self.V(lambda e, a=tmpf, b=xfsh, r=rb: e.tensor_mul(a, b, r))
                self.V(lambda e, a=tmpf, r=cb: e.tensor_add(a, a, r))
                if tc >= 1 and c >= 4:
                    sc = snap[:, (c - 4) * 3 + (tc - 1):(c - 4) * 3 + tc]
                    self.vsync()
                    self.V(lambda e, a=tmpf[:, 0:1], b=sc, r=rb[:, 0:1]: e.tensor_mul(a, b, r))
                    self.vsync()
                    self.V(lambda e, a=tmpf[:, 0:1], r=cb[:, 0:1]: e.tensor_add(a, a, r))
                v_t = self.vmark()
                self.wait('s', sV, v_t)
                self.S(lambda e, a=h2n[:, c * 512:c * 512 + 512], b=tmpf,
                       g=gc[:, c:c + 1], bb=bc[:, c:c + 1]:
                       e.activation(a, b, ACT.Identity, bias=bb, scale=g).then_inc(self.sS.h, 1))
                s_h2 = sS.inc()
                s_h2s[c] = s_h2
            s_h2n = s_h2
            if tc == 0:
                self.probe(f"h2n{layer}", h2n)
            if tc == 1:
                self.probe(f"h2nb{layer}", h2n)
            for ci in range(22):
                wslot = ci % 2
                wdt = min(128, 2730 - 128 * ci)
                if USE_SCRATCH:
                    qe, qs = ('g', self.sZ) if ci % 2 == 0 else ('y', self.sU)
                    emit = self.G if ci % 2 == 0 else self.Y
                    w1src = self.d_fw1b
                else:
                    qe, qs, emit, w1src = 'g', sW, self.G, self.d_fw1s
                self.wait(qe, sT, w1_free[wslot])
                if wdt == 128:
                    dstw = w1sl[wslot][:, 0:2048]
                    s_ = w1src[layer, 262144 * ci:262144 * ci + 262144].rearrange(
                        "(p f) -> p f", p=128)
                    emit(lambda e, d=dstw, s=s_, h=qs.h:
                         e.dma_start(out=d, in_=s).then_inc(h, 16))
                    qs.inc(16)
                else:
                    for ag in range(2):
                        o = 262144 * ci + ag * 1024 * wdt
                        dstw = w1sl[wslot][:, ag * 1024:ag * 1024 + 1024].rearrange(
                            "p (k f) -> p k f", f=128)[:, :, 0:wdt]
                        s_ = w1src[layer, o:o + 1024 * wdt].rearrange(
                            "(p k f) -> p k f", p=128, k=8)
                        emit(lambda e, d=dstw, s=s_, h=qs.h:
                             e.dma_start(out=d, in_=s).then_inc(h, 16))
                        qs.inc(16)
                w1_sem, w1_ready = qs, qs.n
                ppair = (self.pB, self.pC)[ci % 2]
                pa = ppair[0:wdt, 0:512]
                pg = ppair[0:wdt, 512:1024]
                self.wait('t', w1_sem, w1_ready)
                if ci == 0:
                    self.wait('t', sS, s_h2n)
                if ci >= 2:
                    self.wait('t', sV, v_act[ci - 2])
                for kc in range(CH):
                    self.T(lambda e, o=pa, l=w1sl[wslot][:, kc * 128:kc * 128 + wdt],
                           r=h2n[:, kc * 512:kc * 512 + 512], st=(kc == 0), sp=(kc == CH - 1):
                           e.matmul(o, l, r, start=st, stop=sp))
                for kc in range(CH):
                    self.T(lambda e, o=pg, l=w1sl[wslot][:, 1024 + kc * 128:1024 + kc * 128 + wdt],
                           r=h2n[:, kc * 512:kc * 512 + 512], st=(kc == 0), sp=(kc == CH - 1):
                           e.matmul(o, l, r, start=st, stop=sp))
                t_u = self.tmark()
                w1_free[wslot] = t_u
                t_u_last = t_u
                b1a = self.b1c[:, layer * 44 + 2 * ci:layer * 44 + 2 * ci + 1]
                b1g = self.b1c[:, layer * 44 + 2 * ci + 1:layer * 44 + 2 * ci + 2]
                self.wait('s', sT, t_u)
                if ci >= 2:
                    self.wait('s', sV, v_act[ci - 2])
                gbuf = gelb[ci % 2]
                self.S(lambda e, o=gbuf[0:wdt, 0:512], i=pg, b=b1g[0:wdt, :]:
                       e.activation(o, i, ACT.Gelu, bias=b).then_inc(self.sS.h, 1))
                s_gel = sS.inc()
                self.wait('v', sS, s_gel)
                dst = actb[0:wdt, ci * 512:ci * 512 + 512]
                self.V(lambda e, a=dst, b=pa, s=b1a[0:wdt, :], g=gbuf[0:wdt, 0:512]:
                       e.scalar_tensor_tensor(a, b, s, g, ALU.add, ALU.mult))
                v_act[ci] = self.vmark()
            v_act_all = sV.n
            if tc == 0:
                self.probe(f"act{layer}", actb)
            if tc == 1:
                self.probe(f"actb{layer}", actb)
            for fc in range(CH):
                wslot = fc % 2
                if USE_SCRATCH:
                    qe2, qs2 = ('g', self.sZ) if fc % 2 == 0 else ('y', self.sU)
                    emit2 = self.G if fc % 2 == 0 else self.Y
                    w2src = self.d_fw2b
                else:
                    qe2, qs2, emit2, w2src = 'g', sW, self.G, self.d_fw2s
                self.wait(qe2, sT, w2_free[wslot])
                self.wait(qe2, sT, t_u_last)
                off = fc * 349440
                src1 = w2src[layer, off:off + 344064].rearrange(
                    "(p f) -> p f", p=128)
                emit2(lambda e, d=w2sl[wslot][:, 0:2688], s=src1, h=qs2.h:
                      e.dma_start(out=d, in_=s).then_inc(h, 16))
                qs2.inc(16)
                src2 = w2src[layer, off + 344064:off + 349440].rearrange(
                    "(p f) -> p f", p=42)
                emit2(lambda e, d=w2sl[wslot][0:42, 2688:2816], s=src2, h=qs2.h:
                      e.dma_start(out=d, in_=s).then_inc(h, 16))
                w2_sem, w2_ready = qs2, qs2.inc(16)
                po = (self.pC, self.pD)[fc % 2][:, 0:512]
                if fc < 2:
                    self.wait('t', sV, v_act_all)
                self.wait('t', w2_sem, w2_ready)
                for kc in range(22):
                    wdt = min(128, 2730 - 128 * kc)
                    self.T(lambda e, o=po, l=w2sl[wslot][0:wdt, kc * 128:kc * 128 + 128],
                           r=actb[0:wdt, kc * 512:kc * 512 + 512], st=(kc == 0), sp=(kc == 21):
                           e.matmul(o, l, r, start=st, stop=sp))
                t_w2 = self.tmark()
                w2_free[wslot] = t_w2
                t_w2_last = t_w2
                self.wait('v', sT, t_w2)
                xfch = self.xfc(fc, tc * 512, 512)
                self.V(lambda e, a=xfch, b=po, s=b2cc[:, fc:fc + 1]:
                       e.scalar_tensor_tensor(a, b, s, a, ALU.add, ALU.add))
                self.vmark()
            w1_free[0] = w1_free[1] = t_w2_last
        self.ab.free(am3)
        if layer == 0:
            self.wait('s', sV, sV.n)
            self.wait('s', sT, sT.n)
            for c in range(CH):
                self.S(lambda e, a=self.xfbc(c, 0, T), b=self.xfc(c, 0, T): e.copy(a, b))
            self.s_xfb_next = self.smark()
        self.v_ffn = sV.n
        self.s_ffn = sS.n

    def attn(self):
        sT, sV, sS, sW = self.sT, self.sV, self.sS, self.sW
        am0 = self.ab.mark()
        qb = self.ab.buf(1024)
        kbuf = self.ab.buf(1280)
        vbuf = self.ab.buf(1280)
        wsl = [self.ab.buf(3 * 1024) for _ in range(2)]
        oT = self.ab.buf(CH * 1024)
        wot_sl = [self.ab.buf(1024), self.ab.buf(1024)]
        prb = [self.ab.buf(384) for _ in range(2)]
        ptsb = [self.ab.buf(384) for _ in range(2)]
        garb = self.ab.buf(384)
        mxc = [self.svcol() for _ in range(2)]
        nmxc = [self.svcol() for _ in range(2)]
        sgc = [self.svcol() for _ in range(2)]
        lnsgc = [self.svcol() for _ in range(2)]
        b2c = [self.svcol() for _ in range(2)]

        self.wait('g', sT, sT.n)
        self.wait('g', sV, sV.n)
        self.wait('g', sS, sS.n)
        w_slot_free = [0, 0]
        wot_free = [0, 0]
        wq_ready = [0] * CH
        for half in range(2):
            ht0 = half * 1024
            kv0 = max(0, ht0 - 256)

            def issue_aw(p):
                slot = p % 2
                self.wait('g', sT, w_slot_free[slot])
                dst = wsl[slot][:, 0:3072]
                src = self.d_wqkv2[p]
                self.G(lambda e, d=dst, s=src: e.dma_start(out=d, in_=s).then_inc(self.sW.h, 16))
                sW.inc(16)
                wq_ready[p] = sW.n

            issue_aw(0)
            for p in range(CH):
                slot = p % 2
                self.wait('t', sW, wq_ready[p])
                self.wait('t', sV, sV.n)
                self.wait('t', sS, sS.n)
                pq2 = [self.pD[:, 0:512], self.pD[:, 512:1024]]
                ev = []
                ci = 0
                for part, dstbuf, t0r in ((0, qb, ht0), (1, kbuf, kv0), (2, vbuf, kv0)):
                    wsrc = wsl[slot][:, part * 1024:part * 1024 + 1024]
                    if part < 2:
                        cl = [(a0, min(512, ht0 + 1024 - a0)) for a0 in range(t0r, ht0 + 1024, 512)]
                    else:
                        cl = [(tt * 128, 128) for tt in range(kv0 // 128, (ht0 + 1024) // 128)]
                    for (a0, nn) in cl:
                        ps = pq2[ci % 2]
                        if ci >= 2:
                            eng, mk = ev[ci - 2]
                            self.wait('t', eng, mk)
                        if part < 2:
                            for kc in range(CH):
                                self.T(lambda e, o=ps[:, 0:nn], l=wsrc[:, kc * 128:kc * 128 + 128],
                                       r=self.xbc(kc, a0, nn), st=(kc == 0), sp=(kc == CH - 1):
                                       e.matmul(o, l, r, start=st, stop=sp))
                        else:
                            for kc in range(CH):
                                self.T(lambda e, o=ps[:, 0:128], l=self.xbc(kc, a0, 128),
                                       r=wsrc[:, kc * 128:kc * 128 + 128], st=(kc == 0), sp=(kc == CH - 1):
                                       e.matmul(o, l, r, start=st, stop=sp))
                        t_q = self.tmark()
                        dst = dstbuf[:, a0 - t0r:a0 - t0r + nn]
                        if ci % 2 == 0:
                            self.wait('v', sT, t_q)
                            if part == 0:
                                bcl = self.bqc[:, p:p + 1]
                                self.V(lambda e, a=dst, b=ps[:, 0:nn], s=bcl:
                                       e.tensor_scalar(a, b, s, 0.125, ALU.add, ALU.mult))
                            elif part == 1:
                                bcl = self.bqc[:, 8 + p:8 + p + 1]
                                self.V(lambda e, a=dst, b=ps[:, 0:nn], s=bcl:
                                       e.tensor_scalar(a, b, s, None, ALU.add))
                            else:
                                self.V(lambda e, a=dst, b=ps[:, 0:128]: e.tensor_copy(a, b))
                            ev.append((sV, self.vmark()))
                        else:
                            self.wait('s', sT, t_q)
                            if part == 0:
                                bcl = self.bqsc[:, p:p + 1]
                                self.S(lambda e, a=dst, b=ps[:, 0:nn], s=bcl:
                                       e.activation(a, b, ACT.Identity, bias=s,
                                                    scale=self.c0125).then_inc(self.sS.h, 1))
                            elif part == 1:
                                bcl = self.bqc[:, 8 + p:8 + p + 1]
                                self.S(lambda e, a=dst, b=ps[:, 0:nn], s=bcl:
                                       e.activation(a, b, ACT.Identity, bias=s).then_inc(self.sS.h, 1))
                            else:
                                self.S(lambda e, a=dst, b=ps[:, 0:128]: e.copy(a, b).then_inc(self.sS.h, 1))
                            ev.append((sS, sS.inc()))
                        ci += 1
                w_slot_free[slot] = sT.n
                if p < CH - 1:
                    issue_aw(p + 1)
                else:
                    self.wait('t', sV, sV.n)
                    self.wait('t', sS, sS.n)

                if half == 0 and p == 0:
                    self.probe("aqb", qb)
                    self.probe("akb", kbuf[:, 0:1024])
                    self.probe("avb", vbuf[:, 0:1024])
                def mk_ote(p, half):
                    def f(r, pot):
                        dst = oT[:, p * 1024 + (r - half * 8) * 128: p * 1024 + (r - half * 8) * 128 + 128]
                        bv = self.bqc[:, 16 + p:16 + p + 1]
                        self.V(lambda e, a=dst, b=pot[:, 0:128], s=bv:
                               e.tensor_scalar(a, b, s, None, ALU.add))
                    return f
                cfg = dict(
                    tiles=list(range(half * 8, half * 8 + 8)),
                    lookback=256,
                    qbuf=lambda b, r: qb[64 * b:64 * b + 64, (r * 128 - ht0):(r * 128 - ht0) + 128],
                    kbuf=lambda b, kw, W: kbuf[64 * b:64 * b + 64, kw - kv0:kw - kv0 + W],
                    vblk=lambda tt, b: vbuf[:, (tt * 128 - kv0) + 64 * b:(tt * 128 - kv0) + 64 * b + 64],
                    mask=lambda r: (self.am_r0 if r == 0 else (self.am_r1 if r == 1 else self.am_int)),
                    psa=lambda s: (self.pA[:, 0:384], self.pA[:, 512:896], self.pB[:, 0:384])[s],
                    nsa=3,
                    ppt=lambda s: (self.pB[:, 512:896], self.pC[:, 0:384])[s],
                    npt=2,
                    pot=lambda s: (self.pC[:, 512:640], self.pD[:, 0:128])[s],
                    npot=2,
                    probs=lambda s: prb[s], pts=lambda s: ptsb[s],
                    garb=lambda j, g=garb: g,
                    mx=lambda s: mxc[s], nmx=lambda s: nmxc[s], sg=lambda s: sgc[s],
                    lnsg=lambda s: lnsgc[s], b2=lambda s: b2c[s],
                    otevac=mk_ote(p, half),
                )
                self.banded_attention(cfg)
                if half == 0 and p == 0:
                    self.wait('y', sV, sV.n)
                    self.probe("aoT", oT[:, 0:1024])

            self.wait('t', sV, sV.n)
            self.wait('t', sS, sS.n)
            if half == 1:
                self.probe("aoTall", oT[:])
            for fc in range(CH):
                ws = fc % 2
                self.wait('g', sT, wot_free[ws])
                dstw = wot_sl[ws]
                srcw = self.d_wo2[fc]
                self.G(lambda e, d=dstw, s=srcw: e.dma_start(out=d, in_=s).then_inc(self.sW.h, 16))
                w_wo = sW.inc(16)
                self.wait('t', sW, w_wo)
                if half == 1 and fc == 7:
                    self.wait('y', sW, w_wo)
                    self.probe("wotfc7", wot_sl[ws])
                for tc in range(2):
                    po = (self.pA, self.pB)[tc % 2][:, 0:512]
                    for kc in range(CH):
                        self.T(lambda e, o=po, l=wot_sl[ws][:, kc * 128:kc * 128 + 128],
                               r=oT[:, kc * 1024 + tc * 512:kc * 1024 + tc * 512 + 512],
                               st=(kc == 0), sp=(kc == CH - 1): e.matmul(o, l, r, start=st, stop=sp))
                    t_wo = self.tmark()
                    self.wait('v', sT, t_wo)
                    xfch = self.xfc(fc, ht0 + tc * 512, 512)
                    self.V(lambda e, a=xfch, b=po, s=self.boc[:, fc:fc + 1]:
                           e.scalar_tensor_tensor(a, b, s, a, ALU.add, ALU.add))
                    v_wo = self.vmark()
                    self.wait('t', sV, v_wo)
                wot_free[ws] = sT.n
            if half == 1:
                self.wait('y', sV, sV.n)
                self.probe("aoT1", oT[:, 7 * 1024:8 * 1024])
                self.probe("awo", self.xf[:])
        self.ab.free(am0)

    def finish(self):
        sV, sX, sW = self.sV, self.sX, self.sW
        self.probe("xff", self.xf[:])
        self.wait('g', sV, sV.n)
        self.wait('g', self.sS, self.sS.n)
        self.wait('g', self.sT, self.sT.n)
        for c in range(CH):
            dst = self.xfc(c, 0, T)
            src = self.d_x[c * 128:(c + 1) * 128, :]
            self.G(lambda e, d=dst, s=src: e.dma_start(out=d, in_=s, accum_op=ALU.add).then_inc(self.sZ.h, 16))
            self.sZ.inc(16)
        g_acc = self.sZ.n
        self.wait('y', self.sZ, g_acc)
        for c in range(CH):
            dst = self.d_out[c * 128:(c + 1) * 128, :]
            src = self.xfc(c, 0, T)
            self.Y(lambda e, d=dst, s=src: e.dma_start(out=d, in_=s).then_inc(self.sX.h, 16))
            sX.inc(16)
        self.wait('y', sX, sX.n)


def build(stage=99, probes=()):
    return Builder(stage, probes).nc


def _reorder_w1(w1, b1):
    w1o = np.empty_like(w1)
    b1o = np.empty_like(b1)
    a_w, g_w = w1[:, :, :2730], w1[:, :, 2730:]
    a_b, g_b = b1[:, :2730], b1[:, 2730:]
    o = 0
    for c in range(22):
        wdt = min(128, 2730 - 128 * c)
        w1o[:, :, o:o + wdt] = a_w[:, :, 128 * c:128 * c + wdt]
        b1o[:, o:o + wdt] = a_b[:, 128 * c:128 * c + wdt]
        o += wdt
        w1o[:, :, o:o + wdt] = g_w[:, :, 128 * c:128 * c + wdt]
        b1o[:, o:o + wdt] = g_b[:, 128 * c:128 * c + wdt]
        o += wdt
    assert o == 5460
    return w1o, b1o


def _stream_w1(w1r):
    out = np.empty((2, 1024 * 5460), np.float32)
    for l in range(2):
        w = w1r[l].reshape(8, 128, 5460)
        o = 0
        for ci in range(22):
            wdt = min(128, 2730 - 128 * ci)
            if wdt == 128:
                blk = w[:, :, 256 * ci:256 * ci + 256].reshape(8, 128, 2, 128)
                out[l, o:o + 262144] = blk.transpose(1, 2, 0, 3).reshape(-1)
                o += 262144
            else:
                for ag in range(2):
                    blk = w[:, :, 256 * ci + ag * wdt:256 * ci + (ag + 1) * wdt]
                    n = 1024 * wdt
                    out[l, o:o + n] = blk.transpose(1, 0, 2).reshape(-1)
                    o += n
        assert o == 1024 * 5460
    return out


def _stream_w2(w2):
    out = np.empty((2, 2730 * 1024), np.float32)
    for l in range(2):
        o = 0
        for fc in range(8):
            blk = w2[l, 0:2688, fc * 128:fc * 128 + 128].reshape(21, 128, 128)
            out[l, o:o + 344064] = blk.transpose(1, 0, 2).reshape(-1)
            o += 344064
            out[l, o:o + 5376] = w2[l, 2688:2730, fc * 128:fc * 128 + 128].reshape(-1)
            o += 5376
    return out


def _stream_kpf(wT, nf, nblk):
    k = wT.shape[0] // 128
    w = wT.reshape(k, 128, wT.shape[1])
    nchunk = nf // 128
    out = np.empty((nchunk, nblk, 128, k * 128), np.float32)
    for c in range(nchunk):
        for b in range(nblk):
            out[c, b] = w[:, :, b * nf + c * 128:b * nf + c * 128 + 128].transpose(
                1, 0, 2).reshape(128, k * 128)
    return out


def _stream_qkv(wT, nf):
    k = wT.shape[0] // 128
    nblk = wT.shape[1] // nf
    w = wT.reshape(k, 128, wT.shape[1])
    nchunk = nf // 128
    out = np.empty((nchunk, 128, nblk * k * 128), np.float32)
    for c in range(nchunk):
        blk = np.stack([w[:, :, b * nf + c * 128:b * nf + c * 128 + 128]
                        for b in range(nblk)])
        out[c] = blk.transpose(2, 0, 1, 3).reshape(128, nblk * k * 128)
    return out


def kernel(L0_attn_x, L0_attn_Wqkv, L0_attn_bqkv, L0_attn_Wo, L0_attn_bo,
           L0_ffn_wqkv, L0_ffn_q_scale, L0_ffn_k_scale, L0_ffn_wo,
           L0_ffn_ln_g, L0_ffn_ln_b, L0_ffn_w1, L0_ffn_b1, L0_ffn_w2,
           L0_ffn_b2):
    from concourse.bass_utils import run_bass_kernel_spmd
    f32 = lambda a: np.ascontiguousarray(np.asarray(a, dtype=np.float32))
    x = f32(L0_attn_x)
    w1r, b1r = _reorder_w1(f32(L0_ffn_w1), f32(L0_ffn_b1))
    fwqkv = f32(L0_ffn_wqkv)
    fwo = f32(L0_ffn_wo)
    shared = {
        "wqkv2": _stream_qkv(f32(L0_attn_Wqkv).T, 1024),
        "bqkv": f32(L0_attn_bqkv),
        "wo2": _stream_kpf(f32(L0_attn_Wo).T, 1024, 1)[:, 0],
        "bo": f32(L0_attn_bo),
        "fwqkv2": np.stack([_stream_qkv(fwqkv[l], 512) for l in range(2)]),
        "fqs": f32(L0_ffn_q_scale),
        "fks": f32(L0_ffn_k_scale),
        "fwo2": np.stack([_stream_kpf(fwo[l], 1024, 1)[:, 0] for l in range(2)]),
        "flng": f32(L0_ffn_ln_g),
        "flnb": f32(L0_ffn_ln_b),
        "fw1s": _stream_w1(w1r),
        "fb1r": np.ascontiguousarray(b1r),
        "fw2s": _stream_w2(f32(L0_ffn_w2)),
        "fb2": f32(L0_ffn_b2),
    }
    in_maps = [dict(shared, xT=np.ascontiguousarray(x[i].T)) for i in range(8)]
    nc = Builder(99).nc
    res = run_bass_kernel_spmd(nc, in_maps, core_ids=list(range(8)))
    out = np.stack([np.asarray(res.results[i]["outT"], dtype=np.float32).T
                    for i in range(8)])
    return out
